# version 20
# baseline (speedup 1.0000x reference)
import contextlib
import numpy as np
import ml_dtypes
import concourse.bass as bass
import concourse.mybir as mybir
from concourse.bass_utils import run_bass_kernel_spmd

F32 = mybir.dt.float32
BF16 = mybir.dt.bfloat16
AF = mybir.ActivationFunctionType
ALU = mybir.AluOpType
AX = mybir.AxisListType

NCORES = 8
BATCH = 4
T = 2048
DM = 2048
TOK = BATCH * T
TPC = TOK // NCORES

BF = ml_dtypes.bfloat16
ENGINES = ("sync", "tensor", "vector", "scalar", "gpsimd")
NDMASEM = 8


def cdiv(a, b):
    return (a + b - 1) // b


_UID = [0]


def _u(nm):
    _UID[0] += 1
    return f"{nm}_u{_UID[0]}"


def pcol(v, nt=None):
    v = np.asarray(v, np.float32).ravel()
    n = nt if nt is not None else cdiv(v.size, 128)
    out = np.zeros((128, n), np.float32)
    for i in range(min(n, cdiv(v.size, 128))):
        blk = v[i * 128:(i + 1) * 128]
        out[:blk.size, i] = blk
    return out


class Op:
    __slots__ = ("eng", "meth", "args", "kwargs", "deps", "step", "val",
                 "kind", "idx")

    def __init__(self, eng, meth, args, kwargs, step):
        self.eng = eng
        self.meth = meth
        self.args = args
        self.kwargs = kwargs
        self.deps = []
        self.step = step
        self.val = 0
        self.kind = "dma" if step == 16 else "cmp"


class Prog:

    def __init__(self):
        self.ops = {e: [] for e in ENGINES}
        self.last_writer = {}
        self.readers = {}
        self.frontier = []
        self.barrier_pending = set()
        self.dma_count = {}

    def op(self, eng, meth, *args, reads=(), writes=(), deps=(), **kwargs):
        step = 16 if meth in ("dma_start", "dyn_dma") else 1
        o = Op(eng, meth, args, kwargs, step)
        if meth == "collective_compute":
            o.kind = "cc"
        if o.kind == "dma":
            slot = self.dma_count.get(eng, 0)
            self.dma_count[eng] = slot + 1
            o.kind = ("dma", slot % NDMASEM)
        if eng in self.barrier_pending:
            self.barrier_pending.discard(eng)
            deps = list(deps) + self.frontier
        prev = 0
        for q in reversed(self.ops[eng]):
            if q.kind == o.kind:
                prev = q.val
                break
        o.val = prev + step
        dd = list(deps)
        for k in reads:
            w = self.last_writer.get(k)
            if w is not None:
                dd.append(w)
            self.readers.setdefault(k, []).append(o)
        for k in writes:
            w = self.last_writer.get(k)
            if w is not None:
                dd.append(w)
            dd.extend(self.readers.get(k, ()))
            self.last_writer[k] = o
            self.readers[k] = []
        o.deps = dd
        self.ops[eng].append(o)
        return o

    def dma(self, out, in_, reads=(), writes=(), deps=(), eng="sync", **kw):
        return self.op(eng, "dma_start", reads=reads, writes=writes,
                       deps=deps, out=out, in_=in_, **kw)

    def dyn_dma(self, out, handle, ap_list, off_ap, reads=(), writes=(),
                deps=(), eng="sync", out_is_dyn=False):
        return self.op(eng, "dyn_dma", reads=reads, writes=writes, deps=deps,
                       out=out, handle=handle, ap_list=ap_list, off_ap=off_ap,
                       out_is_dyn=out_is_dyn)

    def cc(self, kind, alu, rg, cins, couts, reads=(), writes=(), deps=()):
        return self.op("gpsimd", "collective_compute", kind, alu, rg,
                       cins, couts, reads=reads, writes=writes, deps=deps)

    def barrier(self):
        self.frontier = [q[-1] for q in self.ops.values() if q]
        self.barrier_pending = set(ENGINES)

    def attach(self, nc, stack):
        self.nc = nc
        self._sems = {}
        for e in ENGINES:
            for kd in ("cmp", "cc"):
                self._sems[(e, kd)] = stack.enter_context(
                    nc.semaphore(f"s_{e}_{kd}"))
            for i in range(NDMASEM):
                self._sems[(e, ("dma", i))] = stack.enter_context(
                    nc.semaphore(f"s_{e}_dma{i}"))
        self._block = stack.enter_context(nc.Block())
        self._emitted = {e: 0 for e in ENGINES}
        self._waited = {e: {} for e in ENGINES}
        self._prev_cc = {e: 0 for e in ENGINES}

    def _emit_ops(self, engine_obj, e, ops):
        sems = self._sems
        waited = self._waited[e]
        for o in ops:
            if isinstance(o.kind, tuple) and o.val > 16:
                k = (e, o.kind)
                if waited.get(k, 0) < o.val - 16:
                    engine_obj.wait_ge(sems[k], o.val - 16)
                    waited[k] = o.val - 16
            if o.kind == "cc" and self._prev_cc[e]:
                k = (e, "cc")
                if waited.get(k, 0) < self._prev_cc[e]:
                    engine_obj.wait_ge(sems[k], self._prev_cc[e])
                    waited[k] = self._prev_cc[e]
            for d in o.deps:
                if d is o:
                    continue
                if d.eng == e == "tensor" and d.kind == "cmp" == o.kind:
                    continue
                dk = (d.eng, d.kind)
                if waited.get(dk, 0) >= d.val:
                    continue
                engine_obj.wait_ge(sems[dk], d.val)
                waited[dk] = d.val
            if o.meth == "dyn_dma":
                kw = o.kwargs
                with engine_obj.register(_u("dynr")) as reg:
                    engine_obj.reg_load(reg, kw["off_ap"])
                    dyn_ap = bass.AP(kw["handle"], reg, kw["ap_list"])
                    if kw.get("out_is_dyn"):
                        inst = engine_obj.dma_start(out=dyn_ap, in_=kw["out"])
                    else:
                        inst = engine_obj.dma_start(out=kw["out"], in_=dyn_ap)
            else:
                inst = getattr(engine_obj, o.meth)(*o.args, **o.kwargs)
            inst.then_inc(sems[(e, o.kind)], o.step)
            if o.kind == "cc":
                self._prev_cc[e] = o.val

    def flush(self):
        for e in ENGINES:
            new = self.ops[e][self._emitted[e]:]
            if not new:
                continue
            self._emitted[e] = len(self.ops[e])
            getattr(self._block, e)(
                lambda engine_obj, e=e, new=new: self._emit_ops(engine_obj, e, new))

    def finish(self, final_ops):
        self.op("vector", "engine_nop", deps=list(final_ops))
        self.flush()

    def build(self, nc, final_ops):
        self.op("vector", "engine_nop", deps=list(final_ops))
        with contextlib.ExitStack() as stack:
            sems = {}
            for e in ENGINES:
                kinds = set(str(o.kind) for o in self.ops[e])
                for kd in kinds:
                    sems[(e, kd)] = stack.enter_context(
                        nc.semaphore(f"s_{e}_{kd}"))
            block = stack.enter_context(nc.Block())

            def emit(engine_obj, e):
                waited = {}
                for o in self.ops[e]:
                    if isinstance(o.kind, tuple) and o.val > 16:
                        k = (e, str(o.kind))
                        if waited.get(k, 0) < o.val - 16:
                            engine_obj.wait_ge(sems[k], o.val - 16)
                            waited[k] = o.val - 16
                    for d in o.deps:
                        if d is o:
                            continue
                        if d.eng == e == "tensor" and d.kind == "cmp" == o.kind:
                            continue
                        dk = (d.eng, str(d.kind))
                        if waited.get(dk, 0) >= d.val:
                            continue
                        engine_obj.wait_ge(sems[dk], d.val)
                        waited[dk] = d.val
                    inst = getattr(engine_obj, o.meth)(*o.args, **o.kwargs)
                    inst.then_inc(sems[(e, str(o.kind))], o.step)

            for e in ENGINES:
                if self.ops[e]:
                    getattr(block, e)(lambda engine_obj, e=e: emit(engine_obj, e))


def emit_consts(p, cm, nc, values):
    n = max(1, len(values))
    tile = cm.enter_context(nc.sbuf_tensor(_u("consts"), [128, n], F32))
    key = (_u("constsk"),)
    for j, v in enumerate(values):
        p.op("vector", "memset", tile[:, j:j + 1], float(v), writes=[key])
    return tile, key


def emit_ln(p, src, dst, st, sq, k_src, k_dst, k_st, k_sq, D, eps_col, k_eps):
    p.op("vector", "reduce_sum", st[:, 0:1], src, axis=AX.X,
         reads=[k_src], writes=[k_st])
    p.op("scalar", "activation", sq, src, AF.Square,
         reads=[k_src], writes=[k_sq])
    p.op("vector", "reduce_sum", st[:, 1:2], sq, axis=AX.X,
         reads=[k_sq], writes=[k_st])
    p.op("vector", "tensor_scalar", st[:, 0:1], st[:, 0:1], 1.0 / D, None,
         ALU.mult, reads=[k_st], writes=[k_st])
    p.op("vector", "scalar_tensor_tensor", st[:, 2:3], st[:, 0:1], -1.0,
         st[:, 0:1], ALU.mult, ALU.mult, reads=[k_st], writes=[k_st])
    p.op("vector", "scalar_tensor_tensor", st[:, 2:3], st[:, 1:2], 1.0 / D,
         st[:, 2:3], ALU.mult, ALU.add, reads=[k_st], writes=[k_st])
    p.op("scalar", "activation", st[:, 3:4], st[:, 2:3], AF.Sqrt, bias=eps_col,
         reads=[k_st, k_eps], writes=[k_st])
    p.op("vector", "reciprocal", st[:, 4:5], st[:, 3:4],
         reads=[k_st], writes=[k_st])
    return p.op("vector", "tensor_scalar", dst, src, st[:, 0:1], st[:, 4:5],
                ALU.subtract, ALU.mult, reads=[k_src, k_st], writes=[k_dst])


def build_ln_stage(rows, D, residual, emit_next, final_f32=False):
    nc = bass.Bass("TRN2", target_bir_lowering=False)
    f_in = nc.dram_tensor("f", [rows, D], F32, kind="ExternalInput").ap()
    res_in = nc.dram_tensor("res", [rows, D], F32, kind="ExternalInput").ap() if residual else None
    xo_out = nc.dram_tensor("xo", [rows, D], F32, kind="ExternalOutput").ap() if residual else None
    xn_out = nc.dram_tensor("xn", [rows, D], F32 if final_f32 else BF16,
                            kind="ExternalOutput").ap() if emit_next else None
    ntile = rows // 128
    p = Prog()
    stores = []
    with contextlib.ExitStack() as cm:
        sb = lambda *a: cm.enter_context(nc.sbuf_tensor(*a))
        ft = sb("ft", [128, 2, D], F32)
        rt = sb("rt", [128, 2, D], F32) if residual else None
        sq = sb("sq", [128, 2, D], F32)
        ot = sb("ot", [128, 2, D], F32)
        on = sb("on", [128, 2, D], F32 if final_f32 else BF16)
        st = sb("st", [128, 2, 8], F32)
        cst, kcst = emit_consts(p, cm, nc, [1e-5])

        f_t = f_in.rearrange("(n p) d -> n p d", p=128)
        res_t = res_in.rearrange("(n p) d -> n p d", p=128) if residual else None
        xo_t = xo_out.rearrange("(n p) d -> n p d", p=128) if residual else None
        xn_t = xn_out.rearrange("(n p) d -> n p d", p=128) if emit_next else None

        for i in range(ntile):
            s = i % 2
            kf, kr, ksq, kot, kon, kst = (("ft", s), ("rt", s), ("sq", s),
                                          ("ot", s), ("on", s), ("st", s))
            p.dma(ft[:, s, :], f_t[i], writes=[kf])
            if residual:
                p.dma(rt[:, s, :], res_t[i], writes=[kr])
                emit_ln(p, ft[:, s, :], ot[:, s, :], st[:, s, :], sq[:, s, :],
                        kf, kot, kst, ksq, D, cst[:, 0:1], kcst)
                p.op("vector", "tensor_add", ot[:, s, :], ot[:, s, :],
                     rt[:, s, :], reads=[kot, kr], writes=[kot])
                stores.append(p.dma(xo_t[i], ot[:, s, :], reads=[kot]))
                if emit_next:
                    emit_ln(p, ot[:, s, :], on[:, s, :], st[:, s, :],
                            sq[:, s, :], kot, kon, kst, ksq, D, cst[:, 0:1], kcst)
                    stores.append(p.dma(xn_t[i], on[:, s, :], reads=[kon]))
            else:
                emit_ln(p, ft[:, s, :], on[:, s, :], st[:, s, :], sq[:, s, :],
                        kf, kon, kst, ksq, D, cst[:, 0:1], kcst)
                stores.append(p.dma(xn_t[i], on[:, s, :], reads=[kon]))
        p.build(nc, stores)
    return nc


def run(nc, in_maps, outs):
    res = run_bass_kernel_spmd(nc, in_maps, core_ids=list(range(NCORES)))
    return [[np.asarray(res.results[c][o]) for c in range(NCORES)] for o in outs]


def kernel(**inputs):
    raise NotImplementedError("phase 1 under construction")


def build_mm_stage(Kdim, M, N, act=None, row_scale=False, row_bias=False,
                   softplus=False, col_mult=False, rhs_prescale=False,
                   nchunk=512):
    assert Kdim % 128 == 0 and N % nchunk == 0
    KT = Kdim // 128
    MT = cdiv(M, 128)
    NC = N // nchunk
    nc = bass.Bass("TRN2", target_bir_lowering=False)
    lhsT_in = nc.dram_tensor("lhsT", [Kdim, M], F32, kind="ExternalInput").ap()
    rhs_in = nc.dram_tensor("rhs", [Kdim, N], F32, kind="ExternalInput").ap()
    scale_in = nc.dram_tensor("scale", [128, MT], F32, kind="ExternalInput").ap() if row_scale else None
    bias_in = nc.dram_tensor("bias", [128, MT], F32, kind="ExternalInput").ap() if (row_bias or softplus) else None
    cmul_in = nc.dram_tensor("cmul", [1, N], F32, kind="ExternalInput").ap() if col_mult else None
    pres_in = nc.dram_tensor("pres", [128, KT], F32, kind="ExternalInput").ap() if rhs_prescale else None
    out_out = nc.dram_tensor("out", [M, N], F32, kind="ExternalOutput").ap()

    p = Prog()
    stores = []
    with contextlib.ExitStack() as cm:
        sb = lambda *a: cm.enter_context(nc.sbuf_tensor(*a))
        ps = lambda *a: cm.enter_context(nc.psum_tensor(*a))
        rhs_sb = sb("rhs_sb", [128, KT, N], BF16)
        lhs_sb = sb("lhs_sb", [128, 2, KT, 128], BF16)
        out_sb = sb("out_sb", [128, 2, N], F32)
        tmp1 = sb("tmp1", [128, nchunk], F32)
        tmp2 = sb("tmp2", [128, nchunk], F32)
        vec_sb = sb("vec_sb", [128, 2 * MT + 2], F32)
        ones_sb = sb("ones_sb", [1, 128], F32)
        cmb = sb("cmb", [128, N], F32) if col_mult else None
        psA = ps("psA", [128, nchunk], F32)
        psB = ps("psB", [128, nchunk], F32)
        psC = ps("psC", [128, 512], F32) if col_mult else None
        psum = [psA, psB]

        rhs_t = rhs_in.rearrange("(k p) n -> p k n", p=128)
        for k0 in range(0, KT, 8):
            k1 = min(KT, k0 + 8)
            p.dma(rhs_sb[:, k0:k1, :], rhs_t[:, k0:k1, :], writes=[("rhs", k) for k in range(k0, k1)], eng="gpsimd")
        if rhs_prescale:
            pres_sb = sb("pres_sb", [128, KT], F32)
            p.dma(pres_sb[:], pres_in, writes=[("pres",)])
            for k in range(KT):
                p.op("vector", "tensor_scalar", rhs_sb[:, k, :], rhs_sb[:, k, :],
                     pres_sb[:, k:k + 1], None, ALU.mult,
                     reads=[("rhs", k), ("pres",)], writes=[("rhs", k)])
        if row_scale:
            sc_sb = sb("sc_sb", [128, MT], F32)
            p.dma(sc_sb[:], scale_in, writes=[("sc",)])
        if row_bias or softplus:
            bi_sb = sb("bi_sb", [128, MT], F32)
            p.dma(bi_sb[:], bias_in, writes=[("bi",)])
        if col_mult:
            p.op("vector", "memset", ones_sb[:], 1.0, writes=[("ones",)])
            cm_sb1 = sb("cm_sb1", [1, N], F32)
            p.dma(cm_sb1[:], cmul_in, writes=[("cm1",)])
            for j in range(N // 512):
                p.op("tensor", "matmul", psC[:], ones_sb[:], cm_sb1[:, j * 512:(j + 1) * 512],
                     start=True, stop=True, reads=[("ones",), ("cm1",)], writes=[("psC",)])
                p.op("vector", "tensor_copy", cmb[:, j * 512:(j + 1) * 512], psC[:],
                     reads=[("psC",)], writes=[("cmb",)])

        lhs_t = lhsT_in
        out_ap = out_out

        for mt in range(MT):
            mb = mt % 2
            pm = min(128, M - mt * 128)
            slab = lhs_t[:, mt * 128:mt * 128 + pm].rearrange("(k p) m -> p k m", p=128)
            for k0 in range(0, KT, 8):
                k1 = min(KT, k0 + 8)
                p.dma(lhs_sb[:, mb, k0:k1, 0:pm], slab[:, k0:k1, :],
                      writes=[("lhs", mb, k) for k in range(k0, k1)], eng="gpsimd")
            for nch in range(NC):
                pb = (mt * NC + nch) % 2
                pt = psum[pb]
                for k in range(KT):
                    p.op("tensor", "matmul", pt[0:pm, 0:nchunk], lhs_sb[:, mb, k, 0:pm],
                         rhs_sb[:, k, nch * nchunk:(nch + 1) * nchunk],
                         start=(k == 0), stop=(k == KT - 1),
                         reads=[("lhs", mb, k), ("rhs", k)],
                         writes=([("ps", pb)] if k == 0 else []),
                         deps=[])
                mm_last = p.ops["tensor"][-1]
                src = pt[0:pm, 0:nchunk]
                kread = [("ps", pb)]
                dst = out_sb[0:pm, mb, nch * nchunk:(nch + 1) * nchunk]
                if softplus:
                    p.op("vector", "tensor_scalar", tmp1[0:pm, :], src, bi_sb[0:pm, mt:mt + 1],
                         None, ALU.add, reads=kread + [("bi",)], writes=[("tmp1",)], deps=[mm_last])
                    p.op("scalar", "activation", tmp2[0:pm, :], tmp1[0:pm, :], AF.Abs,
                         reads=[("tmp1",)], writes=[("tmp2",)])
                    p.op("scalar", "activation", tmp2[0:pm, :], tmp2[0:pm, :], AF.Exp,
                         scale=-1.0, reads=[("tmp2",)], writes=[("tmp2",)])
                    p.op("scalar", "activation", tmp2[0:pm, :], tmp2[0:pm, :], AF.Ln,
                         bias=1.0, reads=[("tmp2",)], writes=[("tmp2",)])
                    p.op("vector", "tensor_relu", tmp1[0:pm, :], tmp1[0:pm, :],
                         reads=[("tmp1",)], writes=[("tmp1",)])
                    p.op("vector", "tensor_add", dst, tmp1[0:pm, :], tmp2[0:pm, :],
                         reads=[("tmp1",), ("tmp2",)], writes=[("out", mb)])
                else:
                    cur = src
                    curk = kread
                    if row_scale and act == "silu":
                        p.op("scalar", "activation", dst, cur, AF.Silu,
                             scale=sc_sb[0:pm, mt:mt + 1],
                             reads=curk + [("sc",)], writes=[("out", mb)], deps=[mm_last])
                    elif row_scale and act is None:
                        p.op("vector", "tensor_scalar", dst, cur, sc_sb[0:pm, mt:mt + 1],
                             None, ALU.mult, reads=curk + [("sc",)], writes=[("out", mb)], deps=[mm_last])
                    elif act == "silu":
                        p.op("scalar", "activation", dst, cur, AF.Silu,
                             reads=curk, writes=[("out", mb)], deps=[mm_last])
                    else:
                        p.op("vector", "tensor_copy", dst, cur,
                             reads=curk, writes=[("out", mb)], deps=[mm_last])
                    if col_mult:
                        p.op("vector", "tensor_mul", dst, dst, cmb[0:pm, nch * nchunk:(nch + 1) * nchunk],
                             reads=[("out", mb), ("cmb",)], writes=[("out", mb)])
            so = p.dma(out_ap[mt * 128:mt * 128 + pm, :], out_sb[0:pm, mb, :],
                       reads=[("out", mb)])
            stores.append(so)
        p.build(nc, stores)
    return nc


def build_conv_stage(CH, B, T_, pad_l):
    CT = CH // 128
    nc = bass.Bass("TRN2", target_bir_lowering=False)
    xin = nc.dram_tensor("xin", [CH, B, T_], F32, kind="ExternalInput").ap()
    cw = nc.dram_tensor("cw", [CH, 4], F32, kind="ExternalInput").ap()
    cb = nc.dram_tensor("cb", [128, CH // 128], F32, kind="ExternalInput").ap()
    xact = nc.dram_tensor("xact", [CH, B, T_], F32, kind="ExternalOutput").ap()
    PAD = 4
    p = Prog()
    stores = []
    with contextlib.ExitStack() as cm:
        sb = lambda *a: cm.enter_context(nc.sbuf_tensor(*a))
        buf = sb("buf", [128, 2, T_ + PAD], F32)
        acc = sb("acc", [128, 2, T_], F32)
        ot = sb("ot", [128, 2, T_], F32)
        cw_sb = sb("cw_sb", [128, CT, 4], F32)
        cb_sb = sb("cb_sb", [128, CT], F32)
        p.dma(cw_sb[:], cw.rearrange("(c p) k -> p c k", p=128), writes=[("cw",)])
        p.dma(cb_sb[:], cb, writes=[("cb",)])
        xin_t = xin.rearrange("(c p) b t -> c b p t", p=128)
        xact_t = xact.rearrange("(c p) b t -> c b p t", p=128)
        for u, (c, b) in enumerate([(c, b) for c in range(CT) for b in range(B)]):
            s = u % 2
            kb, ka, ko = ("buf", s), ("acc", s), ("ot", s)
            if u < 2:
                p.op("vector", "memset", buf[:, s, :], 0.0, writes=[kb])
            p.dma(buf[:, s, pad_l:pad_l + T_], xin_t[c, b], writes=[(("bufc", s))],
                  deps=[p.last_writer[kb]] if u < 2 else [])
            kbc = ("bufc", s)
            p.op("vector", "tensor_scalar", acc[:, s, :], buf[:, s, 0:T_],
                 cw_sb[:, c, 0:1], None, ALU.mult, reads=[kbc, ("cw",)], writes=[ka])
            for k in range(1, 4):
                p.op("vector", "scalar_tensor_tensor", acc[:, s, :],
                     buf[:, s, k:k + T_], cw_sb[:, c, k:k + 1], acc[:, s, :],
                     ALU.mult, ALU.add, reads=[kbc, ("cw",), ka], writes=[ka])
            p.op("scalar", "activation", ot[:, s, :], acc[:, s, :], AF.Silu,
                 bias=cb_sb[:, c:c + 1], reads=[ka, ("cb",)], writes=[ko])
            stores.append(p.dma(xact_t[c, b], ot[:, s, :], reads=[ko]))
        p.build(nc, stores)
    return nc


def build_scan_stage(DS, B, T_, NST=16, TC=512, NG=4):
    DT4 = DS // 128
    NTC = T_ // TC
    GS = NST // NG
    nc = bass.Bass("TRN2", target_bir_lowering=False)
    dt_in = nc.dram_tensor("dt", [DS, B, T_], F32, kind="ExternalInput").ap()
    xa_in = nc.dram_tensor("xa", [DS, B, T_], F32, kind="ExternalInput").ap()
    z_in = nc.dram_tensor("z", [DS, B, T_], F32, kind="ExternalInput").ap()
    al_in = nc.dram_tensor("alog", [DS, NST], F32, kind="ExternalInput").ap()
    dv_in = nc.dram_tensor("dv", [128, DS // 128], F32, kind="ExternalInput").ap()
    bm_in = nc.dram_tensor("bm", [B, 1, NST, T_], F32, kind="ExternalInput").ap()
    cmm_in = nc.dram_tensor("cm", [B, 1, NST, T_], F32, kind="ExternalInput").ap()
    id_in = nc.dram_tensor("ident", [128, 128], F32, kind="ExternalInput").ap()
    yg_out = nc.dram_tensor("yg", [DS, B, T_], F32, kind="ExternalOutput").ap()

    p = Prog()
    stores = []
    with contextlib.ExitStack() as cm:
        sb = lambda nm, shp, dt_: cm.enter_context(nc.sbuf_tensor(_u(nm), shp, dt_))
        ps = lambda nm, shp, dt_: cm.enter_context(nc.psum_tensor(_u(nm), shp, dt_))
        dt_sb = sb("dt_sb", [128, 2, DT4, TC], F32)
        xa_sb = sb("xa_sb", [128, 2, DT4, TC], F32)
        z_sb = sb("z_sb", [128, 2, TC], F32)
        dtx_sb = sb("dtx_sb", [128, DT4, TC], F32)
        dA_sb = sb("dA_sb", [128, 2, TC], F32)
        dBx_sb = sb("dBx_sb", [128, 2, TC], F32)
        h_sb = sb("h_sb", [128, 2, TC], F32)
        p_sb = sb("p_sb", [128, 2, TC], BF16)
        bb_sb = sb("bb_sb", [128, 2, TC], BF16)
        cb_sb = sb("cb_sb", [128, 2, TC], BF16)
        yd_sb = sb("yd_sb", [128, TC], F32)
        sz_sb = sb("sz_sb", [128, TC], F32)
        yo_sb = sb("yo_sb", [128, 2, TC], F32)
        hcar = sb("hcar", [128, DT4, NST], F32)
        A_sb = sb("A_sb", [128, DT4, NST], F32)
        dv_sb = sb("dv_sb", [128, DT4], F32)
        bro_sb = sb("bro_sb", [1, 2, GS, T_], BF16)
        cro_sb = sb("cro_sb", [1, 2, GS, T_], BF16)
        ones_bf = sb("ones_bf", [1, 128], BF16)
        id_sb = sb("id_sb", [128, 128], BF16)
        psy = [ps(f"psy{i}", [128, TC], F32) for i in range(DT4)]
        psb = ps("psb", [128, 2, TC], F32)

        p.op("vector", "memset", ones_bf[:], 1.0, writes=[("ones",)])
        p.dma(id_sb[:], id_in, writes=[("id",)], eng="gpsimd")
        p.dma(A_sb[:], al_in.rearrange("(c p) n -> p c n", p=128), writes=[("A",)])
        p.dma(dv_sb[:], dv_in, writes=[("dv",)])
        for c in range(DT4):
            p.op("scalar", "activation", A_sb[:, c, :], A_sb[:, c, :], AF.Exp,
                 reads=[("A",)], writes=[("A",)])
            p.op("vector", "tensor_scalar", A_sb[:, c, :], A_sb[:, c, :], -1.0,
                 None, ALU.mult, reads=[("A",)], writes=[("A",)])

        dt_t = dt_in.rearrange("(c p) b t -> b c p t", p=128)
        xa_t = xa_in.rearrange("(c p) b t -> b c p t", p=128)
        z_t = z_in.rearrange("(c p) b t -> b c p t", p=128)
        yg_t = yg_out.rearrange("(c p) b t -> b c p t", p=128)

        for b in range(B):
            for tc in range(NTC):
                s = tc % 2
                tsl = slice(tc * TC, (tc + 1) * TC)
                for c in range(DT4):
                    p.dma(dt_sb[:, s, c, :], dt_t[b, c, :, tsl], writes=[("dt", s, c)])
                    p.dma(xa_sb[:, s, c, :], xa_t[b, c, :, tsl], writes=[("xa", s, c)])
                    p.op("vector", "tensor_mul", dtx_sb[:, c, :], dt_sb[:, s, c, :],
                         xa_sb[:, s, c, :], reads=[("dt", s, c), ("xa", s, c)],
                         writes=[("dtx", c)])
                for g in range(NG):
                    sg = g % 2
                    p.dma(bro_sb[:, sg, :, :], bm_in[b, :, g * GS:(g + 1) * GS, :],
                          writes=[("bro", sg)], eng="gpsimd")
                    p.dma(cro_sb[:, sg, :, :], cmm_in[b, :, g * GS:(g + 1) * GS, :],
                          writes=[("cro", sg)], eng="gpsimd")
                    for nn in range(GS):
                        n = g * GS + nn
                        sn = n % 2
                        p.op("tensor", "matmul", psb[:, sn, :], ones_bf[:],
                             bro_sb[:, sg, nn, tsl], start=True, stop=True,
                             reads=[("ones",), ("bro", sg)], writes=[("psb", sn)])
                        p.op("scalar", "copy", bb_sb[:, sn, :], psb[:, sn, :],
                             reads=[("psb", sn)], writes=[("bb", sn)])
                        p.op("tensor", "matmul", psb[:, sn, :], ones_bf[:],
                             cro_sb[:, sg, nn, tsl], start=True, stop=True,
                             reads=[("ones",), ("cro", sg)], writes=[("psb", sn)])
                        p.op("scalar", "copy", cb_sb[:, sn, :], psb[:, sn, :],
                             reads=[("psb", sn)], writes=[("cb", sn)])
                        for c in range(DT4):
                            p.op("scalar", "activation", dA_sb[:, sn, :], dt_sb[:, s, c, :],
                                 AF.Exp, scale=A_sb[:, c, n:n + 1],
                                 reads=[("dt", s, c), ("A",)], writes=[("dA", sn)])
                            p.op("gpsimd", "tensor_mul", dBx_sb[:, sn, :], dtx_sb[:, c, :],
                                 bb_sb[:, sn, :], reads=[("dtx", c), ("bb", sn)],
                                 writes=[("dBx", sn)])
                            scan_init = 0.0 if tc == 0 else hcar[:, c, n:n + 1]
                            p.op("vector", "tensor_tensor_scan", h_sb[:, sn, :],
                                 dA_sb[:, sn, :], dBx_sb[:, sn, :], scan_init,
                                 ALU.mult, ALU.add,
                                 reads=[("dA", sn), ("dBx", sn)] + ([("hc", c, n)] if tc else []),
                                 writes=[("h", sn)])
                            if tc < NTC - 1:
                                p.op("vector", "tensor_copy", hcar[:, c, n:n + 1],
                                     h_sb[:, sn, TC - 1:TC], reads=[("h", sn)],
                                     writes=[("hc", c, n)])
                            p.op("vector", "tensor_mul", p_sb[:, sn, :], h_sb[:, sn, :],
                                 cb_sb[:, sn, :], reads=[("h", sn), ("cb", sn)],
                                 writes=[("p", sn)])
                            p.op("tensor", "matmul", psy[c][:], id_sb[:], p_sb[:, sn, :],
                                 start=(n == 0), stop=(n == NST - 1),
                                 reads=[("id",), ("p", sn)],
                                 writes=([("psy", c)] if n in (0, NST - 1) else []))
                for c in range(DT4):
                    p.dma(z_sb[:, s, :], z_t[b, c, :, tsl], writes=[("z", s)])
                    p.op("vector", "scalar_tensor_tensor", yd_sb[:], xa_sb[:, s, c, :],
                         dv_sb[:, c:c + 1], psy[c][:], ALU.mult, ALU.add,
                         reads=[("xa", s, c), ("dv",), ("psy", c)], writes=[("yd",)])
                    p.op("scalar", "activation", sz_sb[:], z_sb[:, s, :], AF.Silu,
                         reads=[("z", s)], writes=[("sz",)])
                    p.op("vector", "tensor_mul", yo_sb[:, s, :], yd_sb[:], sz_sb[:],
                         reads=[("yd",), ("sz",)], writes=[("yo", s)])
                    stores.append(p.dma(yg_t[b, c, :, tsl], yo_sb[:, s, :],
                                        reads=[("yo", s)]))
        p.build(nc, stores)
    return nc


def build_ffn_stage(Kdim, M, N, nchunk=512):
    KT, MT, NC = Kdim // 128, M // 128, N // nchunk
    nc = bass.Bass("TRN2", target_bir_lowering=False)
    lg_in = nc.dram_tensor("lhsT_g", [Kdim, M], F32, kind="ExternalInput").ap()
    lu_in = nc.dram_tensor("lhsT_u", [Kdim, M], F32, kind="ExternalInput").ap()
    rhs_in = nc.dram_tensor("rhs", [Kdim, N], F32, kind="ExternalInput").ap()
    gs_in = nc.dram_tensor("gs", [128, MT], F32, kind="ExternalInput").ap()
    us_in = nc.dram_tensor("us", [128, MT], F32, kind="ExternalInput").ap()
    out_out = nc.dram_tensor("out", [M, N], F32, kind="ExternalOutput").ap()
    p = Prog()
    stores = []
    with contextlib.ExitStack() as cm:
        sb = lambda *a: cm.enter_context(nc.sbuf_tensor(*a))
        ps = lambda *a: cm.enter_context(nc.psum_tensor(*a))
        rhs_sb = sb("rhs_sb", [128, KT, N], BF16)
        lg_sb = sb("lg_sb", [128, 2, KT, 128], BF16)
        lu_sb = sb("lu_sb", [128, 2, KT, 128], BF16)
        out_sb = sb("out_sb", [128, 2, N], F32)
        ga_sb = sb("ga_sb", [128, nchunk], F32)
        gs_sb = sb("gs_sb", [128, MT], F32)
        us_sb = sb("us_sb", [128, MT], F32)
        psg = ps("psg", [128, nchunk], F32)
        psu = ps("psu", [128, nchunk], F32)
        rhs_t = rhs_in.rearrange("(k p) n -> p k n", p=128)
        for k0 in range(0, KT, 8):
            k1 = min(KT, k0 + 8)
            p.dma(rhs_sb[:, k0:k1, :], rhs_t[:, k0:k1, :],
                  writes=[("rhs", k) for k in range(k0, k1)], eng="gpsimd")
        p.dma(gs_sb[:], gs_in, writes=[("gs",)])
        p.dma(us_sb[:], us_in, writes=[("us",)])
        for mt in range(MT):
            mb = mt % 2
            for W, buf, nm in ((lg_in, lg_sb, "lg"), (lu_in, lu_sb, "lu")):
                slab = W[:, mt * 128:(mt + 1) * 128].rearrange("(k p) m -> p k m", p=128)
                for k0 in range(0, KT, 8):
                    k1 = min(KT, k0 + 8)
                    p.dma(buf[:, mb, k0:k1, :], slab[:, k0:k1, :],
                          writes=[(nm, mb, k) for k in range(k0, k1)], eng="gpsimd")
            for nch in range(NC):
                nsl = slice(nch * nchunk, (nch + 1) * nchunk)
                for k in range(KT):
                    p.op("tensor", "matmul", psg[:], lg_sb[:, mb, k, :],
                         rhs_sb[:, k, nsl], start=(k == 0), stop=(k == KT - 1),
                         reads=[("lg", mb, k), ("rhs", k)],
                         writes=([("psg",)] if k in (0, KT - 1) else []))
                for k in range(KT):
                    p.op("tensor", "matmul", psu[:], lu_sb[:, mb, k, :],
                         rhs_sb[:, k, nsl], start=(k == 0), stop=(k == KT - 1),
                         reads=[("lu", mb, k), ("rhs", k)],
                         writes=([("psu",)] if k in (0, KT - 1) else []))
                p.op("scalar", "activation", ga_sb[:], psg[:], AF.Silu,
                     scale=gs_sb[:, mt:mt + 1], reads=[("psg",), ("gs",)],
                     writes=[("ga",)])
                p.op("vector", "tensor_scalar", out_sb[:, mb, nsl], psu[:],
                     us_sb[:, mt:mt + 1], None, ALU.mult,
                     reads=[("psu",), ("us",)], writes=[("out", mb)])
                p.op("vector", "tensor_mul", out_sb[:, mb, nsl], out_sb[:, mb, nsl],
                     ga_sb[:], reads=[("out", mb), ("ga",)], writes=[("out", mb)])
            stores.append(p.dma(out_out[mt * 128:(mt + 1) * 128, :],
                                out_sb[:, mb, :], reads=[("out", mb)]))
        p.build(nc, stores)
    return nc


def build_gating_stage(rows, CHN, eps=1e-6):
    NT = rows // 128
    nc = bass.Bass("TRN2", target_bir_lowering=False)
    y_in = nc.dram_tensor("y", [rows, CHN], F32, kind="ExternalInput").ap()
    h_in = nc.dram_tensor("hsd", [rows, CHN], F32, kind="ExternalInput").ap()
    g_in = nc.dram_tensor("gate", [rows, CHN], F32, kind="ExternalInput").ap()
    yf_out = nc.dram_tensor("yf", [rows, CHN], F32, kind="ExternalOutput").ap()
    rs_out = nc.dram_tensor("rstd", [128, NT], F32, kind="ExternalOutput").ap()
    p = Prog()
    stores = []
    with contextlib.ExitStack() as cm:
        sb = lambda *a: cm.enter_context(nc.sbuf_tensor(*a))
        yt = sb("yt", [128, 2, CHN], F32)
        ht = sb("ht", [128, 2, CHN], F32)
        gt = sb("gt", [128, 2, CHN], F32)
        sq = sb("sq", [128, CHN], F32)
        rs = sb("rs", [128, NT], F32)
        st = sb("st", [128, 4], F32)
        cst, kcst = emit_consts(p, cm, nc, [eps])
        y_t = y_in.rearrange("(n p) d -> n p d", p=128)
        h_t = h_in.rearrange("(n p) d -> n p d", p=128)
        g_t = g_in.rearrange("(n p) d -> n p d", p=128)
        yf_t = yf_out.rearrange("(n p) d -> n p d", p=128)
        for i in range(NT):
            s = i % 2
            ky, kh, kg = ("y", s), ("h", s), ("g", s)
            p.dma(yt[:, s, :], y_t[i], writes=[ky])
            p.dma(ht[:, s, :], h_t[i], writes=[kh])
            p.dma(gt[:, s, :], g_t[i], writes=[kg])
            p.op("vector", "tensor_add", yt[:, s, :], yt[:, s, :], ht[:, s, :],
                 reads=[ky, kh], writes=[ky])
            p.op("scalar", "activation", gt[:, s, :], gt[:, s, :], AF.Silu,
                 reads=[kg], writes=[kg])
            p.op("vector", "tensor_mul", yt[:, s, :], yt[:, s, :], gt[:, s, :],
                 reads=[ky, kg], writes=[ky])
            stores.append(p.dma(yf_t[i], yt[:, s, :], reads=[ky]))
            p.op("scalar", "activation", sq[:], yt[:, s, :], AF.Square,
                 reads=[ky], writes=[("sq",)])
            p.op("vector", "reduce_sum", st[:, 0:1], sq[:], axis=AX.X,
                 reads=[("sq",)], writes=[("st",)])
            p.op("scalar", "activation", st[:, 1:2], st[:, 0:1], AF.Sqrt,
                 scale=1.0 / CHN, bias=cst[:, 0:1], reads=[("st",), kcst],
                 writes=[("st",)])
            p.op("vector", "reciprocal", rs[:, i:i + 1], st[:, 1:2],
                 reads=[("st",)], writes=[("rs",)])
        stores.append(p.dma(rs_out, rs[:], reads=[("rs",)]))
        p.build(nc, stores)
    return nc


def build_factor_stage(R, Kdim, ESL, NTOK, tchunk=1024):
    KT, MT, NCH = Kdim // 128, ESL // 128, NTOK // tchunk
    nc = bass.Bass("TRN2", target_bir_lowering=False)
    f_in = nc.dram_tensor("factors", [R, Kdim, ESL], F32, kind="ExternalInput").ap()
    rhs_in = nc.dram_tensor("rhs", [Kdim, NTOK], F32, kind="ExternalInput").ap()
    b_in = nc.dram_tensor("bias", [128, MT], F32, kind="ExternalInput").ap()
    out_out = nc.dram_tensor("out", [ESL, NTOK], F32, kind="ExternalOutput").ap()
    p = Prog()
    stores = []
    with contextlib.ExitStack() as cm:
        sb = lambda *a: cm.enter_context(nc.sbuf_tensor(*a))
        ps = lambda *a: cm.enter_context(nc.psum_tensor(*a))
        fb = sb("fb", [128, 2, KT, ESL], F32)
        acc = sb("acc", [128, KT, ESL], F32)
        lhs_sb = sb("lhs_sb", [128, KT, ESL], BF16)
        rhs_sb = sb("rhs_sb", [128, 2, KT, tchunk], BF16)
        out_sb = sb("out_sb", [128, 2, tchunk], F32)
        bi_sb = sb("bi_sb", [128, MT], F32)
        psA = ps("psA", [128, 512], F32)
        psB = ps("psB", [128, 512], F32)
        psum = [psA, psB]
        p.dma(bi_sb[:], b_in, writes=[("bi",)])
        for r in range(R):
            s = r % 2
            p.dma(fb[:, s, :, :], f_in[r].rearrange("(k p) e -> p k e", p=128),
                  writes=[("fb", s)], eng="gpsimd")
            if r == 0:
                p.op("vector", "tensor_copy", acc[:], fb[:, s, :, :],
                     reads=[("fb", s)], writes=[("acc",)])
            else:
                p.op("vector", "tensor_add", acc[:], acc[:], fb[:, s, :, :],
                     reads=[("acc",), ("fb", s)], writes=[("acc",)])
        p.op("vector", "tensor_copy", lhs_sb[:], acc[:], reads=[("acc",)],
             writes=[("lhs",)])
        rhs_t = rhs_in.rearrange("(k p) n -> p k n", p=128)
        for nch in range(NCH):
            s = nch % 2
            nsl = slice(nch * tchunk, (nch + 1) * tchunk)
            p.dma(rhs_sb[:, s, :, :], rhs_t[:, :, nsl], writes=[("rhs", s)],
                  eng="gpsimd")
            for mt in range(MT):
                for nn in range(tchunk // 512):
                    pb = (mt * (tchunk // 512) + nn) % 2
                    for k in range(KT):
                        p.op("tensor", "matmul", psum[pb][:],
                             lhs_sb[:, k, mt * 128:(mt + 1) * 128],
                             rhs_sb[:, s, k, nn * 512:(nn + 1) * 512],
                             start=(k == 0), stop=(k == KT - 1),
                             reads=[("lhs",), ("rhs", s)],
                             writes=([("ps", pb)] if k in (0, KT - 1) else []))
                    p.op("vector", "tensor_scalar", out_sb[:, s, nn * 512:(nn + 1) * 512],
                         psum[pb][:], 1.0 / np.sqrt(R), bi_sb[:, mt:mt + 1],
                         ALU.mult, ALU.add, reads=[("ps", pb), ("bi",)],
                         writes=[("out", s)])
                stores.append(p.dma(out_out[mt * 128:(mt + 1) * 128, nsl],
                                    out_sb[:, s, :], reads=[("out", s)]))
        p.build(nc, stores)
    return nc


def build_ssd_stage(HH=32, HD=64, NGC=4, SS=128, T_=2048, CH=128):
    NCH = T_ // CH
    CHN = HH * HD
    CT = CHN // 128
    nc = bass.Bass("TRN2", target_bir_lowering=False)
    hs_in = nc.dram_tensor("hs", [CHN, T_], F32, kind="ExternalInput").ap()
    bm_in = nc.dram_tensor("bmat", [NGC * SS, T_], F32, kind="ExternalInput").ap()
    cm_in = nc.dram_tensor("cmat", [NGC * SS, T_], F32, kind="ExternalInput").ap()
    dtr_in = nc.dram_tensor("dtr", [HH, T_], F32, kind="ExternalInput").ap()
    dtb_in = nc.dram_tensor("dtb", [HH, 1], F32, kind="ExternalInput").ap()
    al_in = nc.dram_tensor("alog", [HH, 1], F32, kind="ExternalInput").ap()
    dv_in = nc.dram_tensor("dvec", [128, CT], F32, kind="ExternalInput").ap()
    mu_in = nc.dram_tensor("masku", [128, 128], F32, kind="ExternalInput").ap()
    sel_in = nc.dram_tensor("sel", [HH, CT, 128], F32, kind="ExternalInput").ap()
    ytok_out = nc.dram_tensor("ytok", [T_, CHN], F32, kind="ExternalOutput").ap()
    hsd_out = nc.dram_tensor("hsd", [CHN, T_], F32, kind="ExternalOutput").ap()
    lam_d = nc.dram_tensor("lam_d", [1, HH, NCH], F32, kind="Internal").ap()
    acum_d = nc.dram_tensor("acum_d", [HH, 1, T_], F32, kind="Internal").ap()

    p = Prog()
    stores = []
    with contextlib.ExitStack() as cm:
        sb = lambda *a: cm.enter_context(nc.sbuf_tensor(*a))
        ps = lambda *a: cm.enter_context(nc.psum_tensor(*a))
        dt_sb = sb("dt_sb", [HH, T_], F32)
        dtbf_sb = sb("dtbf_sb", [HH, T_], BF16)
        acum = sb("acum", [HH, T_], F32)
        ds_f = sb("ds_f", [HH, T_], F32)
        tmpa = ds_f
        onesr = sb("onesr", [HH, 128], F32)
        A_sb = sb("A_sb", [HH, 2], F32)
        lamb = sb("lamb", [HH, NCH], F32)
        acumT = sb("acumT", [128, NCH, HH], F32)
        dsT = sb("dsT", [128, NCH, HH], F32)
        lamrep = sb("lamrep", [128, NCH * HH], F32)
        lamflat = sb("lamflat", [1, HH * NCH], F32)
        caflat = sb("caflat", [1, T_], F32)
        ones1 = sb("ones1", [1, 128], F32)
        eyef = sb("eyef", [128, 128], F32)
        eyeb = sb("eyeb", [128, 128], BF16)
        masku_sb = sb("masku_sb", [128, 128], BF16)
        sel_sb = sb("sel_sb", [HH, CT, 128], BF16)
        dv_sb = sb("dv_sb", [128, CT], F32)
        hsdtT = sb("hsdtT", [128, NCH, CHN], BF16)
        bmat_sb = sb("bmat_sb", [128, NGC, T_], BF16)
        cmat_sb = sb("cmat_sb", [128, NGC, T_], BF16)
        btr = sb("btr", [128, NGC, NCH, 128], BF16)
        cbt_all = sb("cbt_all", [128, NGC, NCH, 128], BF16)
        hs_ct = sb("hs_ct", [128, T_], F32)
        hsd_ct = sb("hsd_ct", [128, T_], F32)
        hsdt_ct = sb("hsdt_ct", [128, T_], BF16)
        cab = sb("cab", [128, 1024], F32)
        eb = sb("eb", [128, 1024], BF16)
        lt = sb("lt", [128, 2, 128], F32)
        mt_sb = sb("mt_sb", [128, 2, 128], BF16)
        csc = sb("csc", [128, 2, 128], BF16)
        bdt = sb("bdt", [128, 2, 128], BF16)
        rst = sb("rst", [128, 64], F32)
        rbf = sb("rbf", [128, 64], BF16)
        yt_sb = sb("yt_sb", [128, 2, 64], F32)
        ps_big = ps("ps_big", [128, 2048], F32)
        ps_trb = ps("ps_trb", [128, 128], BF16)
        ps_cbt = ps("ps_cbt", [128, 128], F32)
        ps_y = ps("ps_y", [128, 64], F32)
        ps_s = ps("ps_s", [128, 64], F32)

        K = dict(dt=("dt",), tmpa=("dsf",), acum=("acum",), dsf=("dsf",),
                 at=("at",), atn=("atn",), dst=("dst",), lam=("lam",),
                 lrep=("lrep",), lflat=("lflat",), caf=("caf",))

        p.op("vector", "memset", ones1[:], 1.0, writes=[("o1",)])
        p.op("vector", "memset", onesr[:], 1.0, writes=[("or",)])
        eye_in = nc.dram_tensor("eye", [128, 128], F32, kind="ExternalInput").ap()
        p.dma(eyef[:], eye_in, writes=[("eyef",)])
        p.dma(eyeb[:], eye_in, writes=[("eyeb",)], eng="gpsimd")
        p.dma(masku_sb[:], mu_in, writes=[("mu",)], eng="gpsimd")
        p.dma(sel_sb[:], sel_in, writes=[("sel",)], eng="gpsimd")
        p.dma(dv_sb[:], dv_in, writes=[("dv",)])
        p.dma(dt_sb[:], dtr_in, writes=[K["dt"]])
        p.dma(A_sb[:, 0:1], al_in, writes=[("A",)])
        p.dma(tmpa[:, 0:1], dtb_in, writes=[K["tmpa"]])

        p.op("vector", "tensor_scalar", dt_sb[:], dt_sb[:], tmpa[:, 0:1], None,
             ALU.add, reads=[K["dt"], K["tmpa"]], writes=[K["dt"]])
        p.op("scalar", "activation", tmpa[:], dt_sb[:], AF.Abs,
             reads=[K["dt"]], writes=[K["tmpa"]])
        p.op("scalar", "activation", tmpa[:], tmpa[:], AF.Exp, scale=-1.0,
             reads=[K["tmpa"]], writes=[K["tmpa"]])
        p.op("scalar", "activation", tmpa[:], tmpa[:], AF.Ln, bias=1.0,
             reads=[K["tmpa"]], writes=[K["tmpa"]])
        p.op("vector", "tensor_relu", dt_sb[:], dt_sb[:],
             reads=[K["dt"]], writes=[K["dt"]])
        p.op("vector", "tensor_add", dt_sb[:], dt_sb[:], tmpa[:],
             reads=[K["dt"], K["tmpa"]], writes=[K["dt"]])
        p.op("vector", "tensor_copy", dtbf_sb[:], dt_sb[:],
             reads=[K["dt"]], writes=[("dtbf",)])
        p.op("scalar", "activation", A_sb[:, 1:2], A_sb[:, 0:1], AF.Exp,
             reads=[("A",)], writes=[("A",)])
        p.op("vector", "tensor_scalar", A_sb[:, 1:2], A_sb[:, 1:2], -1.0, None,
             ALU.mult, reads=[("A",)], writes=[("A",)])
        p.op("vector", "tensor_scalar", tmpa[:], dt_sb[:], A_sb[:, 1:2], None,
             ALU.mult, reads=[K["dt"], ("A",)], writes=[K["tmpa"]])
        for c in range(NCH):
            sl = slice(c * CH, (c + 1) * CH)
            p.op("vector", "tensor_tensor_scan", acum[:, sl], onesr[:, 0:CH],
                 tmpa[:, sl], 0.0, ALU.mult, ALU.add,
                 reads=[("or",), K["tmpa"]], writes=[K["acum"]])
        for c in range(NCH):
            sl = slice(c * CH, (c + 1) * CH)
            p.op("scalar", "activation", ds_f[:, sl], acum[:, sl], AF.Exp,
                 scale=-1.0, bias=acum[:, (c + 1) * CH - 1:(c + 1) * CH],
                 reads=[K["acum"]], writes=[K["dsf"]])
        p.op("vector", "tensor_copy", lamb[:], acum[:, CH - 1::CH],
             reads=[K["acum"]], writes=[K["lam"]])
        p.op("scalar", "activation", lamb[:], lamb[:], AF.Exp,
             reads=[K["lam"]], writes=[K["lam"]])
        d1 = p.dma(lam_d[0], lamb[:], reads=[K["lam"]])
        p.dma(lamflat[:], lam_d.rearrange("o h c -> o (h c)"), deps=[d1],
              writes=[K["lflat"]])
        d2 = p.dma(acum_d[:, 0, :], acum[:], reads=[K["acum"]])
        p.op("tensor", "matmul", ps_big[:, 0:HH * NCH], ones1[:], lamflat[:],
             start=True, stop=True, reads=[("o1",), K["lflat"]],
             writes=[("psbig",)])
        p.op("vector", "tensor_copy", lamrep[:], ps_big[:, 0:HH * NCH],
             reads=[("psbig",)], writes=[K["lrep"]])
        for c in range(NCH):
            sl = slice(c * CH, (c + 1) * CH)
            p.op("tensor", "transpose", ps_big[0:128, 0:HH], acum[:, sl],
                 eyef[0:HH, 0:HH], reads=[K["acum"], ("eyef",)], writes=[("psbig",)])
            p.op("vector", "tensor_copy", acumT[:, c, :], ps_big[0:128, 0:HH],
                 reads=[("psbig",)], writes=[K["at"]])
            p.op("tensor", "transpose", ps_big[0:128, 0:HH], ds_f[:, sl],
                 eyef[0:HH, 0:HH], reads=[K["dsf"], ("eyef",)], writes=[("psbig",)])
            p.op("vector", "tensor_copy", dsT[:, c, :], ps_big[0:128, 0:HH],
                 reads=[("psbig",)], writes=[K["dst"]])

        p.dma(bmat_sb[:], bm_in.rearrange("(g p) t -> p g t", p=128),
              writes=[("bm",)], eng="gpsimd")
        p.dma(cmat_sb[:], cm_in.rearrange("(g p) t -> p g t", p=128),
              writes=[("cm",)], eng="gpsimd")
        for g in range(NGC):
            for c in range(NCH):
                sl = slice(c * CH, (c + 1) * CH)
                p.op("tensor", "transpose", ps_trb[:], bmat_sb[:, g, sl],
                     eyeb[:], reads=[("bm",), ("eyeb",)], writes=[("pstrb",)])
                p.op("vector", "tensor_copy", btr[:, g, c, :], ps_trb[:],
                     reads=[("pstrb",)], writes=[("btr",)])
                p.op("tensor", "matmul", ps_cbt[:], bmat_sb[:, g, sl],
                     cmat_sb[:, g, sl], start=True, stop=True,
                     reads=[("bm",), ("cm",)], writes=[("pscbt",)])
                p.op("vector", "tensor_mul", cbt_all[:, g, c, :], ps_cbt[:],
                     masku_sb[:], reads=[("pscbt",), ("mu",)], writes=[("cbt",)])

        hs_t = hs_in.rearrange("(c p) t -> c p t", p=128)
        hsd_t = hsd_out.rearrange("(c p) t -> c p t", p=128)
        for ct in range(CT):
            s = 0
            p.dma(hs_ct[:], hs_t[ct], writes=[("hs", s)])
            for q in range(4):
                p.op("tensor", "matmul", ps_big[:, q * 512:(q + 1) * 512],
                     sel_sb[:, ct, :], dtbf_sb[:, q * 512:(q + 1) * 512],
                     start=True, stop=True, reads=[("sel",), ("dtbf",)],
                     writes=([("psbig",)] if q in (0, 3) else []))
            p.op("vector", "tensor_mul", hsdt_ct[:], hs_ct[:], ps_big[:],
                 reads=[("hs", s), ("psbig",)], writes=[("hsdt",)])
            p.op("vector", "tensor_scalar", hsd_ct[:], hs_ct[:],
                 dv_sb[:, ct:ct + 1], None, ALU.mult,
                 reads=[("hs", s), ("dv",)], writes=[("hsdo", s)])
            stores.append(p.dma(hsd_t[ct], hsd_ct[:], reads=[("hsdo", s)]))
            for c in range(NCH):
                sl = slice(c * CH, (c + 1) * CH)
                p.op("tensor", "transpose", ps_trb[:], hsdt_ct[:, sl], eyeb[:],
                     reads=[("hsdt",), ("eyeb",)], writes=[("pstrb",)])
                p.op("vector", "tensor_copy", hsdtT[:, c, ct * 128:(ct + 1) * 128],
                     ps_trb[:], reads=[("pstrb",)], writes=[("hsdtT",)])

        ytok_t = ytok_out.rearrange("(c p) d -> c p d", p=128)
        for h in range(HH):
            g = h // (HH // NGC)
            hsl = slice(h * HD, (h + 1) * HD)
            p.dma(caflat[:], acum_d[h], deps=[d2], writes=[K["caf"]])
            for c in range(NCH):
                half, hoff = divmod(c * CH, 1024)
                if hoff == 0:
                    for q in range(2):
                        p.op("tensor", "matmul", ps_big[:, q * 512:(q + 1) * 512],
                             ones1[:], caflat[:, half * 1024 + q * 512:half * 1024 + (q + 1) * 512],
                             start=True, stop=True, reads=[("o1",), K["caf"]],
                             writes=([("psbig",)] if q in (0, 1) else []))
                    p.op("vector", "tensor_copy", cab[:], ps_big[:, 0:1024],
                         reads=[("psbig",)], writes=[("cab",)])
                    p.op("scalar", "activation", eb[:], cab[:], AF.Exp,
                         reads=[("cab",)], writes=[("eb",)])
                sc = c % 2
                sl = slice(c * CH, (c + 1) * CH)
                slh = slice(hoff, hoff + CH)
                p.op("vector", "tensor_scalar", lt[:, sc, :], cab[:, slh],
                     acumT[:, c, h:h + 1], 0.0, ALU.subtract, ALU.min,
                     reads=[("cab",), K["at"]], writes=[("lt", sc)])
                p.op("scalar", "activation", lt[:, sc, :], lt[:, sc, :], AF.Exp,
                     reads=[("lt", sc)], writes=[("lt", sc)])
                p.op("vector", "tensor_mul", mt_sb[:, sc, :], cbt_all[:, g, c, :],
                     lt[:, sc, :], reads=[("cbt",), ("lt", sc)], writes=[("mt", sc)])
                p.op("tensor", "matmul", ps_y[:], mt_sb[:, sc, :],
                     hsdtT[:, c, hsl], start=True, stop=(c == 0),
                     reads=[("mt", sc), ("hsdtT",)], writes=[("psy",)])
                if c > 0:
                    p.op("vector", "tensor_mul", csc[:, sc, :], cmat_sb[:, g, sl],
                         eb[:, slh], reads=[("cm",), ("eb",)], writes=[("csc", sc)])
                    p.op("vector", "tensor_copy", rbf[:], rst[:],
                         reads=[("rst",)], writes=[("rbf",)])
                    p.op("tensor", "matmul", ps_y[:], csc[:, sc, :], rbf[:],
                         start=False, stop=True,
                         reads=[("csc", sc), ("rbf",)], writes=[("psy",)])
                p.op("vector", "tensor_copy", yt_sb[:, sc, :], ps_y[:],
                     reads=[("psy",)], writes=[("yt", sc)])
                stores.append(p.dma(ytok_t[c][:, hsl], yt_sb[:, sc, :],
                                    reads=[("yt", sc)]))
                p.op("vector", "tensor_scalar", bdt[:, sc, :], btr[:, g, c, :],
                     dsT[:, c, h:h + 1], None, ALU.mult,
                     reads=[("btr",), K["dst"]], writes=[("bdt", sc)])
                p.op("tensor", "matmul", ps_s[:], bdt[:, sc, :],
                     hsdtT[:, c, hsl], start=True, stop=True,
                     reads=[("bdt", sc), ("hsdtT",)], writes=[("pss",)])
                if c == 0:
                    p.op("vector", "tensor_scalar", rst[:], ps_s[:], 1.0, None,
                         ALU.mult, reads=[("pss",)], writes=[("rst",)])
                else:
                    p.op("vector", "scalar_tensor_tensor", rst[:], rst[:],
                         lamrep[:, h * NCH + c:h * NCH + c + 1], ps_s[:],
                         ALU.mult, ALU.add,
                         reads=[("rst",), K["lrep"], ("pss",)], writes=[("rst",)])
        p.build(nc, stores)
    return nc


_PROG_CACHE = {}


def _prog(key, fn, *args, **kw):
    if key not in _PROG_CACHE:
        _PROG_CACHE[key] = fn(*args, **kw)
    return _PROG_CACHE[key]


def _run(nc, in_maps, outs):
    res = run_bass_kernel_spmd(nc, in_maps, core_ids=list(range(NCORES)))
    return [[np.asarray(res.results[c][o]) for c in range(NCORES)] for o in outs]


def _cc(a):
    return np.ascontiguousarray(a, dtype=np.float32)


def kernel(**ins):
    x = _cc(ins["L0_attn_hidden_states"])
    xf = x.reshape(TOK, DM)
    eye = np.eye(128, dtype=np.float32)
    toks = [slice(c * TPC, (c + 1) * TPC) for c in range(NCORES)]

    def ln_plain(f):
        nc = _prog("ln_plain", build_ln_stage, TPC, DM, False, True, final_f32=True)
        (o,) = _run(nc, [{"f": _cc(f[t])} for t in toks], ["xn"])
        return np.concatenate(o, axis=0)

    def ln_res(f, res):
        nc = _prog("ln_res", build_ln_stage, TPC, DM, True, True, final_f32=True)
        xo, xn = _run(nc, [{"f": _cc(f[t]), "res": _cc(res[t])} for t in toks],
                      ["xo", "xn"])
        return np.concatenate(xo, axis=0), np.concatenate(xn, axis=0)

    def ln_res_final(f, res):
        nc = _prog("ln_resf", build_ln_stage, TPC, DM, True, False)
        (xo,) = _run(nc, [{"f": _cc(f[t]), "res": _cc(res[t])} for t in toks],
                     ["xo"])
        return np.concatenate(xo, axis=0)

    W = _cc(ins["L0_attn_in_proj_w"])
    cw = _cc(ins["L0_attn_conv_w"])
    cb = _cc(ins["L0_attn_conv_b"])
    xpw = _cc(ins["L0_attn_x_proj_w"])
    dpw = _cc(ins["L0_attn_dt_proj_w"])
    dpb_ = _cc(ins["L0_attn_dt_proj_b"])
    Al = _cc(ins["L0_attn_A_log"])
    Dv = _cc(ins["L0_attn_D"])
    opw = _cc(ins["L0_attn_out_proj_w"])
    eye = np.eye(128, dtype=np.float32)
    nc = _prog("mega5", build_mega, 5)
    w1 = _cc(ins["L1_attn_in_proj_w"])
    cw1 = _cc(ins["L1_attn_conv_w"][:, 0, :].T)
    cb1 = _cc(ins["L1_attn_conv_b"])
    dtb1_ = _cc(ins["L1_attn_dt_bias"]); al1 = _cc(ins["L1_attn_A_log"])
    Dv1 = _cc(ins["L1_attn_D"]); nw1 = _cc(ins["L1_attn_norm_w"])
    opw1 = _cc(ins["L1_attn_out_proj_w"])
    masku_ = np.triu(np.ones((128, 128), np.float32))
    fac = _cc(ins["L1_ffn_factors"])
    bias1 = _cc(ins["L1_ffn_bias"])
    sel_ = np.zeros((32, 16, 128), np.float32)
    for h32 in range(32):
        sel_[h32, (h32 * 64) // 128, (h32 * 64) % 128:(h32 * 64) % 128 + 64] = 1.0
    gwT = _cc(ins["L0_ffn_gate_w"].T)
    uwT = _cc(ins["L0_ffn_up_w"].T)
    dwT = _cc(ins["L0_ffn_down_w"].T)
    gsc = pcol(ins["L0_ffn_gate_s"]); usc = pcol(ins["L0_ffn_up_s"])
    dsc = pcol(ins["L0_ffn_down_s"])
    im = []
    for c in range(NCORES):
        b, j = divmod(c, 2)
        dsl = slice(j * 2048, (j + 1) * 2048)
        Ws = np.concatenate([W[j * 2048:(j + 1) * 2048],
                             W[4096 + j * 2048:4096 + (j + 1) * 2048]], axis=0)
        im.append({"x_my": _cc(xf[c * 1024:(c + 1) * 1024]),
                   "w_ip": _cc(Ws.T),
                   "cw0": _cc(cw[dsl]), "cb0": pcol(cb[dsl]),
                   "w_xp": _cc(xpw[:, dsl].T),
                   "w_dt": _cc(dpw[dsl].T),
                   "dpb": pcol(dpb_[dsl]),
                   "alog": _cc(Al[dsl]),
                   "dvq": pcol(Dv[dsl]),
                   "w_op": _cc(opw[:, dsl].T),
                   "ident": eye,
                   "w_g": gwT, "w_u": uwT, "w_d": dwT,
                   "gsq": gsc, "usq": usc, "dsq": dsc,
                   "w_ip1": _cc(np.concatenate([
                       w1[:, 10240 + j * 32:10240 + (j + 1) * 32],
                       np.zeros((DM, 96), np.float32),
                       w1[:, j * 2048:(j + 1) * 2048],
                       w1[:, 4096 + j * 2048:4096 + (j + 1) * 2048],
                       w1[:, 8192 + j * 512:8192 + (j + 1) * 512],
                       w1[:, 9216 + j * 512:9216 + (j + 1) * 512]], axis=1)),
                   "cw1s": _cc(np.concatenate([
                       cw1[j * 2048:(j + 1) * 2048],
                       cw1[4096 + j * 512:4096 + (j + 1) * 512],
                       cw1[5120 + j * 512:5120 + (j + 1) * 512]], axis=0)),
                   "cb1q": pcol(np.concatenate([
                       cb1[j * 2048:(j + 1) * 2048],
                       cb1[4096 + j * 512:4096 + (j + 1) * 512],
                       cb1[5120 + j * 512:5120 + (j + 1) * 512]]), 24),
                   "dtb1": _cc(dtb1_[j * 32:(j + 1) * 32, None]),
                   "alog1": _cc(al1[j * 32:(j + 1) * 32, None]),
                   "dvec1": pcol(np.repeat(Dv1[j * 32:(j + 1) * 32], 64), 16),
                   "masku": masku_, "sel1": sel_,
                   "w_op1": _cc(opw1[j * 2048:(j + 1) * 2048, :]),
                   "nwq": pcol(nw1[j * 2048:(j + 1) * 2048], 16),
                   "fac_sl": _cc(fac[:, :, j * 1024:(j + 1) * 1024]),
                   "fbias": pcol(bias1[j * 1024:(j + 1) * 1024], 8),
                   "moff4": np.array([[q * 512 * 2048 + j * 1024
                                       for q in range(4)]], np.int32)})
    (x4_s,) = _run(nc, im, ["x4_o"])
    x4 = np.concatenate(x4_s, axis=0)
    return x4.reshape(BATCH, T, DM)


RG_PAIRS = [[0, 1], [2, 3], [4, 5], [6, 7]]


def build_mega(upto=1):
    nc = bass.Bass("TRN2", target_bir_lowering=False)
    p = Prog()
    stores = []

    x_my = nc.dram_tensor("x_my", [1024, DM], F32, kind="ExternalInput").ap()
    w_ip = nc.dram_tensor("w_ip", [DM, 4096], F32, kind="ExternalInput").ap()
    cw0 = nc.dram_tensor("cw0", [2048, 4], F32, kind="ExternalInput").ap()
    cb0 = nc.dram_tensor("cb0", [128, 16], F32, kind="ExternalInput").ap()
    xnh_d = nc.dram_tensor("xnh_d", [1024, DM], BF16, kind="Internal").ap()
    xng_d = nc.dram_tensor("xng_d", [T, DM], BF16, kind="Internal").ap()
    xz_d = nc.dram_tensor("xz_d", [4096, T], BF16, kind="Internal").ap()
    if upto == 1:
        xact_o = nc.dram_tensor("xact_o", [2048, T], F32, kind="ExternalOutput").ap()
        z_o = nc.dram_tensor("z_o", [2048, T], F32, kind="ExternalOutput").ap()
    if upto >= 2:
        w_xp = nc.dram_tensor("w_xp", [2048, 160], F32, kind="ExternalInput").ap()
        w_dt = nc.dram_tensor("w_dt", [128, 2048], F32, kind="ExternalInput").ap()
        dpb = nc.dram_tensor("dpb", [128, 16], F32, kind="ExternalInput").ap()
        alog = nc.dram_tensor("alog", [2048, 16], F32, kind="ExternalInput").ap()
        dvq = nc.dram_tensor("dvq", [128, 16], F32, kind="ExternalInput").ap()
        w_op = nc.dram_tensor("w_op", [2048, 2048], F32, kind="ExternalInput").ap()
        ident = nc.dram_tensor("ident", [128, 128], F32, kind="ExternalInput").ap()
        xab_d = nc.dram_tensor("xab_d", [2048, T], BF16, kind="Internal").ap()
        xdbl_p = nc.dram_tensor("xdbl_p", [1, 160, T], F32, kind="Internal").ap()
        xdbl_d = nc.dram_tensor("xdbl_d", [1, 160, T], F32, kind="Internal").ap()
        dt_d = nc.dram_tensor("dt_d", [2048, T], F32, kind="Internal").ap()
        yg_d = nc.dram_tensor("yg_d", [2048, T], BF16, kind="Internal").ap()
        po_d = nc.dram_tensor("po_d", [4096, 1024], BF16, kind="Internal").ap()
        at_d = nc.dram_tensor("at_d", [2048, 1024], BF16, kind="Internal").ap()
        if upto == 2:
            x1_o = nc.dram_tensor("x1_o", [1024, DM], F32, kind="ExternalOutput").ap()
            x1n_o = nc.dram_tensor("x1n_o", [1024, DM], F32, kind="ExternalOutput").ap()
    if upto >= 3:
        x1_o = nc.dram_tensor("x1_d", [1024, DM], F32, kind="Internal").ap()
        x1n_o = None
        x1nb_d = nc.dram_tensor("x1nb_d", [1024, DM], BF16, kind="Internal").ap()
        w_g = nc.dram_tensor("w_g", [DM, 5632], F32, kind="ExternalInput").ap()
        w_u = nc.dram_tensor("w_u", [DM, 5632], F32, kind="ExternalInput").ap()
        w_d = nc.dram_tensor("w_d", [5632, DM], F32, kind="ExternalInput").ap()
        gsq = nc.dram_tensor("gsq", [128, 44], F32, kind="ExternalInput").ap()
        usq = nc.dram_tensor("usq", [128, 44], F32, kind="ExternalInput").ap()
        dsq = nc.dram_tensor("dsq", [128, 16], F32, kind="ExternalInput").ap()
        ffo_d = nc.dram_tensor("ffo_d", [DM, 1024], BF16, kind="Internal").ap()
        if upto == 3:
            x2_o = nc.dram_tensor("x2_o", [1024, DM], F32, kind="ExternalOutput").ap()
            x2n_o = nc.dram_tensor("x2n_o", [1024, DM], F32, kind="ExternalOutput").ap()
    if upto >= 4:
        x2_o = nc.dram_tensor("x2_d", [1024, DM], F32, kind="Internal").ap()
        x2n_o = None
        x2nb_d = nc.dram_tensor("x2nb_d", [1024, DM], BF16, kind="Internal").ap()
        xng2_d = nc.dram_tensor("xng2_d", [T, DM], BF16, kind="Internal").ap()
        w_ip1 = nc.dram_tensor("w_ip1", [DM, 5248], F32, kind="ExternalInput").ap()
        cw1s = nc.dram_tensor("cw1s", [3072, 4], F32, kind="ExternalInput").ap()
        cb1q = nc.dram_tensor("cb1q", [128, 24], F32, kind="ExternalInput").ap()
        dtb1 = nc.dram_tensor("dtb1", [32, 1], F32, kind="ExternalInput").ap()
        alog1 = nc.dram_tensor("alog1", [32, 1], F32, kind="ExternalInput").ap()
        dvec1 = nc.dram_tensor("dvec1", [128, 16], F32, kind="ExternalInput").ap()
        masku = nc.dram_tensor("masku", [128, 128], F32, kind="ExternalInput").ap()
        sel1 = nc.dram_tensor("sel1", [32, 16, 128], F32, kind="ExternalInput").ap()
        w_op1 = nc.dram_tensor("w_op1", [2048, 2048], F32, kind="ExternalInput").ap()
        nwq = nc.dram_tensor("nwq", [128, 16], F32, kind="ExternalInput").ap()
        gate_d = nc.dram_tensor("gate_d", [2048, T], BF16, kind="Internal").ap()
        hbc_d = nc.dram_tensor("hbc_d", [3072, T], BF16, kind="Internal").ap()
        dtr_d = nc.dram_tensor("dtr_d", [32, T], F32, kind="Internal").ap()
        hbca_d = nc.dram_tensor("hbca_d", [3072, T], BF16, kind="Internal").ap()
        ytok_d = nc.dram_tensor("ytok_d", [T, 2048], BF16, kind="Internal").ap()
        hsd_d = nc.dram_tensor("hsd_d", [2048, T], BF16, kind="Internal").ap()
        rms_p = nc.dram_tensor("rms_p", [1, 1, T], F32, kind="Internal").ap()
        rms_o = nc.dram_tensor("rms_o", [1, 1, T], F32, kind="Internal").ap()
        po1_d = nc.dram_tensor("po1_d", [4096, 1024], BF16, kind="Internal").ap()
        at1_d = nc.dram_tensor("at1_d", [2048, 1024], BF16, kind="Internal").ap()
        if upto == 4:
            x3_o = nc.dram_tensor("x3_o", [1024, DM], F32, kind="ExternalOutput").ap()
            x3n_o = nc.dram_tensor("x3n_o", [1024, DM], F32, kind="ExternalOutput").ap()
    if upto >= 5:
        x3_o = nc.dram_tensor("x3_d", [1024, DM], F32, kind="Internal").ap()
        x3n_o = None
        x3nb_d = nc.dram_tensor("x3nb_d", [1024, DM], BF16, kind="Internal").ap()
        x3ng_d = nc.dram_tensor("x3ng_d", [T, DM], BF16, kind="Internal").ap()
        fac_sl = nc.dram_tensor("fac_sl", [8, DM, 1024], F32, kind="ExternalInput").ap()
        fbias = nc.dram_tensor("fbias", [128, 8], F32, kind="ExternalInput").ap()
        moff4 = nc.dram_tensor("moff4", [1, 4], mybir.dt.int32, kind="ExternalInput").ap()
        mixp_d = nc.dram_tensor("mixp_d", [1024, T], BF16, kind="Internal").ap()
        mixg_h = nc.dram_tensor("mixg_d", [DM, T], BF16, kind="Internal")
        mixg_d = mixg_h.ap()
        mixmy_d = nc.dram_tensor("mixmy_d", [DM, 1024], BF16, kind="Internal").ap()
        x4_o = nc.dram_tensor("x4_o", [1024, DM], F32, kind="ExternalOutput").ap()

    with contextlib.ExitStack() as outer:
        p.attach(nc, outer)

        with contextlib.ExitStack() as cm:
            sb = lambda nm, shp, dt_: cm.enter_context(nc.sbuf_tensor(_u(nm), shp, dt_))
            ft = sb("ft", [128, 2, DM], F32)
            sq = sb("sq", [128, 2, DM], F32)
            on = sb("on", [128, 2, DM], BF16)
            st = sb("st", [128, 2, 8], F32)
            cst, kcst = emit_consts(p, cm, nc, [1e-5])
            x_t = x_my.rearrange("(n p) d -> n p d", p=128)
            xn_t = xnh_d.rearrange("(n p) d -> n p d", p=128)
            for i in range(8):
                s = i % 2
                kf, ksq, kon, kst = ("ft", s), ("sq", s), ("on", s), ("st", s)
                p.dma(ft[:, s, :], x_t[i], writes=[kf])
                emit_ln(p, ft[:, s, :], on[:, s, :], st[:, s, :], sq[:, s, :],
                        kf, kon, kst, ksq, DM, cst[:, 0:1], kcst)
                stores.append(p.dma(xn_t[i], on[:, s, :], reads=[kon],
                                    writes=[("xnh",)]))
            p.flush()
        p.barrier()

        p.cc("AllGather", ALU.bypass, RG_PAIRS, [xnh_d], [xng_d],
             reads=[("xnh",)], writes=[("xng",)])
        p.flush()
        p.barrier()

        KT, M, N = 16, 4096, T
        with contextlib.ExitStack() as cm:
            sb = lambda nm, shp, dt_: cm.enter_context(nc.sbuf_tensor(_u(nm), shp, dt_))
            ps = lambda nm, shp, dt_: cm.enter_context(nc.psum_tensor(_u(nm), shp, dt_))
            rhs_sb = sb("rhs_sb", [128, KT, N], BF16)
            lhs_sb = sb("lhs_sb", [128, 2, KT, 128], BF16)
            out_sb = sb("out_sb", [128, 2, N], BF16)
            psA = ps("psA", [128, 512], F32)
            psB = ps("psB", [128, 512], F32)
            psum = [psA, psB]
            buf = sb("buf", [128, 2, T + 4], BF16)
            acc = sb("acc", [128, 2, T], F32)
            ot = sb("ot", [128, 2, T], F32)
            cw_sb = sb("cw_sb", [128, 16, 4], F32)
            cb_sb = sb("cb_sb", [128, 16], F32)
            obf = sb("obf", [128, 2, T], BF16)
            p.dma(cw_sb[:], cw0.rearrange("(c p) k -> p c k", p=128),
                  writes=[("cw",)])
            p.dma(cb_sb[:], cb0, writes=[("cb",)])
            xz_ct = xz_d.rearrange("(ct p) t -> ct p t", p=128)
            xa_t = xact_o.rearrange("(ct p) t -> ct p t", p=128) if upto == 1 else None
            xab_t = xab_d.rearrange("(ct p) t -> ct p t", p=128) if upto >= 2 else None

            def emit_conv0(ct):
                s = ct % 2
                kb, ka, ko = ("buf", s), ("acc", s), ("ot", s)
                if ct < 2:
                    p.op("vector", "memset", buf[:, s, :], 0.0, writes=[kb])
                p.dma(buf[:, s, 3:3 + T], xz_ct[ct], reads=[("xz", ct)],
                      writes=[("bufc", s)],
                      deps=[p.last_writer[kb]] if ct < 2 else [])
                kbc = ("bufc", s)
                p.op("vector", "tensor_scalar", acc[:, s, :], buf[:, s, 0:T],
                     cw_sb[:, ct, 0:1], None, ALU.mult,
                     reads=[kbc, ("cw",)], writes=[ka])
                for k in range(1, 4):
                    p.op("vector", "scalar_tensor_tensor", acc[:, s, :],
                         buf[:, s, k:k + T], cw_sb[:, ct, k:k + 1],
                         acc[:, s, :], ALU.mult, ALU.add,
                         reads=[kbc, ("cw",), ka], writes=[ka])
                p.op("scalar", "activation", ot[:, s, :], acc[:, s, :],
                     AF.Silu, bias=cb_sb[:, ct:ct + 1],
                     reads=[ka, ("cb",)], writes=[ko])
                if upto == 1:
                    stores.append(p.dma(xa_t[ct], ot[:, s, :], reads=[ko]))
                else:
                    p.op("gpsimd", "tensor_copy", obf[:, s, :], ot[:, s, :],
                         reads=[ko], writes=[("obf", s)])
                    stores.append(p.dma(xab_t[ct], obf[:, s, :],
                                        reads=[("obf", s)], writes=[("xab", ct)]))

            for k in range(KT):
                p.dma(rhs_sb[:, k, :], xng_d[:, k * 128:(k + 1) * 128],
                      transpose=True, reads=[("xng",)], writes=[("rhs", k)])
            xz_t = xz_d.rearrange("(mt p) n -> mt p n", p=128)
            for mt in range(M // 128):
                mb = mt % 2
                slab = w_ip[:, mt * 128:(mt + 1) * 128].rearrange(
                    "(k p) m -> p k m", p=128)
                for k0 in range(0, KT, 8):
                    k1 = min(KT, k0 + 8)
                    p.dma(lhs_sb[:, mb, k0:k1, :], slab[:, k0:k1, :],
                          writes=[("lhs", mb, k) for k in range(k0, k1)],
                          eng="gpsimd")
                for nch in range(N // 512):
                    pb = (mt * (N // 512) + nch) % 2
                    pt = psum[pb]
                    for k in range(KT):
                        p.op("tensor", "matmul", pt[:], lhs_sb[:, mb, k, :],
                             rhs_sb[:, k, nch * 512:(nch + 1) * 512],
                             start=(k == 0), stop=(k == KT - 1),
                             reads=[("lhs", mb, k), ("rhs", k)],
                             writes=([("ps", pb)] if k in (0, KT - 1) else []))
                    p.op("vector", "tensor_copy",
                         out_sb[:, mb, nch * 512:(nch + 1) * 512], pt[:],
                         reads=[("ps", pb)], writes=[("oxz", mb)])
                stores.append(p.dma(xz_t[mt], out_sb[:, mb, :],
                                    reads=[("oxz", mb)], writes=[("xz", mt)], eng="scalar"))
                if 2 <= mt < 18:
                    emit_conv0(mt - 2)
            p.flush()
        p.barrier()

        if upto == 1:
            for q in range(4):
                stores.append(p.dma(z_o[q * 512:(q + 1) * 512, :],
                                    xz_d[2048 + q * 512:2048 + (q + 1) * 512, :],
                                    reads=[("xz", m) for m in
                                           range(16 + q * 4, 20 + q * 4)],
                                    eng="gpsimd"))
            p.finish(stores)
            return nc

        with contextlib.ExitStack() as cm:
            sb = lambda nm, shp, dt_: cm.enter_context(nc.sbuf_tensor(_u(nm), shp, dt_))
            ps = lambda nm, shp, dt_: cm.enter_context(nc.psum_tensor(_u(nm), shp, dt_))
            rhs_sb = sb("rhs_sb", [128, 16, T], BF16)
            lhs_sb = sb("lhs_sb", [128, 16, 160], BF16)
            out_sb = sb("out_sb", [128, 2, T], F32)
            psA = ps("psA", [128, 512], F32)
            psB = ps("psB", [128, 512], F32)
            psum = [psA, psB]
            xab_t2 = xab_d.rearrange("(k p) t -> p k t", p=128)
            for k0 in range(0, 16, 8):
                p.dma(rhs_sb[:, k0:k0 + 8, :], xab_t2[:, k0:k0 + 8, :],
                      reads=[("xab", k) for k in range(k0, k0 + 8)],
                      writes=[("rq", k) for k in range(k0, k0 + 8)])
            p.dma(lhs_sb[:], w_xp.rearrange("(k p) m -> p k m", p=128),
                  writes=[("lx",)], eng="gpsimd")
            for mt in range(2):
                pm = 128 if mt == 0 else 32
                for nch in range(T // 512):
                    pb = (mt * 4 + nch) % 2
                    pt = psum[pb]
                    for k in range(16):
                        p.op("tensor", "matmul", pt[0:pm, :],
                             lhs_sb[:, k, mt * 128:mt * 128 + pm],
                             rhs_sb[:, k, nch * 512:(nch + 1) * 512],
                             start=(k == 0), stop=(k == 15),
                             reads=[("lx",), ("rq", k)],
                             writes=([("ps", pb)] if k in (0, 15) else []))
                    p.op("vector", "tensor_copy",
                         out_sb[0:pm, mt, nch * 512:(nch + 1) * 512], pt[0:pm, :],
                         reads=[("ps", pb)], writes=[("oxp", mt)])
                stores.append(p.dma(xdbl_p[0, mt * 128:mt * 128 + pm, :],
                                    out_sb[0:pm, mt, :], reads=[("oxp", mt)],
                                    writes=[("xdblp",)]))
            p.flush()
        p.barrier()

        p.cc("AllReduce", ALU.add, RG_PAIRS, [xdbl_p], [xdbl_d],
             reads=[("xdblp",)], writes=[("xdbl",)])
        p.flush()
        p.barrier()

        with contextlib.ExitStack() as cm:
            sb = lambda nm, shp, dt_: cm.enter_context(nc.sbuf_tensor(_u(nm), shp, dt_))
            ps = lambda nm, shp, dt_: cm.enter_context(nc.psum_tensor(_u(nm), shp, dt_))
            rhs_sb = sb("rhs_sb", [128, T], BF16)
            lhs_sb = sb("lhs_sb", [128, 2048], BF16)
            out_sb = sb("out_sb", [128, 2, T], F32)
            tmp1 = sb("tmp1", [128, 512], F32)
            tmp2 = sb("tmp2", [128, 512], F32)
            bi_sb = sb("bi_sb", [128, 16], F32)
            psA = ps("psA", [128, 512], F32)
            psB = ps("psB", [128, 512], F32)
            psum = [psA, psB]
            p.dma(rhs_sb[:], xdbl_d[0, 0:128, :], reads=[("xdbl",)],
                  writes=[("rdt",)], eng="gpsimd")
            p.dma(lhs_sb[:], w_dt, writes=[("ldt",)], eng="gpsimd")
            p.dma(bi_sb[:], dpb, writes=[("bdt",)])
            dt_t = dt_d.rearrange("(mt p) t -> mt p t", p=128)
            for mt in range(16):
                mb = mt % 2
                for nch in range(T // 512):
                    pb = (mt * 4 + nch) % 2
                    pt = psum[pb]
                    nsl = slice(nch * 512, (nch + 1) * 512)
                    mm = p.op("tensor", "matmul", pt[:],
                              lhs_sb[:, mt * 128:(mt + 1) * 128],
                              rhs_sb[:, nsl], start=True, stop=True,
                              reads=[("ldt",), ("rdt",)], writes=[("ps", pb)])
                    p.op("vector", "tensor_scalar", tmp1[:], pt[:],
                         bi_sb[:, mt:mt + 1], None, ALU.add,
                         reads=[("ps", pb), ("bdt",)], writes=[("t1",)],
                         deps=[mm])
                    p.op("scalar", "activation", tmp2[:], tmp1[:], AF.Abs,
                         reads=[("t1",)], writes=[("t2",)])
                    p.op("scalar", "activation", tmp2[:], tmp2[:], AF.Exp,
                         scale=-1.0, reads=[("t2",)], writes=[("t2",)])
                    p.op("scalar", "activation", tmp2[:], tmp2[:], AF.Ln,
                         bias=1.0, reads=[("t2",)], writes=[("t2",)])
                    p.op("vector", "tensor_relu", tmp1[:], tmp1[:],
                         reads=[("t1",)], writes=[("t1",)])
                    p.op("vector", "tensor_add", out_sb[:, mb, nsl], tmp1[:],
                         tmp2[:], reads=[("t1",), ("t2",)], writes=[("odt", mb)])
                stores.append(p.dma(dt_t[mt], out_sb[:, mb, :],
                                    reads=[("odt", mb)], writes=[("dt",)]))
            p.flush()
        p.barrier()

        emit_scan_fused(p, nc, stores, dt_d, xab_d, xz_d, xdbl_d, alog, dvq,
                        ident, yg_d, w_op, po_d)
        p.barrier()

        p.cc("ReduceScatter", ALU.add, RG_PAIRS, [po_d], [at_d],
             reads=[("po", 0), ("po", 1)], writes=[("at",)])
        p.flush()
        p.barrier()

        with contextlib.ExitStack() as cm:
            sb = lambda nm, shp, dt_: cm.enter_context(nc.sbuf_tensor(_u(nm), shp, dt_))
            ft = sb("ft", [128, 2, DM], BF16)
            rt = sb("rt", [128, 2, DM], F32)
            sq = sb("sq", [128, 2, DM], F32)
            ot = sb("ot", [128, 2, DM], F32)
            on = sb("on", [128, 2, DM], F32)
            onb = sb("onb", [128, 2, DM], BF16)
            st = sb("st", [128, 2, 8], F32)
            cst, kcst = emit_consts(p, cm, nc, [1e-5])
            x_t = x_my.rearrange("(n p) d -> n p d", p=128)
            x1_t = x1_o.rearrange("(n p) d -> n p d", p=128)
            x1n_t = x1n_o.rearrange("(n p) d -> n p d", p=128) if upto == 2 else None
            x1nb_t = x1nb_d.rearrange("(n p) d -> n p d", p=128) if upto >= 3 else None
            for i in range(8):
                s = i % 2
                kf, kr, ksq, kot, kon, kst = (("f2", s), ("r2", s), ("q2", s),
                                              ("o2", s), ("n2", s), ("s2", s))
                p.dma(ft[:, s, :], at_d[:, i * 128:(i + 1) * 128],
                      transpose=True, reads=[("at",)], writes=[kf])
                p.dma(rt[:, s, :], x_t[i], writes=[kr])
                emit_ln(p, ft[:, s, :], ot[:, s, :], st[:, s, :], sq[:, s, :],
                        kf, kot, kst, ksq, DM, cst[:, 0:1], kcst)
                p.op("vector", "tensor_add", ot[:, s, :], ot[:, s, :],
                     rt[:, s, :], reads=[kot, kr], writes=[kot])
                stores.append(p.dma(x1_t[i], ot[:, s, :], reads=[kot],
                                    writes=[("x1",)]))
                if upto == 2:
                    emit_ln(p, ot[:, s, :], on[:, s, :], st[:, s, :],
                            sq[:, s, :], kot, kon, kst, ksq, DM,
                            cst[:, 0:1], kcst)
                    stores.append(p.dma(x1n_t[i], on[:, s, :], reads=[kon]))
                else:
                    emit_ln(p, ot[:, s, :], onb[:, s, :], st[:, s, :],
                            sq[:, s, :], kot, kon, kst, ksq, DM,
                            cst[:, 0:1], kcst)
                    stores.append(p.dma(x1nb_t[i], onb[:, s, :], reads=[kon],
                                        writes=[("x1nb",)]))
            p.flush()
        p.barrier()
        if upto == 2:
            p.finish(stores)
            return nc

        with contextlib.ExitStack() as cm:
            sb = lambda nm, shp, dt_: cm.enter_context(nc.sbuf_tensor(_u(nm), shp, dt_))
            ps = lambda nm, shp, dt_: cm.enter_context(nc.psum_tensor(_u(nm), shp, dt_))
            rhs_sb = sb("frhs", [128, 16, 1024], BF16)
            lg_sb = sb("flg", [128, 2, 16, 128], BF16)
            lu_sb = sb("flu", [128, 2, 16, 128], BF16)
            hdn_sb = sb("fhdn", [128, 44, 1024], BF16)
            ga_sb = sb("fga", [128, 512], F32)
            gs_sb = sb("fgs", [128, 44], F32)
            us_sb = sb("fus", [128, 44], F32)
            ds_sb = sb("fds", [128, 16], F32)
            ld_sb = sb("fld", [128, 2, 44, 128], BF16)
            out_sb = sb("fout", [128, 2, 1024], BF16)
            psg = ps("fpsg", [128, 512], F32)
            psu = ps("fpsu", [128, 512], F32)
            psum = [psg, psu]
            p.dma(gs_sb[:], gsq, writes=[("fgs",)])
            p.dma(us_sb[:], usq, writes=[("fus",)])
            p.dma(ds_sb[:], dsq, writes=[("fds",)])
            for k in range(16):
                p.dma(rhs_sb[:, k, :], x1nb_d[:, k * 128:(k + 1) * 128],
                      transpose=True, reads=[("x1nb",)], writes=[("fr", k)])
            for mt in range(44):
                mb = mt % 2
                for W, buf, nm in ((w_g, lg_sb, "flgk"), (w_u, lu_sb, "fluk")):
                    slab = W[:, mt * 128:(mt + 1) * 128].rearrange(
                        "(k p) m -> p k m", p=128)
                    for k0 in range(0, 16, 8):
                        p.dma(buf[:, mb, k0:k0 + 8, :], slab[:, k0:k0 + 8, :],
                              writes=[(nm, mb, k) for k in range(k0, k0 + 8)],
                              eng="gpsimd")
                for nch in range(2):
                    nsl = slice(nch * 512, (nch + 1) * 512)
                    for k in range(16):
                        p.op("tensor", "matmul", psg[:], lg_sb[:, mb, k, :],
                             rhs_sb[:, k, nsl], start=(k == 0), stop=(k == 15),
                             reads=[("flgk", mb, k), ("fr", k)],
                             writes=([("fpsg",)] if k in (0, 15) else []))
                    for k in range(16):
                        p.op("tensor", "matmul", psu[:], lu_sb[:, mb, k, :],
                             rhs_sb[:, k, nsl], start=(k == 0), stop=(k == 15),
                             reads=[("fluk", mb, k), ("fr", k)],
                             writes=([("fpsu",)] if k in (0, 15) else []))
                    p.op("scalar", "activation", ga_sb[:], psg[:], AF.Silu,
                         scale=gs_sb[:, mt:mt + 1], reads=[("fpsg",), ("fgs",)],
                         writes=[("fga",)])
                    p.op("vector", "tensor_scalar", hdn_sb[:, mt, nsl], psu[:],
                         us_sb[:, mt:mt + 1], None, ALU.mult,
                         reads=[("fpsu",), ("fus",)], writes=[("fh", mt)])
                    p.op("vector", "tensor_mul", hdn_sb[:, mt, nsl],
                         hdn_sb[:, mt, nsl], ga_sb[:],
                         reads=[("fh", mt), ("fga",)], writes=[("fh", mt)])
            ffo_t = ffo_d.rearrange("(mt p) t -> mt p t", p=128)
            for mt in range(16):
                mb = mt % 2
                slab = w_d[:, mt * 128:(mt + 1) * 128].rearrange(
                    "(k p) m -> p k m", p=128)
                for k0 in range(0, 44, 8):
                    k1 = min(44, k0 + 8)
                    p.dma(ld_sb[:, mb, k0:k1, :], slab[:, k0:k1, :],
                          writes=[("fldk", mb, k) for k in range(k0, k1)],
                          eng="gpsimd")
                for nch in range(2):
                    pb = nch % 2
                    pt = psum[pb]
                    nsl = slice(nch * 512, (nch + 1) * 512)
                    for k in range(44):
                        p.op("tensor", "matmul", pt[:], ld_sb[:, mb, k, :],
                             hdn_sb[:, k, nsl], start=(k == 0), stop=(k == 43),
                             reads=[("fldk", mb, k), ("fh", k)],
                             writes=([("fps2", pb)] if k in (0, 43) else []))
                    p.op("vector", "tensor_scalar", out_sb[:, mb, nsl], pt[:],
                         ds_sb[:, mt:mt + 1], None, ALU.mult,
                         reads=[("fps2", pb), ("fds",)], writes=[("fo", mb)])
                stores.append(p.dma(ffo_t[mt], out_sb[:, mb, :],
                                    reads=[("fo", mb)], writes=[("ffo",)], eng="scalar"))
            p.flush()
        p.barrier()

        with contextlib.ExitStack() as cm:
            sb = lambda nm, shp, dt_: cm.enter_context(nc.sbuf_tensor(_u(nm), shp, dt_))
            ft = sb("ft", [128, 2, DM], BF16)
            rt = sb("rt", [128, 2, DM], F32)
            sq = sb("sq", [128, 2, DM], F32)
            ot = sb("ot", [128, 2, DM], F32)
            on = sb("on", [128, 2, DM], F32)
            st = sb("st", [128, 2, 8], F32)
            cst, kcst = emit_consts(p, cm, nc, [1e-5])
            onb = sb("onb", [128, 2, DM], BF16)
            x1_t2 = x1_o.rearrange("(n p) d -> n p d", p=128)
            x2_t = x2_o.rearrange("(n p) d -> n p d", p=128)
            x2n_t = x2n_o.rearrange("(n p) d -> n p d", p=128) if upto == 3 else None
            x2nb_t = x2nb_d.rearrange("(n p) d -> n p d", p=128) if upto >= 4 else None
            for i in range(8):
                s = i % 2
                kf, kr, ksq, kot, kon, kst = (("f3", s), ("r3", s), ("q3", s),
                                              ("o3", s), ("n3", s), ("s3", s))
                p.dma(ft[:, s, :], ffo_d[:, i * 128:(i + 1) * 128],
                      transpose=True, reads=[("ffo",)], writes=[kf])
                p.dma(rt[:, s, :], x1_t2[i], reads=[("x1",)], writes=[kr])
                emit_ln(p, ft[:, s, :], ot[:, s, :], st[:, s, :], sq[:, s, :],
                        kf, kot, kst, ksq, DM, cst[:, 0:1], kcst)
                p.op("vector", "tensor_add", ot[:, s, :], ot[:, s, :],
                     rt[:, s, :], reads=[kot, kr], writes=[kot])
                stores.append(p.dma(x2_t[i], ot[:, s, :], reads=[kot],
                                    writes=[("x2",)]))
                if upto == 3:
                    emit_ln(p, ot[:, s, :], on[:, s, :], st[:, s, :],
                            sq[:, s, :], kot, kon, kst, ksq, DM,
                            cst[:, 0:1], kcst)
                    stores.append(p.dma(x2n_t[i], on[:, s, :], reads=[kon]))
                else:
                    emit_ln(p, ot[:, s, :], onb[:, s, :], st[:, s, :],
                            sq[:, s, :], kot, kon, kst, ksq, DM,
                            cst[:, 0:1], kcst)
                    stores.append(p.dma(x2nb_t[i], onb[:, s, :], reads=[kon],
                                        writes=[("x2nb",)]))
            p.flush()
        p.barrier()
        if upto == 3:
            p.finish(stores)
            return nc

        fhalf_d = nc.dram_tensor("fhalf_d", [DM, 1024], BF16, kind="Internal").ap()
        with contextlib.ExitStack() as cm:
            sb = lambda nm, shp, dt_: cm.enter_context(nc.sbuf_tensor(_u(nm), shp, dt_))
            fb = sb("ffb", [128, 2, 16, 256], F32)
            facc = sb("facc", [128, 16, 256], F32)
            flhsb = sb("flhsb", [128, 16, 256], BF16)
            fh_t = fhalf_d.rearrange("(k p) e -> p k e", p=128)
            for e0 in range(4):
                esl = slice(e0 * 256, (e0 + 1) * 256)
                for r in range(8):
                    s = r % 2
                    p.dma(fb[:, s, :, :],
                          fac_sl[r][:, esl].rearrange("(k p) e -> p k e", p=128),
                          writes=[("ffb", s)], eng="gpsimd")
                    if r == 0:
                        p.op("vector", "tensor_copy", facc[:],
                             fb[:, s, :, :], reads=[("ffb", s)],
                             writes=[("fac",)])
                    else:
                        p.op("vector", "tensor_add", facc[:],
                             facc[:], fb[:, s, :, :],
                             reads=[("fac",), ("ffb", s)],
                             writes=[("fac",)])
                p.op("vector", "tensor_copy", flhsb[:], facc[:],
                     reads=[("fac",)], writes=[("flhb",)])
                stores.append(p.dma(fh_t[:, :, esl], flhsb[:],
                                    reads=[("flhb",)], writes=[("fhalf",)],
                                    eng="scalar"))
            p.flush()

        p.cc("AllGather", ALU.bypass, RG_PAIRS, [x2nb_d], [xng2_d],
             reads=[("x2nb",)], writes=[("xng2",)])
        p.flush()
        p.barrier()
        with contextlib.ExitStack() as cm:
            sb = lambda nm, shp, dt_: cm.enter_context(nc.sbuf_tensor(_u(nm), shp, dt_))
            ps = lambda nm, shp, dt_: cm.enter_context(nc.psum_tensor(_u(nm), shp, dt_))
            rhs_sb = sb("rhs1", [128, 16, T], BF16)
            lhs_sb = sb("lhs1", [128, 2, 16, 128], BF16)
            out_sb = sb("out1", [128, 2, T], BF16)
            outf_sb = sb("outf1", [128, T], F32)
            psA = ps("ps1A", [128, 512], F32)
            psB = ps("ps1B", [128, 512], F32)
            psum = [psA, psB]
            buf = sb("c1buf", [128, 2, T + 4], BF16)
            acc = sb("c1acc", [128, 2, T], F32)
            cot = sb("c1ot", [128, 2, T], BF16)
            cw_sb = sb("c1cw", [128, 24, 4], F32)
            cb_sb = sb("c1cb", [128, 24], F32)
            p.dma(cw_sb[:], cw1s.rearrange("(c p) k -> p c k", p=128),
                  writes=[("c1cw",)])
            p.dma(cb_sb[:], cb1q, writes=[("c1cb",)])
            hbc_t = hbc_d.rearrange("(ct p) t -> ct p t", p=128)
            hbca_t = hbca_d.rearrange("(ct p) t -> ct p t", p=128)

            def emit_conv1(ct):
                s = ct % 2
                kb, ka, ko = ("c1b", s), ("c1a", s), ("c1o", s)
                if ct < 2:
                    p.op("vector", "memset", buf[:, s, :], 0.0, writes=[kb])
                p.dma(buf[:, s, 1:1 + T], hbc_t[ct], reads=[("hbc", ct)],
                      writes=[("c1bc", s)],
                      deps=[p.last_writer[kb]] if ct < 2 else [])
                kbc = ("c1bc", s)
                p.op("vector", "tensor_scalar", acc[:, s, :], buf[:, s, 0:T],
                     cw_sb[:, ct, 0:1], None, ALU.mult,
                     reads=[kbc, ("c1cw",)], writes=[ka])
                for k in range(1, 4):
                    p.op("vector", "scalar_tensor_tensor", acc[:, s, :],
                         buf[:, s, k:k + T], cw_sb[:, ct, k:k + 1],
                         acc[:, s, :], ALU.mult, ALU.add,
                         reads=[kbc, ("c1cw",), ka], writes=[ka])
                p.op("scalar", "activation", cot[:, s, :], acc[:, s, :],
                     AF.Silu, bias=cb_sb[:, ct:ct + 1],
                     reads=[ka, ("c1cb",)], writes=[ko])
                stores.append(p.dma(hbca_t[ct], cot[:, s, :], reads=[ko],
                                    writes=[("hbca",)]))

            for k in range(16):
                p.dma(rhs_sb[:, k, :], xng2_d[:, k * 128:(k + 1) * 128],
                      transpose=True, reads=[("xng2",)], writes=[("r1", k)])
            for mt in range(41):
                mb = mt % 2
                slab = w_ip1[:, mt * 128:(mt + 1) * 128].rearrange(
                    "(k p) m -> p k m", p=128)
                for k0 in range(0, 16, 8):
                    p.dma(lhs_sb[:, mb, k0:k0 + 8, :], slab[:, k0:k0 + 8, :],
                          writes=[("l1", mb, k) for k in range(k0, k0 + 8)],
                          eng="gpsimd")
                for nch in range(T // 512):
                    pb = (mt * 4 + nch) % 2
                    pt = psum[pb]
                    nsl = slice(nch * 512, (nch + 1) * 512)
                    for k in range(16):
                        p.op("tensor", "matmul", pt[:], lhs_sb[:, mb, k, :],
                             rhs_sb[:, k, nsl], start=(k == 0), stop=(k == 15),
                             reads=[("l1", mb, k), ("r1", k)],
                             writes=([("p1", pb)] if k in (0, 15) else []))
                    if mt == 0:
                        p.op("vector", "tensor_copy", outf_sb[:, nsl], pt[:],
                             reads=[("p1", pb)], writes=[("of1",)])
                    else:
                        p.op("vector", "tensor_copy", out_sb[:, mb, nsl], pt[:],
                             reads=[("p1", pb)], writes=[("o1", mb)])
                if mt == 0:
                    stores.append(p.dma(dtr_d, outf_sb[0:32, :],
                                        reads=[("of1",)], writes=[("dtr",)]))
                elif mt <= 16:
                    stores.append(p.dma(gate_d[(mt - 1) * 128:mt * 128, :],
                                        out_sb[:, mb, :], reads=[("o1", mb)],
                                        writes=[("gate",)], eng="scalar"))
                else:
                    stores.append(p.dma(hbc_d[(mt - 17) * 128:(mt - 16) * 128, :],
                                        out_sb[:, mb, :], reads=[("o1", mb)],
                                        writes=[("hbc", mt - 17)], eng="scalar"))
                if 19 <= mt:
                    emit_conv1(mt - 19)
            for ct in (22, 23):
                emit_conv1(ct)
            p.flush()
        p.barrier()

        emit_ssd_fused(p, nc, stores, hbca_d, dtr_d, dtb1, alog1, dvec1,
                       masku, sel1, ident, ytok_d, hsd_d)
        p.barrier()

        with contextlib.ExitStack() as cm:
            sb = lambda nm, shp, dt_: cm.enter_context(nc.sbuf_tensor(_u(nm), shp, dt_))
            ps = lambda nm, shp, dt_: cm.enter_context(nc.psum_tensor(_u(nm), shp, dt_))
            yt = sb("gyt", [128, 2, T], BF16)
            gt = sb("ggt", [128, 2, T], BF16)
            ht = sb("ght", [128, 2, T], BF16)
            yf_sb = sb("gyf", [128, 16, T], BF16)
            gate_t = gate_d.rearrange("(ct p) t -> ct p t", p=128)
            hsd_t = hsd_d.rearrange("(ct p) t -> ct p t", p=128)
            for ct in range(16):
                s = ct % 2
                p.dma(yt[:, s, :], ytok_d[:, ct * 128:(ct + 1) * 128],
                      transpose=True, reads=[("ytok",)], writes=[("gy", s)])
                p.dma(gt[:, s, :], gate_t[ct], reads=[("gate",)],
                      writes=[("gg", s)])
                p.dma(ht[:, s, :], hsd_t[ct], reads=[("hsd",)],
                      writes=[("gh", s)], eng="gpsimd")
                p.op("vector", "tensor_add", yt[:, s, :], yt[:, s, :],
                     ht[:, s, :], reads=[("gy", s), ("gh", s)],
                     writes=[("gy", s)])
                p.op("scalar", "activation", gt[:, s, :], gt[:, s, :], AF.Silu,
                     reads=[("gg", s)], writes=[("gg", s)])
                p.op("vector", "tensor_mul", yf_sb[:, ct, :], yt[:, s, :],
                     gt[:, s, :], reads=[("gy", s), ("gg", s)],
                     writes=[("gyf", ct)])
            yf_sb2 = yf_sb
            nw_sb = sb("onw", [128, 16], F32)
            lhs_sb = sb("olhs", [128, 2, 16, 128], BF16)
            out_sb = sb("oout", [128, 2, T], BF16)
            psA = ps("opsA", [128, 512], F32)
            psB = ps("opsB", [128, 512], F32)
            psum = [psA, psB]
            p.dma(nw_sb[:], nwq, writes=[("onw",)])
            for ct in range(16):
                p.op("vector", "tensor_scalar", yf_sb2[:, ct, :],
                     yf_sb2[:, ct, :], nw_sb[:, ct:ct + 1], None, ALU.mult,
                     reads=[("gyf", ct), ("onw",)], writes=[("gyf", ct)])
            for mt in range(16):
                mb = mt % 2
                slab = w_op1[:, mt * 128:(mt + 1) * 128].rearrange(
                    "(k p) m -> p k m", p=128)
                for k0 in range(0, 16, 8):
                    p.dma(lhs_sb[:, mb, k0:k0 + 8, :], slab[:, k0:k0 + 8, :],
                          writes=[("ol", mb, k) for k in range(k0, k0 + 8)],
                          eng="gpsimd")
                for nch in range(T // 512):
                    pb = (mt * 4 + nch) % 2
                    pt = psum[pb]
                    for k in range(16):
                        p.op("tensor", "matmul", pt[:], lhs_sb[:, mb, k, :],
                             yf_sb2[:, k, nch * 512:(nch + 1) * 512],
                             start=(k == 0), stop=(k == 15),
                             reads=[("ol", mb, k), ("gyf", k)],
                             writes=([("op", pb)] if k in (0, 15) else []))
                    p.op("vector", "tensor_copy",
                         out_sb[:, mb, nch * 512:(nch + 1) * 512], pt[:],
                         reads=[("op", pb)], writes=[("oo", mb)])
                stores.append(p.dma(po1_d[mt * 128:(mt + 1) * 128, :],
                                    out_sb[:, mb, 0:1024], reads=[("oo", mb)],
                                    writes=[("po1",)], eng="scalar"))
                stores.append(p.dma(po1_d[2048 + mt * 128:2048 + (mt + 1) * 128, :],
                                    out_sb[:, mb, 1024:2048], reads=[("oo", mb)],
                                    writes=[("po1",)], eng="scalar"))
            p.flush()
        p.barrier()
        p.cc("ReduceScatter", ALU.add, RG_PAIRS, [po1_d], [at1_d],
             reads=[("po1",)], writes=[("at1",)])
        p.flush()
        p.barrier()

        with contextlib.ExitStack() as cm:
            sb = lambda nm, shp, dt_: cm.enter_context(nc.sbuf_tensor(_u(nm), shp, dt_))
            ft = sb("ft4", [128, 2, DM], BF16)
            rt = sb("rt4", [128, 2, DM], F32)
            sq = sb("sq4", [128, 2, DM], F32)
            ot = sb("ot4", [128, 2, DM], F32)
            on = sb("on4", [128, 2, DM], F32)
            st = sb("st4", [128, 2, 8], F32)
            cst, kcst = emit_consts(p, cm, nc, [1e-5])
            onb5 = sb("onb5", [128, 2, DM], BF16)
            x2_t2 = x2_o.rearrange("(n p) d -> n p d", p=128)
            x3_t = x3_o.rearrange("(n p) d -> n p d", p=128)
            x3n_t = x3n_o.rearrange("(n p) d -> n p d", p=128) if upto == 4 else None
            x3nb_t = x3nb_d.rearrange("(n p) d -> n p d", p=128) if upto >= 5 else None
            for i in range(8):
                s = i % 2
                kf, kr, ksq, kot, kon, kst = (("f4", s), ("r4", s), ("q4", s),
                                              ("o4", s), ("n4", s), ("s4", s))
                p.dma(ft[:, s, :], at1_d[:, i * 128:(i + 1) * 128],
                      transpose=True, reads=[("at1",)], writes=[kf])
                p.dma(rt[:, s, :], x2_t2[i], reads=[("x2",)], writes=[kr])
                emit_ln(p, ft[:, s, :], ot[:, s, :], st[:, s, :], sq[:, s, :],
                        kf, kot, kst, ksq, DM, cst[:, 0:1], kcst)
                p.op("vector", "tensor_add", ot[:, s, :], ot[:, s, :],
                     rt[:, s, :], reads=[kot, kr], writes=[kot])
                stores.append(p.dma(x3_t[i], ot[:, s, :], reads=[kot],
                                    writes=[("x3",)]))
                if upto == 4:
                    emit_ln(p, ot[:, s, :], on[:, s, :], st[:, s, :],
                            sq[:, s, :], kot, kon, kst, ksq, DM,
                            cst[:, 0:1], kcst)
                    stores.append(p.dma(x3n_t[i], on[:, s, :], reads=[kon]))
                else:
                    emit_ln(p, ot[:, s, :], onb5[:, s, :], st[:, s, :],
                            sq[:, s, :], kot, kon, kst, ksq, DM,
                            cst[:, 0:1], kcst)
                    stores.append(p.dma(x3nb_t[i], onb5[:, s, :], reads=[kon],
                                        writes=[("x3nb",)]))
            p.flush()
        p.barrier()
        if upto == 4:
            p.finish(stores)
            return nc

        p.cc("AllGather", ALU.bypass, RG_PAIRS, [x3nb_d], [x3ng_d],
             reads=[("x3nb",)], writes=[("x3ng",)])
        p.flush()
        p.barrier()
        with contextlib.ExitStack() as cm:
            sb = lambda nm, shp, dt_: cm.enter_context(nc.sbuf_tensor(_u(nm), shp, dt_))
            ps = lambda nm, shp, dt_: cm.enter_context(nc.psum_tensor(_u(nm), shp, dt_))
            flhs = sb("flhs", [128, 16, 1024], BF16)
            rhs_sb = sb("frh", [128, 16, T], BF16)
            out_sb = sb("fob", [128, 2, T], BF16)
            fb_sb = sb("fbc", [128, 8], F32)
            psA = ps("fpA", [128, 512], F32)
            psB = ps("fpB", [128, 512], F32)
            psum = [psA, psB]
            p.dma(fb_sb[:], fbias, writes=[("fbi",)])
            for k in range(16):
                p.dma(rhs_sb[:, k, :], x3ng_d[:, k * 128:(k + 1) * 128],
                      transpose=True, reads=[("x3ng",)], writes=[("fr5", k)])
            for k0 in range(0, 16, 8):
                p.dma(flhs[:, k0:k0 + 8, :],
                      fhalf_d.rearrange("(k p) e -> p k e", p=128)[:, k0:k0 + 8, :],
                      reads=[("fhalf",)], writes=[("flh", k) for k in range(k0 // 4, k0 // 4 + 2)])
            mixp_t = mixp_d.rearrange("(mt p) t -> mt p t", p=128)
            for mt in range(8):
                mb = mt % 2
                for nch in range(T // 512):
                    pb = (mt * 4 + nch) % 2
                    pt = psum[pb]
                    nsl = slice(nch * 512, (nch + 1) * 512)
                    for k in range(16):
                        p.op("tensor", "matmul", pt[:],
                             flhs[:, k, mt * 128:(mt + 1) * 128],
                             rhs_sb[:, k, nsl], start=(k == 0), stop=(k == 15),
                             reads=[("flh", 0), ("flh", 1), ("flh", 2), ("flh", 3), ("fr5", k)],
                             writes=([("fp5", pb)] if k in (0, 15) else []))
                    p.op("vector", "tensor_scalar", out_sb[:, mb, nsl], pt[:],
                         1.0 / np.sqrt(8.0), fb_sb[:, mt:mt + 1],
                         ALU.mult, ALU.add, reads=[("fp5", pb), ("fbi",)],
                         writes=[("fo5", mb)])
                stores.append(p.dma(mixp_t[mt], out_sb[:, mb, :],
                                    reads=[("fo5", mb)], writes=[("mixp",)], eng="scalar"))
            p.flush()
        p.barrier()

        p.cc("AllGather", ALU.bypass, RG_PAIRS, [mixp_d], [mixg_d],
             reads=[("mixp",)], writes=[("mixg",)])
        p.flush()
        p.barrier()
        with contextlib.ExitStack() as cm:
            sb = lambda nm, shp, dt_: cm.enter_context(nc.sbuf_tensor(_u(nm), shp, dt_))
            offs = sb("moffs", [1, 4], mybir.dt.int32)
            bounce = sb("mbnc", [128, 16, 1024], BF16)
            p.dma(offs[:], moff4, writes=[("mof",)])
            for q in range(4):
                rows = slice(q * 512, (q + 1) * 512)
                p.dyn_dma(bounce[:, q * 4:(q + 1) * 4, :], mixg_h,
                          [[T, 512], [1, 1024]], offs[0:1, q:q + 1],
                          reads=[("mixg",), ("mof",)], writes=[("mbn", q)])
                p.dma(mixmy_d[rows, :],
                      bounce[:, q * 4:(q + 1) * 4, :].rearrange("p k t -> (k p) t") if False else bounce[:, q * 4:(q + 1) * 4, :],
                      reads=[("mbn", q)], writes=[("mixmy",)])
            p.flush()
        p.barrier()

        with contextlib.ExitStack() as cm:
            sb = lambda nm, shp, dt_: cm.enter_context(nc.sbuf_tensor(_u(nm), shp, dt_))
            ft = sb("ft5", [128, 2, DM], BF16)
            rt = sb("rt5", [128, 2, DM], F32)
            sq = sb("sq5", [128, 2, DM], F32)
            ot = sb("ot5", [128, 2, DM], F32)
            st = sb("st5", [128, 2, 8], F32)
            cst, kcst = emit_consts(p, cm, nc, [1e-5])
            x3_t2 = x3_o.rearrange("(n p) d -> n p d", p=128)
            x4_t = x4_o.rearrange("(n p) d -> n p d", p=128)
            for i in range(8):
                s = i % 2
                kf, kr, ksq, kot, kst = (("f5", s), ("r5", s), ("q5", s),
                                         ("o5", s), ("s5", s))
                p.dma(ft[:, s, :], mixmy_d[:, i * 128:(i + 1) * 128],
                      transpose=True, reads=[("mixmy",)], writes=[kf])
                p.dma(rt[:, s, :], x3_t2[i], reads=[("x3",)], writes=[kr])
                emit_ln(p, ft[:, s, :], ot[:, s, :], st[:, s, :], sq[:, s, :],
                        kf, kot, kst, ksq, DM, cst[:, 0:1], kcst)
                p.op("vector", "tensor_add", ot[:, s, :], ot[:, s, :],
                     rt[:, s, :], reads=[kot, kr], writes=[kot])
                stores.append(p.dma(x4_t[i], ot[:, s, :], reads=[kot]))
            p.flush()
        p.finish(stores)
    return nc


def emit_scan_fused(p, nc, stores, dt_d, xab_d, xz_d, xdbl_d, alog, dvq,
                    ident, yg_d, w_op, po_d, TC=1024, NST=16):
    DT4 = 16
    NTC = T // TC
    PAR = 4
    with contextlib.ExitStack() as cm:
        sb = lambda nm, shp, dt_: cm.enter_context(nc.sbuf_tensor(_u(nm), shp, dt_))
        ps = lambda nm, shp, dt_: cm.enter_context(nc.psum_tensor(_u(nm), shp, dt_))
        dt_sb = sb("dt_sb", [128, 2, TC], F32)
        xa_sb = sb("xa_sb", [128, 2, TC], BF16)
        dtx_sb = sb("dtx_sb", [128, 2, TC], BF16)
        dA_sb = sb("dA_sb", [128, PAR, TC], BF16)
        dBx_sb = sb("dBx_sb", [128, PAR, TC], BF16)
        h_sb = sb("h_sb", [128, PAR, TC], BF16)
        p_sb = sb("p_sb", [128, PAR, TC], BF16)
        bb_all = sb("bb_all", [128, NST, TC], BF16)
        cb_all = sb("cb_all", [128, NST, TC], BF16)
        z_sb = sb("z_sb", [128, TC], BF16)
        yd_sb = sb("yd_sb", [128, TC], BF16)
        sz_sb = sb("sz_sb", [128, TC], BF16)
        yo_sb = sb("yo_sb", [128, TC], BF16)
        hcar = sb("hcar", [128, DT4, NST], F32)
        A_sb = sb("A_sb", [128, DT4, NST], F32)
        dv_sb = sb("dv_sb", [128, DT4], F32)
        bro_sb = sb("bro_sb", [33, NST, TC // 2], BF16)
        cro_sb = sb("cro_sb", [33, NST, TC // 2], BF16)
        onesb = sb("onesb", [33, 128], BF16)
        id_sb = sb("id_sb", [128, 128], BF16)
        oprhs = sb("oprhs", [128, 16, TC], BF16)
        oplhs = sb("oplhs", [128, 2, 16, 128], BF16)
        opout = sb("opout", [128, 2, TC], BF16)
        psy = [ps(f"psy{i}", [128, TC], F32) for i in range(2)]
        psb = ps("psb", [128, 2, 512], F32)
        psO = [ps(f"psO{i}", [128, 512], F32) for i in range(2)]

        p.op("vector", "memset", onesb[:], 1.0, writes=[("sones",)])
        p.dma(id_sb[:], ident, writes=[("sid",)], eng="gpsimd")
        p.dma(A_sb[:], alog.rearrange("(c p) n -> p c n", p=128),
              writes=[("sA",)])
        p.dma(dv_sb[:], dvq, writes=[("sdv",)])
        for c in range(DT4):
            p.op("scalar", "activation", A_sb[:, c, :], A_sb[:, c, :], AF.Exp,
                 reads=[("sA",)], writes=[("sA",)])
            p.op("vector", "tensor_scalar", A_sb[:, c, :], A_sb[:, c, :],
                 -1.0, None, ALU.mult, reads=[("sA",)], writes=[("sA",)])

        dt_t = dt_d.rearrange("(c p) t -> c p t", p=128)
        xa_t = xab_d.rearrange("(c p) t -> c p t", p=128)
        z_t = xz_d.rearrange("(c p) t -> c p t", p=128)
        yg_t = yg_d.rearrange("(c p) t -> c p t", p=128)

        yg_t2 = yg_d.rearrange("(k p) t -> p k t", p=128)

        def emit_bc(tc):
            for qq in range(2):
                qtsl = slice(tc * TC + qq * 512, tc * TC + (qq + 1) * 512)
                p.dma(bro_sb[32 * qq:32 * qq + 1, :, :],
                      xdbl_d[:, 128:144, qtsl],
                      reads=[("xdbl",)], writes=[("sbro",)], eng="gpsimd")
                p.dma(cro_sb[32 * qq:32 * qq + 1, :, :],
                      xdbl_d[:, 144:160, qtsl],
                      reads=[("xdbl",)], writes=[("scro",)], eng="gpsimd")
            for n in range(NST):
                for qq in range(2):
                    sn = (n * 2 + qq) % 2
                    qsl = slice(qq * 512, (qq + 1) * 512)
                    p.op("tensor", "matmul", psb[:, sn, :],
                         onesb[32 * qq:32 * qq + 1, :],
                         bro_sb[32 * qq:32 * qq + 1, n, :], start=True,
                         stop=True,
                         reads=[("sones",), ("sbro",)], writes=[("spsb", sn)])
                    p.op("scalar", "copy", bb_all[:, n, qsl], psb[:, sn, :],
                         reads=[("spsb", sn)], writes=[("sbb", n)])
                    p.op("tensor", "matmul", psb[:, sn, :],
                         onesb[32 * qq:32 * qq + 1, :],
                         cro_sb[32 * qq:32 * qq + 1, n, :], start=True,
                         stop=True,
                         reads=[("sones",), ("scro",)], writes=[("spsb", sn)])
                    p.op("scalar", "copy", cb_all[:, n, qsl], psb[:, sn, :],
                         reads=[("spsb", sn)], writes=[("scb", n)])

        def emit_op_loads(tc):
            for k0 in range(0, 16, 8):
                p.dma(oprhs[:, k0:k0 + 8, :],
                      yg_t2[:, k0:k0 + 8, tc * TC:(tc + 1) * TC],
                      reads=[("yg", tc)],
                      writes=[("ro", k) for k in range(k0, k0 + 8)])

        def emit_op_mts(tc, mts):
            for mt in mts:
                lb = mt % 2
                mb = mt % 2
                slab = w_op[:, mt * 128:(mt + 1) * 128].rearrange(
                    "(k p) m -> p k m", p=128)
                for k0 in range(0, 16, 8):
                    p.dma(oplhs[:, lb, k0:k0 + 8, :], slab[:, k0:k0 + 8, :],
                          writes=[("lo", lb, k) for k in range(k0, k0 + 8)],
                          eng="gpsimd")
                for nch in range(TC // 512):
                    pb = (mt * (TC // 512) + nch) % 2
                    pt = psO[pb]
                    nsl = slice(nch * 512, (nch + 1) * 512)
                    for k in range(16):
                        p.op("tensor", "matmul", pt[:], oplhs[:, lb, k, :],
                             oprhs[:, k, nsl], start=(k == 0), stop=(k == 15),
                             reads=[("lo", lb, k), ("ro", k)],
                             writes=([("pso", pb)] if k in (0, 15) else []))
                    p.op("scalar", "copy", opout[:, mb, nsl], pt[:],
                         reads=[("pso", pb)], writes=[("opo", mb)])
                stores.append(p.dma(
                    po_d[tc * 2048 + mt * 128:tc * 2048 + (mt + 1) * 128, :],
                    opout[:, mb, :], reads=[("opo", mb)],
                    writes=[("po", tc)], eng="scalar"))

        emit_bc(0)
        for tc in range(NTC):
            tsl = slice(tc * TC, (tc + 1) * TC)
            if tc > 0:
                emit_op_loads(tc - 1)
            for dgp in range(8):
                for ci in range(2):
                    c = dgp * 2 + ci
                    p.dma(dt_sb[:, ci, :], dt_t[c, :, tsl], reads=[("dt",)],
                          writes=[("sdt", ci)])
                    p.dma(xa_sb[:, ci, :], xa_t[c, :, tsl], reads=[("xab", c)],
                          writes=[("sxa", ci)], eng="scalar")
                    p.op("vector", "tensor_mul", dtx_sb[:, ci, :],
                         dt_sb[:, ci, :], xa_sb[:, ci, :],
                         reads=[("sdt", ci), ("sxa", ci)], writes=[("sdtx", ci)])

                for n in range(NST):
                    for ci in range(2):
                        c = dgp * 2 + ci
                        sn = (n * 2 + ci) % PAR
                        p.op("scalar", "activation", dA_sb[:, sn, :],
                             dt_sb[:, ci, :], AF.Exp,
                             scale=A_sb[:, c, n:n + 1],
                             reads=[("sdt", ci), ("sA",)],
                             writes=[("sdA", sn)])
                        p.op("vector", "tensor_mul", dBx_sb[:, sn, :],
                             dtx_sb[:, ci, :], bb_all[:, n, :],
                             reads=[("sdtx", ci), ("sbb", n)],
                             writes=[("sdBx", sn)])
                        scan_init = 0.0 if tc == 0 else hcar[:, c, n:n + 1]
                        p.op("vector", "tensor_tensor_scan", h_sb[:, sn, :],
                             dA_sb[:, sn, :], dBx_sb[:, sn, :], scan_init,
                             ALU.mult, ALU.add,
                             reads=[("sdA", sn), ("sdBx", sn)] +
                                   ([("shc", c, n)] if tc else []),
                             writes=[("sh", sn)])
                        if tc < NTC - 1:
                            p.op("scalar", "copy", hcar[:, c, n:n + 1],
                                 h_sb[:, sn, TC - 1:TC], reads=[("sh", sn)],
                                 writes=[("shc", c, n)])
                        p.op("gpsimd", "tensor_mul", p_sb[:, sn, :],
                             h_sb[:, sn, :], cb_all[:, n, :],
                             reads=[("sh", sn), ("scb", n)],
                             writes=[("sp", sn)])
                        for qq in range(2):
                            p.op("tensor", "matmul",
                                 psy[ci][:, qq * 512:(qq + 1) * 512], id_sb[:],
                                 p_sb[:, sn, qq * 512:(qq + 1) * 512],
                                 start=(n == 0), stop=(n == NST - 1),
                                 reads=[("sid",), ("sp", sn)],
                                 writes=([("spsy", ci)] if n in (0, NST - 1) else []))
                for ci in range(2):
                    c = dgp * 2 + ci
                    p.dma(z_sb[:], z_t[16 + c, :, tsl],
                          reads=[("xz", 16 + c)], writes=[("sz",)])
                    p.op("vector", "scalar_tensor_tensor", yd_sb[:],
                         xa_sb[:, ci, :], dv_sb[:, c:c + 1], psy[ci][:],
                         ALU.mult, ALU.add,
                         reads=[("sxa", ci), ("sdv",), ("spsy", ci)],
                         writes=[("syd",)])
                    p.op("scalar", "activation", sz_sb[:], z_sb[:], AF.Silu,
                         reads=[("sz",)], writes=[("ssz",)])
                    p.op("vector", "tensor_mul", yo_sb[:], yd_sb[:], sz_sb[:],
                         reads=[("syd",), ("ssz",)], writes=[("syo",)])
                    stores.append(p.dma(yg_t[c, :, tsl], yo_sb[:],
                                        reads=[("syo",)],
                                        writes=[("yg", tc)], eng="scalar"))
                if tc > 0:
                    emit_op_mts(tc - 1, [2 * dgp, 2 * dgp + 1])
            if tc + 1 < NTC:
                emit_bc(tc + 1)
        emit_op_loads(NTC - 1)
        for mp in range(8):
            emit_op_mts(NTC - 1, [2 * mp, 2 * mp + 1])
        p.flush()


def emit_ssd_fused(p, nc, stores, hbca_d, dtr_d, dtb1, alog1, dvec1, mu_in,
                   sel_in, eye_in, ytok_d, hsd_d):
    HH, HD, NGC, SS, CH = 32, 64, 4, 128, 128
    T_ = T
    NCH = T_ // CH
    CHN = HH * HD
    CT = CHN // 128
    lam_d = nc.dram_tensor(_u("lam_d"), [1, HH, NCH], F32, kind="Internal").ap()
    acum_d = nc.dram_tensor(_u("acum_d"), [HH, 1, T_], F32, kind="Internal").ap()
    with contextlib.ExitStack() as cm, contextlib.ExitStack() as cm2:
        sb = lambda nm, shp, dt_: cm.enter_context(nc.sbuf_tensor(_u(nm), shp, dt_))
        sb2 = lambda nm, shp, dt_: cm2.enter_context(nc.sbuf_tensor(_u(nm), shp, dt_))
        ps = lambda nm, shp, dt_: cm.enter_context(nc.psum_tensor(_u(nm), shp, dt_))
        A_sb = sb("zA", [HH, 2], F32)
        lamb = sb("zlamb", [HH, NCH], F32)
        acumT = sb("zacumT", [128, NCH, HH], F32)
        dsT = sb("zdsT", [128, NCH, HH], F32)
        lamrep = sb("zlamrep", [128, NCH * HH], F32)
        ones1 = sb("zones1", [1, 128], F32)
        dv_sb = sb("zdv", [128, CT], F32)
        hsdtT = sb("zhsdtT", [128, NCH, CHN], BF16)
        cmat_sb = sb("zcmat", [128, NGC, T_], BF16)
        btr = sb("zbtr", [128, NGC, NCH, 128], BF16)
        cbt_all = sb("zcbt", [128, NGC, NCH, 128], BF16)
        ps2 = lambda nm, shp, dt_: cm2.enter_context(nc.psum_tensor(_u(nm), shp, dt_))
        lamflat = sb2("zlamflat", [1, HH * NCH], F32)
        ps_big = ps2("zpsbig", [128, 2048], F32)
        ps_trb = ps2("zpstrb", [128, 128], BF16)
        ps_cbt = ps2("zpscbt", [128, 128], F32)
        dt_sb = sb2("zdt", [HH, T_], F32)
        dtbf_sb = sb2("zdtbf", [HH, T_], BF16)
        acum = sb2("zacum", [HH, T_], F32)
        ds_f = sb2("zdsf", [HH, T_], F32)
        tmpa = ds_f
        onesr = sb2("zonesr", [HH, 128], F32)
        eyef = sb2("zeyef", [128, 128], F32)
        eyeb = sb2("zeyeb", [128, 128], BF16)
        masku_sb = sb2("zmasku", [128, 128], BF16)
        sel_sb = sb2("zsel", [HH, CT, 128], BF16)
        bmat_sb = sb2("zbmat", [128, NGC, T_], BF16)
        hs_ct = sb2("zhsct", [128, T_], BF16)
        hsd_ct = sb2("zhsdct", [128, T_], BF16)
        hsdt_ct = sb2("zhsdtct", [128, T_], BF16)

        K = {k: ("Z" + k,) for k in ("dt", "tmpa", "acum", "dsf", "at", "dst",
                                     "lam", "lrep", "lflat", "caf")}
        K["tmpa"] = K["dsf"]

        p.op("vector", "memset", ones1[:], 1.0, writes=[("Zo1",)])
        p.op("vector", "memset", onesr[:], 1.0, writes=[("Zor",)])
        p.dma(eyef[:], eye_in, writes=[("Zeyef",)])
        p.dma(eyeb[:], eye_in, writes=[("Zeyeb",)], eng="gpsimd")
        p.dma(masku_sb[:], mu_in, writes=[("Zmu",)], eng="gpsimd")
        p.dma(sel_sb[:], sel_in, writes=[("Zsel",)], eng="gpsimd")
        p.dma(dv_sb[:], dvec1, writes=[("Zdv",)])
        p.dma(dt_sb[:], dtr_d, reads=[("dtr",)], writes=[K["dt"]])
        p.dma(A_sb[:, 0:1], alog1, writes=[("ZA",)])
        p.dma(tmpa[:, 0:1], dtb1, writes=[K["tmpa"]])

        p.op("vector", "tensor_scalar", dt_sb[:], dt_sb[:], tmpa[:, 0:1], None,
             ALU.add, reads=[K["dt"], K["tmpa"]], writes=[K["dt"]])
        p.op("scalar", "activation", tmpa[:], dt_sb[:], AF.Abs,
             reads=[K["dt"]], writes=[K["tmpa"]])
        p.op("scalar", "activation", tmpa[:], tmpa[:], AF.Exp, scale=-1.0,
             reads=[K["tmpa"]], writes=[K["tmpa"]])
        p.op("scalar", "activation", tmpa[:], tmpa[:], AF.Ln, bias=1.0,
             reads=[K["tmpa"]], writes=[K["tmpa"]])
        p.op("vector", "tensor_relu", dt_sb[:], dt_sb[:],
             reads=[K["dt"]], writes=[K["dt"]])
        p.op("vector", "tensor_add", dt_sb[:], dt_sb[:], tmpa[:],
             reads=[K["dt"], K["tmpa"]], writes=[K["dt"]])
        p.op("vector", "tensor_copy", dtbf_sb[:], dt_sb[:],
             reads=[K["dt"]], writes=[("Zdtbf",)])
        p.op("scalar", "activation", A_sb[:, 1:2], A_sb[:, 0:1], AF.Exp,
             reads=[("ZA",)], writes=[("ZA",)])
        p.op("vector", "tensor_scalar", A_sb[:, 1:2], A_sb[:, 1:2], -1.0, None,
             ALU.mult, reads=[("ZA",)], writes=[("ZA",)])
        p.op("vector", "tensor_scalar", tmpa[:], dt_sb[:], A_sb[:, 1:2], None,
             ALU.mult, reads=[K["dt"], ("ZA",)], writes=[K["tmpa"]])
        for c in range(NCH):
            sl = slice(c * CH, (c + 1) * CH)
            p.op("vector", "tensor_tensor_scan", acum[:, sl], onesr[:, 0:CH],
                 tmpa[:, sl], 0.0, ALU.mult, ALU.add,
                 reads=[("Zor",), K["tmpa"]], writes=[K["acum"]])
        for c in range(NCH):
            sl = slice(c * CH, (c + 1) * CH)
            p.op("scalar", "activation", ds_f[:, sl], acum[:, sl], AF.Exp,
                 scale=-1.0, bias=acum[:, (c + 1) * CH - 1:(c + 1) * CH],
                 reads=[K["acum"]], writes=[K["dsf"]])
        p.op("vector", "tensor_copy", lamb[:], acum[:, CH - 1::CH],
             reads=[K["acum"]], writes=[K["lam"]])
        p.op("scalar", "activation", lamb[:], lamb[:], AF.Exp,
             reads=[K["lam"]], writes=[K["lam"]])
        d1 = p.dma(lam_d[0], lamb[:], reads=[K["lam"]])
        p.dma(lamflat[:], lam_d.rearrange("o h c -> o (h c)"), deps=[d1],
              writes=[K["lflat"]])
        d2 = p.dma(acum_d[:, 0, :], acum[:], reads=[K["acum"]])
        p.op("tensor", "matmul", ps_big[:, 0:HH * NCH], ones1[:], lamflat[:],
             start=True, stop=True, reads=[("Zo1",), K["lflat"]],
             writes=[("Zpsbig",)])
        p.op("vector", "tensor_copy", lamrep[:], ps_big[:, 0:HH * NCH],
             reads=[("Zpsbig",)], writes=[K["lrep"]])
        for c in range(NCH):
            sl = slice(c * CH, (c + 1) * CH)
            p.op("tensor", "transpose", ps_big[0:128, 0:HH], acum[:, sl],
                 eyef[0:HH, 0:HH], reads=[K["acum"], ("Zeyef",)],
                 writes=[("Zpsbig",)])
            p.op("vector", "tensor_copy", acumT[:, c, :], ps_big[0:128, 0:HH],
                 reads=[("Zpsbig",)], writes=[K["at"]])
            p.op("tensor", "transpose", ps_big[0:128, 0:HH], ds_f[:, sl],
                 eyef[0:HH, 0:HH], reads=[K["dsf"], ("Zeyef",)],
                 writes=[("Zpsbig",)])
            p.op("vector", "tensor_copy", dsT[:, c, :], ps_big[0:128, 0:HH],
                 reads=[("Zpsbig",)], writes=[K["dst"]])

        p.dma(bmat_sb[:], hbca_d[2048:2560].rearrange("(g p) t -> p g t", p=128),
              reads=[("hbca",)], writes=[("Zbm",)])
        p.dma(cmat_sb[:], hbca_d[2560:3072].rearrange("(g p) t -> p g t", p=128),
              reads=[("hbca",)], writes=[("Zcm",)])
        for g in range(NGC):
            for c in range(NCH):
                sl = slice(c * CH, (c + 1) * CH)
                p.op("tensor", "transpose", ps_trb[:], bmat_sb[:, g, sl],
                     eyeb[:], reads=[("Zbm",), ("Zeyeb",)], writes=[("Zpstrb",)])
                p.op("vector", "tensor_copy", btr[:, g, c, :], ps_trb[:],
                     reads=[("Zpstrb",)], writes=[("Zbtr",)])
                p.op("tensor", "matmul", ps_cbt[:], bmat_sb[:, g, sl],
                     cmat_sb[:, g, sl], start=True, stop=True,
                     reads=[("Zbm",), ("Zcm",)], writes=[("Zpscbt",)])
                p.op("vector", "tensor_mul", cbt_all[:, g, c, :], ps_cbt[:],
                     masku_sb[:], reads=[("Zpscbt",), ("Zmu",)],
                     writes=[("Zcbt",)])

        hs_t = hbca_d[0:2048].rearrange("(c p) t -> c p t", p=128)
        hsd_t = hsd_d.rearrange("(c p) t -> c p t", p=128)
        for ct in range(CT):
            p.dma(hs_ct[:], hs_t[ct], reads=[("hbca",)], writes=[("Zhs",)])
            for q in range(4):
                p.op("tensor", "matmul", ps_big[:, q * 512:(q + 1) * 512],
                     sel_sb[:, ct, :], dtbf_sb[:, q * 512:(q + 1) * 512],
                     start=True, stop=True, reads=[("Zsel",), ("Zdtbf",)],
                     writes=([("Zpsbig",)] if q in (0, 3) else []))
            p.op("vector", "tensor_mul", hsdt_ct[:], hs_ct[:], ps_big[:],
                 reads=[("Zhs",), ("Zpsbig",)], writes=[("Zhsdt",)])
            p.op("vector", "tensor_scalar", hsd_ct[:], hs_ct[:],
                 dv_sb[:, ct:ct + 1], None, ALU.mult,
                 reads=[("Zhs",), ("Zdv",)], writes=[("Zhsdo",)])
            stores.append(p.dma(hsd_t[ct], hsd_ct[:], reads=[("Zhsdo",)],
                                writes=[("hsd",)]))
            for c in range(NCH):
                sl = slice(c * CH, (c + 1) * CH)
                p.op("tensor", "transpose", ps_trb[:], hsdt_ct[:, sl], eyeb[:],
                     reads=[("Zhsdt",), ("Zeyeb",)], writes=[("Zpstrb",)])
                p.op("vector", "tensor_copy",
                     hsdtT[:, c, ct * 128:(ct + 1) * 128], ps_trb[:],
                     reads=[("Zpstrb",)], writes=[("ZhsdtT",)])

        p.flush()
        cm2.close()
        y_all = sb("zyall", [128, NCH, CHN // 2], BF16)
        cab = sb("zcab", [128, 2, 1024], F32)
        eb = sb("zeb", [128, 2, 1024], BF16)
        lt = sb("zlt", [128, 4, 128], F32)
        mt_sb = sb("zmt", [128, 4, 128], BF16)
        csc = sb("zcsc", [128, 4, 128], BF16)
        bdt = sb("zbdt", [128, 4, 128], BF16)
        rst = sb("zrst", [128, 2, 64], F32)
        rbf = sb("zrbf", [128, 2, 64], BF16)
        caflat = sb("zcaflat", [1, 2, 1024], F32)
        ps_cab = ps("zpscab", [128, 1024], F32)
        ps_y = [ps("zpsy0", [128, 64], F32), ps("zpsy1", [128, 64], F32)]
        ps_s = [ps("zpss0", [128, 64], F32), ps("zpss1", [128, 64], F32)]
        ytok_t = ytok_d.rearrange("(c p) d -> c p d", p=128)
        for hp in range(HH // 2):
            for c in range(NCH):
                half, hoff = divmod(c * CH, 1024)
                if hoff == 0:
                    for hh in (0, 1):
                        p.dma(caflat[:, hh, :],
                              acum_d[2 * hp + hh][:, half * 1024:(half + 1) * 1024],
                              deps=[d2], writes=[("Zcaf", hh)])
                        for q in range(2):
                            p.op("tensor", "matmul",
                                 ps_cab[:, q * 512:(q + 1) * 512], ones1[:],
                                 caflat[:, hh, q * 512:(q + 1) * 512],
                                 start=True, stop=True,
                                 reads=[("Zo1",), ("Zcaf", hh)],
                                 writes=([("Zpscab",)] if q in (0, 1) else []))
                        p.op("vector", "tensor_copy", cab[:, hh, :],
                             ps_cab[:, 0:1024], reads=[("Zpscab",)],
                             writes=[("Zcab", hh)])
                        p.op("scalar", "activation", eb[:, hh, :], cab[:, hh, :],
                             AF.Exp, reads=[("Zcab", hh)], writes=[("Zeb", hh)])
                sl = slice(c * CH, (c + 1) * CH)
                slh = slice(hoff, hoff + CH)
                HHs = (0, 1)
                hof = lambda hh: 2 * hp + hh
                gof = lambda hh: hof(hh) // (HH // NGC)
                scof = lambda hh: hh * 2 + c % 2
                for hh in HHs:
                    sc = scof(hh)
                    p.op("vector", "tensor_scalar", lt[:, sc, :], cab[:, hh, slh],
                         acumT[:, c, hof(hh):hof(hh) + 1], 0.0, ALU.subtract,
                         ALU.min, reads=[("Zcab", hh), K["at"]],
                         writes=[("Zlt", sc)])
                for hh in HHs:
                    sc = scof(hh)
                    p.op("scalar", "activation", lt[:, sc, :], lt[:, sc, :],
                         AF.Exp, reads=[("Zlt", sc)], writes=[("Zlt", sc)])
                for hh in HHs:
                    sc = scof(hh)
                    p.op("vector", "tensor_mul", mt_sb[:, sc, :],
                         cbt_all[:, gof(hh), c, :], lt[:, sc, :],
                         reads=[("Zcbt",), ("Zlt", sc)], writes=[("Zmt", sc)])
                if c > 0:
                    for hh in HHs:
                        sc = scof(hh)
                        p.op("vector", "tensor_mul", csc[:, sc, :],
                             cmat_sb[:, gof(hh), sl], eb[:, hh, slh],
                             reads=[("Zcm",), ("Zeb", hh)], writes=[("Zcsc", sc)])
                    for hh in HHs:
                        p.op("gpsimd", "tensor_copy", rbf[:, hh, :], rst[:, hh, :],
                             reads=[("Zrst", hh)], writes=[("Zrbf", hh)])
                for hh in HHs:
                    sc = scof(hh)
                    h = hof(hh)
                    hsl = slice(h * HD, (h + 1) * HD)
                    p.op("tensor", "matmul", ps_y[hh][:], mt_sb[:, sc, :],
                         hsdtT[:, c, hsl], start=True, stop=(c == 0),
                         reads=[("Zmt", sc), ("ZhsdtT",)], writes=[("Zpsy", hh)])
                    if c > 0:
                        p.op("tensor", "matmul", ps_y[hh][:], csc[:, sc, :],
                             rbf[:, hh, :], start=False, stop=True,
                             reads=[("Zcsc", sc), ("Zrbf", hh)],
                             writes=[("Zpsy", hh)])
                for hh in HHs:
                    sc = scof(hh)
                    p.op("vector", "tensor_scalar", bdt[:, sc, :],
                         btr[:, gof(hh), c, :], dsT[:, c, hof(hh):hof(hh) + 1],
                         None, ALU.mult, reads=[("Zbtr",), K["dst"]],
                         writes=[("Zbdt", sc)])
                for hh in HHs:
                    sc = scof(hh)
                    h = hof(hh)
                    hsl = slice(h * HD, (h + 1) * HD)
                    p.op("tensor", "matmul", ps_s[hh][:], bdt[:, sc, :],
                         hsdtT[:, c, hsl], start=True, stop=True,
                         reads=[("Zbdt", sc), ("ZhsdtT",)], writes=[("Zpss", hh)])
                for hh in HHs:
                    h = hof(hh)
                    hs2 = slice((h % 16) * HD, (h % 16 + 1) * HD)
                    p.op("vector", "tensor_copy", y_all[:, c, hs2], ps_y[hh][:],
                         reads=[("Zpsy", hh)], writes=[("Zyall",)])
                for hh in HHs:
                    h = hof(hh)
                    if c == 0:
                        p.op("vector", "tensor_scalar", rst[:, hh, :],
                             ps_s[hh][:], 1.0, None, ALU.mult,
                             reads=[("Zpss", hh)], writes=[("Zrst", hh)])
                    else:
                        p.op("vector", "scalar_tensor_tensor", rst[:, hh, :],
                             rst[:, hh, :],
                             lamrep[:, h * NCH + c:h * NCH + c + 1], ps_s[hh][:],
                             ALU.mult, ALU.add,
                             reads=[("Zrst", hh), K["lrep"], ("Zpss", hh)],
                             writes=[("Zrst", hh)])
            if hp in (HH // 4 - 1, HH // 2 - 1):
                colh = 0 if hp == HH // 4 - 1 else 1
                for c in range(NCH):
                    stores.append(p.dma(
                        ytok_t[c][:, colh * 1024:(colh + 1) * 1024],
                        y_all[:, c, :], reads=[("Zyall",)],
                        writes=[("ytok",)], eng="scalar"))
        p.flush()


# version 21
# speedup vs baseline: 1.0698x; 1.0698x over previous
import contextlib
import numpy as np
import ml_dtypes
import concourse.bass as bass
import concourse.mybir as mybir
from concourse.bass_utils import run_bass_kernel_spmd

F32 = mybir.dt.float32
BF16 = mybir.dt.bfloat16
AF = mybir.ActivationFunctionType
ALU = mybir.AluOpType
AX = mybir.AxisListType

NCORES = 8
BATCH = 4
T = 2048
DM = 2048
TOK = BATCH * T
TPC = TOK // NCORES

BF = ml_dtypes.bfloat16
ENGINES = ("sync", "tensor", "vector", "scalar", "gpsimd")
NDMASEM = 8


def cdiv(a, b):
    return (a + b - 1) // b


_UID = [0]


def _u(nm):
    _UID[0] += 1
    return f"{nm}_u{_UID[0]}"


def pcol(v, nt=None):
    v = np.asarray(v, np.float32).ravel()
    n = nt if nt is not None else cdiv(v.size, 128)
    out = np.zeros((128, n), np.float32)
    for i in range(min(n, cdiv(v.size, 128))):
        blk = v[i * 128:(i + 1) * 128]
        out[:blk.size, i] = blk
    return out


class Op:
    __slots__ = ("eng", "meth", "args", "kwargs", "deps", "step", "val",
                 "kind", "idx")

    def __init__(self, eng, meth, args, kwargs, step):
        self.eng = eng
        self.meth = meth
        self.args = args
        self.kwargs = kwargs
        self.deps = []
        self.step = step
        self.val = 0
        self.kind = "dma" if step == 16 else "cmp"


class Prog:

    def __init__(self):
        self.ops = {e: [] for e in ENGINES}
        self.last_writer = {}
        self.readers = {}
        self.frontier = []
        self.barrier_pending = set()
        self.dma_count = {}

    def op(self, eng, meth, *args, reads=(), writes=(), deps=(), **kwargs):
        step = 16 if meth in ("dma_start", "dyn_dma") else 1
        o = Op(eng, meth, args, kwargs, step)
        if meth == "collective_compute":
            o.kind = "cc"
        if o.kind == "dma":
            slot = self.dma_count.get(eng, 0)
            self.dma_count[eng] = slot + 1
            o.kind = ("dma", slot % NDMASEM)
        if eng in self.barrier_pending:
            self.barrier_pending.discard(eng)
            deps = list(deps) + self.frontier
        prev = 0
        for q in reversed(self.ops[eng]):
            if q.kind == o.kind:
                prev = q.val
                break
        o.val = prev + step
        dd = list(deps)
        for k in reads:
            w = self.last_writer.get(k)
            if w is not None:
                dd.append(w)
            self.readers.setdefault(k, []).append(o)
        for k in writes:
            w = self.last_writer.get(k)
            if w is not None:
                dd.append(w)
            dd.extend(self.readers.get(k, ()))
            self.last_writer[k] = o
            self.readers[k] = []
        o.deps = dd
        self.ops[eng].append(o)
        return o

    def dma(self, out, in_, reads=(), writes=(), deps=(), eng="sync", **kw):
        return self.op(eng, "dma_start", reads=reads, writes=writes,
                       deps=deps, out=out, in_=in_, **kw)

    def dyn_dma(self, out, handle, ap_list, off_ap, reads=(), writes=(),
                deps=(), eng="sync", out_is_dyn=False):
        return self.op(eng, "dyn_dma", reads=reads, writes=writes, deps=deps,
                       out=out, handle=handle, ap_list=ap_list, off_ap=off_ap,
                       out_is_dyn=out_is_dyn)

    def cc(self, kind, alu, rg, cins, couts, reads=(), writes=(), deps=()):
        return self.op("gpsimd", "collective_compute", kind, alu, rg,
                       cins, couts, reads=reads, writes=writes, deps=deps)

    def barrier(self):
        self.frontier = [q[-1] for q in self.ops.values() if q]
        self.barrier_pending = set(ENGINES)

    def attach(self, nc, stack):
        self.nc = nc
        self._sems = {}
        for e in ENGINES:
            for kd in ("cmp", "cc"):
                self._sems[(e, kd)] = stack.enter_context(
                    nc.semaphore(f"s_{e}_{kd}"))
            for i in range(NDMASEM):
                self._sems[(e, ("dma", i))] = stack.enter_context(
                    nc.semaphore(f"s_{e}_dma{i}"))
        self._block = stack.enter_context(nc.Block())
        self._emitted = {e: 0 for e in ENGINES}
        self._waited = {e: {} for e in ENGINES}
        self._prev_cc = {e: 0 for e in ENGINES}

    def _emit_ops(self, engine_obj, e, ops):
        sems = self._sems
        waited = self._waited[e]
        for o in ops:
            if isinstance(o.kind, tuple) and o.val > 16:
                k = (e, o.kind)
                if waited.get(k, 0) < o.val - 16:
                    engine_obj.wait_ge(sems[k], o.val - 16)
                    waited[k] = o.val - 16
            if o.kind == "cc" and self._prev_cc[e]:
                k = (e, "cc")
                if waited.get(k, 0) < self._prev_cc[e]:
                    engine_obj.wait_ge(sems[k], self._prev_cc[e])
                    waited[k] = self._prev_cc[e]
            for d in o.deps:
                if d is o:
                    continue
                if d.eng == e == "tensor" and d.kind == "cmp" == o.kind:
                    continue
                dk = (d.eng, d.kind)
                if waited.get(dk, 0) >= d.val:
                    continue
                engine_obj.wait_ge(sems[dk], d.val)
                waited[dk] = d.val
            if o.meth == "dyn_dma":
                kw = o.kwargs
                with engine_obj.register(_u("dynr")) as reg:
                    engine_obj.reg_load(reg, kw["off_ap"])
                    dyn_ap = bass.AP(kw["handle"], reg, kw["ap_list"])
                    if kw.get("out_is_dyn"):
                        inst = engine_obj.dma_start(out=dyn_ap, in_=kw["out"])
                    else:
                        inst = engine_obj.dma_start(out=kw["out"], in_=dyn_ap)
            else:
                inst = getattr(engine_obj, o.meth)(*o.args, **o.kwargs)
            inst.then_inc(sems[(e, o.kind)], o.step)
            if o.kind == "cc":
                self._prev_cc[e] = o.val

    def flush(self):
        for e in ENGINES:
            new = self.ops[e][self._emitted[e]:]
            if not new:
                continue
            self._emitted[e] = len(self.ops[e])
            getattr(self._block, e)(
                lambda engine_obj, e=e, new=new: self._emit_ops(engine_obj, e, new))

    def finish(self, final_ops):
        self.op("vector", "engine_nop", deps=list(final_ops))
        self.flush()

    def build(self, nc, final_ops):
        self.op("vector", "engine_nop", deps=list(final_ops))
        with contextlib.ExitStack() as stack:
            sems = {}
            for e in ENGINES:
                kinds = set(str(o.kind) for o in self.ops[e])
                for kd in kinds:
                    sems[(e, kd)] = stack.enter_context(
                        nc.semaphore(f"s_{e}_{kd}"))
            block = stack.enter_context(nc.Block())

            def emit(engine_obj, e):
                waited = {}
                for o in self.ops[e]:
                    if isinstance(o.kind, tuple) and o.val > 16:
                        k = (e, str(o.kind))
                        if waited.get(k, 0) < o.val - 16:
                            engine_obj.wait_ge(sems[k], o.val - 16)
                            waited[k] = o.val - 16
                    for d in o.deps:
                        if d is o:
                            continue
                        if d.eng == e == "tensor" and d.kind == "cmp" == o.kind:
                            continue
                        dk = (d.eng, str(d.kind))
                        if waited.get(dk, 0) >= d.val:
                            continue
                        engine_obj.wait_ge(sems[dk], d.val)
                        waited[dk] = d.val
                    inst = getattr(engine_obj, o.meth)(*o.args, **o.kwargs)
                    inst.then_inc(sems[(e, str(o.kind))], o.step)

            for e in ENGINES:
                if self.ops[e]:
                    getattr(block, e)(lambda engine_obj, e=e: emit(engine_obj, e))


def emit_consts(p, cm, nc, values):
    n = max(1, len(values))
    tile = cm.enter_context(nc.sbuf_tensor(_u("consts"), [128, n], F32))
    key = (_u("constsk"),)
    for j, v in enumerate(values):
        p.op("vector", "memset", tile[:, j:j + 1], float(v), writes=[key])
    return tile, key


def emit_ln(p, src, dst, st, sq, k_src, k_dst, k_st, k_sq, D, eps_col, k_eps):
    p.op("vector", "reduce_sum", st[:, 0:1], src, axis=AX.X,
         reads=[k_src], writes=[k_st])
    p.op("scalar", "activation", sq, src, AF.Square,
         reads=[k_src], writes=[k_sq])
    p.op("vector", "reduce_sum", st[:, 1:2], sq, axis=AX.X,
         reads=[k_sq], writes=[k_st])
    p.op("vector", "tensor_scalar", st[:, 0:1], st[:, 0:1], 1.0 / D, None,
         ALU.mult, reads=[k_st], writes=[k_st])
    p.op("vector", "scalar_tensor_tensor", st[:, 2:3], st[:, 0:1], -1.0,
         st[:, 0:1], ALU.mult, ALU.mult, reads=[k_st], writes=[k_st])
    p.op("vector", "scalar_tensor_tensor", st[:, 2:3], st[:, 1:2], 1.0 / D,
         st[:, 2:3], ALU.mult, ALU.add, reads=[k_st], writes=[k_st])
    p.op("scalar", "activation", st[:, 3:4], st[:, 2:3], AF.Sqrt, bias=eps_col,
         reads=[k_st, k_eps], writes=[k_st])
    p.op("vector", "reciprocal", st[:, 4:5], st[:, 3:4],
         reads=[k_st], writes=[k_st])
    return p.op("vector", "tensor_scalar", dst, src, st[:, 0:1], st[:, 4:5],
                ALU.subtract, ALU.mult, reads=[k_src, k_st], writes=[k_dst])


def build_ln_stage(rows, D, residual, emit_next, final_f32=False):
    nc = bass.Bass("TRN2", target_bir_lowering=False)
    f_in = nc.dram_tensor("f", [rows, D], F32, kind="ExternalInput").ap()
    res_in = nc.dram_tensor("res", [rows, D], F32, kind="ExternalInput").ap() if residual else None
    xo_out = nc.dram_tensor("xo", [rows, D], F32, kind="ExternalOutput").ap() if residual else None
    xn_out = nc.dram_tensor("xn", [rows, D], F32 if final_f32 else BF16,
                            kind="ExternalOutput").ap() if emit_next else None
    ntile = rows // 128
    p = Prog()
    stores = []
    with contextlib.ExitStack() as cm:
        sb = lambda *a: cm.enter_context(nc.sbuf_tensor(*a))
        ft = sb("ft", [128, 2, D], F32)
        rt = sb("rt", [128, 2, D], F32) if residual else None
        sq = sb("sq", [128, 2, D], F32)
        ot = sb("ot", [128, 2, D], F32)
        on = sb("on", [128, 2, D], F32 if final_f32 else BF16)
        st = sb("st", [128, 2, 8], F32)
        cst, kcst = emit_consts(p, cm, nc, [1e-5])

        f_t = f_in.rearrange("(n p) d -> n p d", p=128)
        res_t = res_in.rearrange("(n p) d -> n p d", p=128) if residual else None
        xo_t = xo_out.rearrange("(n p) d -> n p d", p=128) if residual else None
        xn_t = xn_out.rearrange("(n p) d -> n p d", p=128) if emit_next else None

        for i in range(ntile):
            s = i % 2
            kf, kr, ksq, kot, kon, kst = (("ft", s), ("rt", s), ("sq", s),
                                          ("ot", s), ("on", s), ("st", s))
            p.dma(ft[:, s, :], f_t[i], writes=[kf])
            if residual:
                p.dma(rt[:, s, :], res_t[i], writes=[kr])
                emit_ln(p, ft[:, s, :], ot[:, s, :], st[:, s, :], sq[:, s, :],
                        kf, kot, kst, ksq, D, cst[:, 0:1], kcst)
                p.op("vector", "tensor_add", ot[:, s, :], ot[:, s, :],
                     rt[:, s, :], reads=[kot, kr], writes=[kot])
                stores.append(p.dma(xo_t[i], ot[:, s, :], reads=[kot]))
                if emit_next:
                    emit_ln(p, ot[:, s, :], on[:, s, :], st[:, s, :],
                            sq[:, s, :], kot, kon, kst, ksq, D, cst[:, 0:1], kcst)
                    stores.append(p.dma(xn_t[i], on[:, s, :], reads=[kon]))
            else:
                emit_ln(p, ft[:, s, :], on[:, s, :], st[:, s, :], sq[:, s, :],
                        kf, kon, kst, ksq, D, cst[:, 0:1], kcst)
                stores.append(p.dma(xn_t[i], on[:, s, :], reads=[kon]))
        p.build(nc, stores)
    return nc


def run(nc, in_maps, outs):
    res = run_bass_kernel_spmd(nc, in_maps, core_ids=list(range(NCORES)))
    return [[np.asarray(res.results[c][o]) for c in range(NCORES)] for o in outs]


def kernel(**inputs):
    raise NotImplementedError("phase 1 under construction")


def build_mm_stage(Kdim, M, N, act=None, row_scale=False, row_bias=False,
                   softplus=False, col_mult=False, rhs_prescale=False,
                   nchunk=512):
    assert Kdim % 128 == 0 and N % nchunk == 0
    KT = Kdim // 128
    MT = cdiv(M, 128)
    NC = N // nchunk
    nc = bass.Bass("TRN2", target_bir_lowering=False)
    lhsT_in = nc.dram_tensor("lhsT", [Kdim, M], F32, kind="ExternalInput").ap()
    rhs_in = nc.dram_tensor("rhs", [Kdim, N], F32, kind="ExternalInput").ap()
    scale_in = nc.dram_tensor("scale", [128, MT], F32, kind="ExternalInput").ap() if row_scale else None
    bias_in = nc.dram_tensor("bias", [128, MT], F32, kind="ExternalInput").ap() if (row_bias or softplus) else None
    cmul_in = nc.dram_tensor("cmul", [1, N], F32, kind="ExternalInput").ap() if col_mult else None
    pres_in = nc.dram_tensor("pres", [128, KT], F32, kind="ExternalInput").ap() if rhs_prescale else None
    out_out = nc.dram_tensor("out", [M, N], F32, kind="ExternalOutput").ap()

    p = Prog()
    stores = []
    with contextlib.ExitStack() as cm:
        sb = lambda *a: cm.enter_context(nc.sbuf_tensor(*a))
        ps = lambda *a: cm.enter_context(nc.psum_tensor(*a))
        rhs_sb = sb("rhs_sb", [128, KT, N], BF16)
        lhs_sb = sb("lhs_sb", [128, 2, KT, 128], BF16)
        out_sb = sb("out_sb", [128, 2, N], F32)
        tmp1 = sb("tmp1", [128, nchunk], F32)
        tmp2 = sb("tmp2", [128, nchunk], F32)
        vec_sb = sb("vec_sb", [128, 2 * MT + 2], F32)
        ones_sb = sb("ones_sb", [1, 128], F32)
        cmb = sb("cmb", [128, N], F32) if col_mult else None
        psA = ps("psA", [128, nchunk], F32)
        psB = ps("psB", [128, nchunk], F32)
        psC = ps("psC", [128, 512], F32) if col_mult else None
        psum = [psA, psB]

        rhs_t = rhs_in.rearrange("(k p) n -> p k n", p=128)
        for k0 in range(0, KT, 8):
            k1 = min(KT, k0 + 8)
            p.dma(rhs_sb[:, k0:k1, :], rhs_t[:, k0:k1, :], writes=[("rhs", k) for k in range(k0, k1)], eng="gpsimd")
        if rhs_prescale:
            pres_sb = sb("pres_sb", [128, KT], F32)
            p.dma(pres_sb[:], pres_in, writes=[("pres",)])
            for k in range(KT):
                p.op("vector", "tensor_scalar", rhs_sb[:, k, :], rhs_sb[:, k, :],
                     pres_sb[:, k:k + 1], None, ALU.mult,
                     reads=[("rhs", k), ("pres",)], writes=[("rhs", k)])
        if row_scale:
            sc_sb = sb("sc_sb", [128, MT], F32)
            p.dma(sc_sb[:], scale_in, writes=[("sc",)])
        if row_bias or softplus:
            bi_sb = sb("bi_sb", [128, MT], F32)
            p.dma(bi_sb[:], bias_in, writes=[("bi",)])
        if col_mult:
            p.op("vector", "memset", ones_sb[:], 1.0, writes=[("ones",)])
            cm_sb1 = sb("cm_sb1", [1, N], F32)
            p.dma(cm_sb1[:], cmul_in, writes=[("cm1",)])
            for j in range(N // 512):
                p.op("tensor", "matmul", psC[:], ones_sb[:], cm_sb1[:, j * 512:(j + 1) * 512],
                     start=True, stop=True, reads=[("ones",), ("cm1",)], writes=[("psC",)])
                p.op("vector", "tensor_copy", cmb[:, j * 512:(j + 1) * 512], psC[:],
                     reads=[("psC",)], writes=[("cmb",)])

        lhs_t = lhsT_in
        out_ap = out_out

        for mt in range(MT):
            mb = mt % 2
            pm = min(128, M - mt * 128)
            slab = lhs_t[:, mt * 128:mt * 128 + pm].rearrange("(k p) m -> p k m", p=128)
            for k0 in range(0, KT, 8):
                k1 = min(KT, k0 + 8)
                p.dma(lhs_sb[:, mb, k0:k1, 0:pm], slab[:, k0:k1, :],
                      writes=[("lhs", mb, k) for k in range(k0, k1)], eng="gpsimd")
            for nch in range(NC):
                pb = (mt * NC + nch) % 2
                pt = psum[pb]
                for k in range(KT):
                    p.op("tensor", "matmul", pt[0:pm, 0:nchunk], lhs_sb[:, mb, k, 0:pm],
                         rhs_sb[:, k, nch * nchunk:(nch + 1) * nchunk],
                         start=(k == 0), stop=(k == KT - 1),
                         reads=[("lhs", mb, k), ("rhs", k)],
                         writes=([("ps", pb)] if k == 0 else []),
                         deps=[])
                mm_last = p.ops["tensor"][-1]
                src = pt[0:pm, 0:nchunk]
                kread = [("ps", pb)]
                dst = out_sb[0:pm, mb, nch * nchunk:(nch + 1) * nchunk]
                if softplus:
                    p.op("vector", "tensor_scalar", tmp1[0:pm, :], src, bi_sb[0:pm, mt:mt + 1],
                         None, ALU.add, reads=kread + [("bi",)], writes=[("tmp1",)], deps=[mm_last])
                    p.op("scalar", "activation", tmp2[0:pm, :], tmp1[0:pm, :], AF.Abs,
                         reads=[("tmp1",)], writes=[("tmp2",)])
                    p.op("scalar", "activation", tmp2[0:pm, :], tmp2[0:pm, :], AF.Exp,
                         scale=-1.0, reads=[("tmp2",)], writes=[("tmp2",)])
                    p.op("scalar", "activation", tmp2[0:pm, :], tmp2[0:pm, :], AF.Ln,
                         bias=1.0, reads=[("tmp2",)], writes=[("tmp2",)])
                    p.op("vector", "tensor_relu", tmp1[0:pm, :], tmp1[0:pm, :],
                         reads=[("tmp1",)], writes=[("tmp1",)])
                    p.op("vector", "tensor_add", dst, tmp1[0:pm, :], tmp2[0:pm, :],
                         reads=[("tmp1",), ("tmp2",)], writes=[("out", mb)])
                else:
                    cur = src
                    curk = kread
                    if row_scale and act == "silu":
                        p.op("scalar", "activation", dst, cur, AF.Silu,
                             scale=sc_sb[0:pm, mt:mt + 1],
                             reads=curk + [("sc",)], writes=[("out", mb)], deps=[mm_last])
                    elif row_scale and act is None:
                        p.op("vector", "tensor_scalar", dst, cur, sc_sb[0:pm, mt:mt + 1],
                             None, ALU.mult, reads=curk + [("sc",)], writes=[("out", mb)], deps=[mm_last])
                    elif act == "silu":
                        p.op("scalar", "activation", dst, cur, AF.Silu,
                             reads=curk, writes=[("out", mb)], deps=[mm_last])
                    else:
                        p.op("vector", "tensor_copy", dst, cur,
                             reads=curk, writes=[("out", mb)], deps=[mm_last])
                    if col_mult:
                        p.op("vector", "tensor_mul", dst, dst, cmb[0:pm, nch * nchunk:(nch + 1) * nchunk],
                             reads=[("out", mb), ("cmb",)], writes=[("out", mb)])
            so = p.dma(out_ap[mt * 128:mt * 128 + pm, :], out_sb[0:pm, mb, :],
                       reads=[("out", mb)])
            stores.append(so)
        p.build(nc, stores)
    return nc


def build_conv_stage(CH, B, T_, pad_l):
    CT = CH // 128
    nc = bass.Bass("TRN2", target_bir_lowering=False)
    xin = nc.dram_tensor("xin", [CH, B, T_], F32, kind="ExternalInput").ap()
    cw = nc.dram_tensor("cw", [CH, 4], F32, kind="ExternalInput").ap()
    cb = nc.dram_tensor("cb", [128, CH // 128], F32, kind="ExternalInput").ap()
    xact = nc.dram_tensor("xact", [CH, B, T_], F32, kind="ExternalOutput").ap()
    PAD = 4
    p = Prog()
    stores = []
    with contextlib.ExitStack() as cm:
        sb = lambda *a: cm.enter_context(nc.sbuf_tensor(*a))
        buf = sb("buf", [128, 2, T_ + PAD], F32)
        acc = sb("acc", [128, 2, T_], F32)
        ot = sb("ot", [128, 2, T_], F32)
        cw_sb = sb("cw_sb", [128, CT, 4], F32)
        cb_sb = sb("cb_sb", [128, CT], F32)
        p.dma(cw_sb[:], cw.rearrange("(c p) k -> p c k", p=128), writes=[("cw",)])
        p.dma(cb_sb[:], cb, writes=[("cb",)])
        xin_t = xin.rearrange("(c p) b t -> c b p t", p=128)
        xact_t = xact.rearrange("(c p) b t -> c b p t", p=128)
        for u, (c, b) in enumerate([(c, b) for c in range(CT) for b in range(B)]):
            s = u % 2
            kb, ka, ko = ("buf", s), ("acc", s), ("ot", s)
            if u < 2:
                p.op("vector", "memset", buf[:, s, :], 0.0, writes=[kb])
            p.dma(buf[:, s, pad_l:pad_l + T_], xin_t[c, b], writes=[(("bufc", s))],
                  deps=[p.last_writer[kb]] if u < 2 else [])
            kbc = ("bufc", s)
            p.op("vector", "tensor_scalar", acc[:, s, :], buf[:, s, 0:T_],
                 cw_sb[:, c, 0:1], None, ALU.mult, reads=[kbc, ("cw",)], writes=[ka])
            for k in range(1, 4):
                p.op("vector", "scalar_tensor_tensor", acc[:, s, :],
                     buf[:, s, k:k + T_], cw_sb[:, c, k:k + 1], acc[:, s, :],
                     ALU.mult, ALU.add, reads=[kbc, ("cw",), ka], writes=[ka])
            p.op("scalar", "activation", ot[:, s, :], acc[:, s, :], AF.Silu,
                 bias=cb_sb[:, c:c + 1], reads=[ka, ("cb",)], writes=[ko])
            stores.append(p.dma(xact_t[c, b], ot[:, s, :], reads=[ko]))
        p.build(nc, stores)
    return nc


def build_scan_stage(DS, B, T_, NST=16, TC=512, NG=4):
    DT4 = DS // 128
    NTC = T_ // TC
    GS = NST // NG
    nc = bass.Bass("TRN2", target_bir_lowering=False)
    dt_in = nc.dram_tensor("dt", [DS, B, T_], F32, kind="ExternalInput").ap()
    xa_in = nc.dram_tensor("xa", [DS, B, T_], F32, kind="ExternalInput").ap()
    z_in = nc.dram_tensor("z", [DS, B, T_], F32, kind="ExternalInput").ap()
    al_in = nc.dram_tensor("alog", [DS, NST], F32, kind="ExternalInput").ap()
    dv_in = nc.dram_tensor("dv", [128, DS // 128], F32, kind="ExternalInput").ap()
    bm_in = nc.dram_tensor("bm", [B, 1, NST, T_], F32, kind="ExternalInput").ap()
    cmm_in = nc.dram_tensor("cm", [B, 1, NST, T_], F32, kind="ExternalInput").ap()
    id_in = nc.dram_tensor("ident", [128, 128], F32, kind="ExternalInput").ap()
    yg_out = nc.dram_tensor("yg", [DS, B, T_], F32, kind="ExternalOutput").ap()

    p = Prog()
    stores = []
    with contextlib.ExitStack() as cm:
        sb = lambda nm, shp, dt_: cm.enter_context(nc.sbuf_tensor(_u(nm), shp, dt_))
        ps = lambda nm, shp, dt_: cm.enter_context(nc.psum_tensor(_u(nm), shp, dt_))
        dt_sb = sb("dt_sb", [128, 2, DT4, TC], F32)
        xa_sb = sb("xa_sb", [128, 2, DT4, TC], F32)
        z_sb = sb("z_sb", [128, 2, TC], F32)
        dtx_sb = sb("dtx_sb", [128, DT4, TC], F32)
        dA_sb = sb("dA_sb", [128, 2, TC], F32)
        dBx_sb = sb("dBx_sb", [128, 2, TC], F32)
        h_sb = sb("h_sb", [128, 2, TC], F32)
        p_sb = sb("p_sb", [128, 2, TC], BF16)
        bb_sb = sb("bb_sb", [128, 2, TC], BF16)
        cb_sb = sb("cb_sb", [128, 2, TC], BF16)
        yd_sb = sb("yd_sb", [128, TC], F32)
        sz_sb = sb("sz_sb", [128, TC], F32)
        yo_sb = sb("yo_sb", [128, 2, TC], F32)
        hcar = sb("hcar", [128, DT4, NST], F32)
        A_sb = sb("A_sb", [128, DT4, NST], F32)
        dv_sb = sb("dv_sb", [128, DT4], F32)
        bro_sb = sb("bro_sb", [1, 2, GS, T_], BF16)
        cro_sb = sb("cro_sb", [1, 2, GS, T_], BF16)
        ones_bf = sb("ones_bf", [1, 128], BF16)
        id_sb = sb("id_sb", [128, 128], BF16)
        psy = [ps(f"psy{i}", [128, TC], F32) for i in range(DT4)]
        psb = ps("psb", [128, 2, TC], F32)

        p.op("vector", "memset", ones_bf[:], 1.0, writes=[("ones",)])
        p.dma(id_sb[:], id_in, writes=[("id",)], eng="gpsimd")
        p.dma(A_sb[:], al_in.rearrange("(c p) n -> p c n", p=128), writes=[("A",)])
        p.dma(dv_sb[:], dv_in, writes=[("dv",)])
        for c in range(DT4):
            p.op("scalar", "activation", A_sb[:, c, :], A_sb[:, c, :], AF.Exp,
                 reads=[("A",)], writes=[("A",)])
            p.op("vector", "tensor_scalar", A_sb[:, c, :], A_sb[:, c, :], -1.0,
                 None, ALU.mult, reads=[("A",)], writes=[("A",)])

        dt_t = dt_in.rearrange("(c p) b t -> b c p t", p=128)
        xa_t = xa_in.rearrange("(c p) b t -> b c p t", p=128)
        z_t = z_in.rearrange("(c p) b t -> b c p t", p=128)
        yg_t = yg_out.rearrange("(c p) b t -> b c p t", p=128)

        for b in range(B):
            for tc in range(NTC):
                s = tc % 2
                tsl = slice(tc * TC, (tc + 1) * TC)
                for c in range(DT4):
                    p.dma(dt_sb[:, s, c, :], dt_t[b, c, :, tsl], writes=[("dt", s, c)])
                    p.dma(xa_sb[:, s, c, :], xa_t[b, c, :, tsl], writes=[("xa", s, c)])
                    p.op("vector", "tensor_mul", dtx_sb[:, c, :], dt_sb[:, s, c, :],
                         xa_sb[:, s, c, :], reads=[("dt", s, c), ("xa", s, c)],
                         writes=[("dtx", c)])
                for g in range(NG):
                    sg = g % 2
                    p.dma(bro_sb[:, sg, :, :], bm_in[b, :, g * GS:(g + 1) * GS, :],
                          writes=[("bro", sg)], eng="gpsimd")
                    p.dma(cro_sb[:, sg, :, :], cmm_in[b, :, g * GS:(g + 1) * GS, :],
                          writes=[("cro", sg)], eng="gpsimd")
                    for nn in range(GS):
                        n = g * GS + nn
                        sn = n % 2
                        p.op("tensor", "matmul", psb[:, sn, :], ones_bf[:],
                             bro_sb[:, sg, nn, tsl], start=True, stop=True,
                             reads=[("ones",), ("bro", sg)], writes=[("psb", sn)])
                        p.op("scalar", "copy", bb_sb[:, sn, :], psb[:, sn, :],
                             reads=[("psb", sn)], writes=[("bb", sn)])
                        p.op("tensor", "matmul", psb[:, sn, :], ones_bf[:],
                             cro_sb[:, sg, nn, tsl], start=True, stop=True,
                             reads=[("ones",), ("cro", sg)], writes=[("psb", sn)])
                        p.op("scalar", "copy", cb_sb[:, sn, :], psb[:, sn, :],
                             reads=[("psb", sn)], writes=[("cb", sn)])
                        for c in range(DT4):
                            p.op("scalar", "activation", dA_sb[:, sn, :], dt_sb[:, s, c, :],
                                 AF.Exp, scale=A_sb[:, c, n:n + 1],
                                 reads=[("dt", s, c), ("A",)], writes=[("dA", sn)])
                            p.op("gpsimd", "tensor_mul", dBx_sb[:, sn, :], dtx_sb[:, c, :],
                                 bb_sb[:, sn, :], reads=[("dtx", c), ("bb", sn)],
                                 writes=[("dBx", sn)])
                            scan_init = 0.0 if tc == 0 else hcar[:, c, n:n + 1]
                            p.op("vector", "tensor_tensor_scan", h_sb[:, sn, :],
                                 dA_sb[:, sn, :], dBx_sb[:, sn, :], scan_init,
                                 ALU.mult, ALU.add,
                                 reads=[("dA", sn), ("dBx", sn)] + ([("hc", c, n)] if tc else []),
                                 writes=[("h", sn)])
                            if tc < NTC - 1:
                                p.op("vector", "tensor_copy", hcar[:, c, n:n + 1],
                                     h_sb[:, sn, TC - 1:TC], reads=[("h", sn)],
                                     writes=[("hc", c, n)])
                            p.op("vector", "tensor_mul", p_sb[:, sn, :], h_sb[:, sn, :],
                                 cb_sb[:, sn, :], reads=[("h", sn), ("cb", sn)],
                                 writes=[("p", sn)])
                            p.op("tensor", "matmul", psy[c][:], id_sb[:], p_sb[:, sn, :],
                                 start=(n == 0), stop=(n == NST - 1),
                                 reads=[("id",), ("p", sn)],
                                 writes=([("psy", c)] if n in (0, NST - 1) else []))
                for c in range(DT4):
                    p.dma(z_sb[:, s, :], z_t[b, c, :, tsl], writes=[("z", s)])
                    p.op("vector", "scalar_tensor_tensor", yd_sb[:], xa_sb[:, s, c, :],
                         dv_sb[:, c:c + 1], psy[c][:], ALU.mult, ALU.add,
                         reads=[("xa", s, c), ("dv",), ("psy", c)], writes=[("yd",)])
                    p.op("scalar", "activation", sz_sb[:], z_sb[:, s, :], AF.Silu,
                         reads=[("z", s)], writes=[("sz",)])
                    p.op("vector", "tensor_mul", yo_sb[:, s, :], yd_sb[:], sz_sb[:],
                         reads=[("yd",), ("sz",)], writes=[("yo", s)])
                    stores.append(p.dma(yg_t[b, c, :, tsl], yo_sb[:, s, :],
                                        reads=[("yo", s)]))
        p.build(nc, stores)
    return nc


def build_ffn_stage(Kdim, M, N, nchunk=512):
    KT, MT, NC = Kdim // 128, M // 128, N // nchunk
    nc = bass.Bass("TRN2", target_bir_lowering=False)
    lg_in = nc.dram_tensor("lhsT_g", [Kdim, M], F32, kind="ExternalInput").ap()
    lu_in = nc.dram_tensor("lhsT_u", [Kdim, M], F32, kind="ExternalInput").ap()
    rhs_in = nc.dram_tensor("rhs", [Kdim, N], F32, kind="ExternalInput").ap()
    gs_in = nc.dram_tensor("gs", [128, MT], F32, kind="ExternalInput").ap()
    us_in = nc.dram_tensor("us", [128, MT], F32, kind="ExternalInput").ap()
    out_out = nc.dram_tensor("out", [M, N], F32, kind="ExternalOutput").ap()
    p = Prog()
    stores = []
    with contextlib.ExitStack() as cm:
        sb = lambda *a: cm.enter_context(nc.sbuf_tensor(*a))
        ps = lambda *a: cm.enter_context(nc.psum_tensor(*a))
        rhs_sb = sb("rhs_sb", [128, KT, N], BF16)
        lg_sb = sb("lg_sb", [128, 2, KT, 128], BF16)
        lu_sb = sb("lu_sb", [128, 2, KT, 128], BF16)
        out_sb = sb("out_sb", [128, 2, N], F32)
        ga_sb = sb("ga_sb", [128, nchunk], F32)
        gs_sb = sb("gs_sb", [128, MT], F32)
        us_sb = sb("us_sb", [128, MT], F32)
        psg = ps("psg", [128, nchunk], F32)
        psu = ps("psu", [128, nchunk], F32)
        rhs_t = rhs_in.rearrange("(k p) n -> p k n", p=128)
        for k0 in range(0, KT, 8):
            k1 = min(KT, k0 + 8)
            p.dma(rhs_sb[:, k0:k1, :], rhs_t[:, k0:k1, :],
                  writes=[("rhs", k) for k in range(k0, k1)], eng="gpsimd")
        p.dma(gs_sb[:], gs_in, writes=[("gs",)])
        p.dma(us_sb[:], us_in, writes=[("us",)])
        for mt in range(MT):
            mb = mt % 2
            for W, buf, nm in ((lg_in, lg_sb, "lg"), (lu_in, lu_sb, "lu")):
                slab = W[:, mt * 128:(mt + 1) * 128].rearrange("(k p) m -> p k m", p=128)
                for k0 in range(0, KT, 8):
                    k1 = min(KT, k0 + 8)
                    p.dma(buf[:, mb, k0:k1, :], slab[:, k0:k1, :],
                          writes=[(nm, mb, k) for k in range(k0, k1)], eng="gpsimd")
            for nch in range(NC):
                nsl = slice(nch * nchunk, (nch + 1) * nchunk)
                for k in range(KT):
                    p.op("tensor", "matmul", psg[:], lg_sb[:, mb, k, :],
                         rhs_sb[:, k, nsl], start=(k == 0), stop=(k == KT - 1),
                         reads=[("lg", mb, k), ("rhs", k)],
                         writes=([("psg",)] if k in (0, KT - 1) else []))
                for k in range(KT):
                    p.op("tensor", "matmul", psu[:], lu_sb[:, mb, k, :],
                         rhs_sb[:, k, nsl], start=(k == 0), stop=(k == KT - 1),
                         reads=[("lu", mb, k), ("rhs", k)],
                         writes=([("psu",)] if k in (0, KT - 1) else []))
                p.op("scalar", "activation", ga_sb[:], psg[:], AF.Silu,
                     scale=gs_sb[:, mt:mt + 1], reads=[("psg",), ("gs",)],
                     writes=[("ga",)])
                p.op("vector", "tensor_scalar", out_sb[:, mb, nsl], psu[:],
                     us_sb[:, mt:mt + 1], None, ALU.mult,
                     reads=[("psu",), ("us",)], writes=[("out", mb)])
                p.op("vector", "tensor_mul", out_sb[:, mb, nsl], out_sb[:, mb, nsl],
                     ga_sb[:], reads=[("out", mb), ("ga",)], writes=[("out", mb)])
            stores.append(p.dma(out_out[mt * 128:(mt + 1) * 128, :],
                                out_sb[:, mb, :], reads=[("out", mb)]))
        p.build(nc, stores)
    return nc


def build_gating_stage(rows, CHN, eps=1e-6):
    NT = rows // 128
    nc = bass.Bass("TRN2", target_bir_lowering=False)
    y_in = nc.dram_tensor("y", [rows, CHN], F32, kind="ExternalInput").ap()
    h_in = nc.dram_tensor("hsd", [rows, CHN], F32, kind="ExternalInput").ap()
    g_in = nc.dram_tensor("gate", [rows, CHN], F32, kind="ExternalInput").ap()
    yf_out = nc.dram_tensor("yf", [rows, CHN], F32, kind="ExternalOutput").ap()
    rs_out = nc.dram_tensor("rstd", [128, NT], F32, kind="ExternalOutput").ap()
    p = Prog()
    stores = []
    with contextlib.ExitStack() as cm:
        sb = lambda *a: cm.enter_context(nc.sbuf_tensor(*a))
        yt = sb("yt", [128, 2, CHN], F32)
        ht = sb("ht", [128, 2, CHN], F32)
        gt = sb("gt", [128, 2, CHN], F32)
        sq = sb("sq", [128, CHN], F32)
        rs = sb("rs", [128, NT], F32)
        st = sb("st", [128, 4], F32)
        cst, kcst = emit_consts(p, cm, nc, [eps])
        y_t = y_in.rearrange("(n p) d -> n p d", p=128)
        h_t = h_in.rearrange("(n p) d -> n p d", p=128)
        g_t = g_in.rearrange("(n p) d -> n p d", p=128)
        yf_t = yf_out.rearrange("(n p) d -> n p d", p=128)
        for i in range(NT):
            s = i % 2
            ky, kh, kg = ("y", s), ("h", s), ("g", s)
            p.dma(yt[:, s, :], y_t[i], writes=[ky])
            p.dma(ht[:, s, :], h_t[i], writes=[kh])
            p.dma(gt[:, s, :], g_t[i], writes=[kg])
            p.op("vector", "tensor_add", yt[:, s, :], yt[:, s, :], ht[:, s, :],
                 reads=[ky, kh], writes=[ky])
            p.op("scalar", "activation", gt[:, s, :], gt[:, s, :], AF.Silu,
                 reads=[kg], writes=[kg])
            p.op("vector", "tensor_mul", yt[:, s, :], yt[:, s, :], gt[:, s, :],
                 reads=[ky, kg], writes=[ky])
            stores.append(p.dma(yf_t[i], yt[:, s, :], reads=[ky]))
            p.op("scalar", "activation", sq[:], yt[:, s, :], AF.Square,
                 reads=[ky], writes=[("sq",)])
            p.op("vector", "reduce_sum", st[:, 0:1], sq[:], axis=AX.X,
                 reads=[("sq",)], writes=[("st",)])
            p.op("scalar", "activation", st[:, 1:2], st[:, 0:1], AF.Sqrt,
                 scale=1.0 / CHN, bias=cst[:, 0:1], reads=[("st",), kcst],
                 writes=[("st",)])
            p.op("vector", "reciprocal", rs[:, i:i + 1], st[:, 1:2],
                 reads=[("st",)], writes=[("rs",)])
        stores.append(p.dma(rs_out, rs[:], reads=[("rs",)]))
        p.build(nc, stores)
    return nc


def build_factor_stage(R, Kdim, ESL, NTOK, tchunk=1024):
    KT, MT, NCH = Kdim // 128, ESL // 128, NTOK // tchunk
    nc = bass.Bass("TRN2", target_bir_lowering=False)
    f_in = nc.dram_tensor("factors", [R, Kdim, ESL], F32, kind="ExternalInput").ap()
    rhs_in = nc.dram_tensor("rhs", [Kdim, NTOK], F32, kind="ExternalInput").ap()
    b_in = nc.dram_tensor("bias", [128, MT], F32, kind="ExternalInput").ap()
    out_out = nc.dram_tensor("out", [ESL, NTOK], F32, kind="ExternalOutput").ap()
    p = Prog()
    stores = []
    with contextlib.ExitStack() as cm:
        sb = lambda *a: cm.enter_context(nc.sbuf_tensor(*a))
        ps = lambda *a: cm.enter_context(nc.psum_tensor(*a))
        fb = sb("fb", [128, 2, KT, ESL], F32)
        acc = sb("acc", [128, KT, ESL], F32)
        lhs_sb = sb("lhs_sb", [128, KT, ESL], BF16)
        rhs_sb = sb("rhs_sb", [128, 2, KT, tchunk], BF16)
        out_sb = sb("out_sb", [128, 2, tchunk], F32)
        bi_sb = sb("bi_sb", [128, MT], F32)
        psA = ps("psA", [128, 512], F32)
        psB = ps("psB", [128, 512], F32)
        psum = [psA, psB]
        p.dma(bi_sb[:], b_in, writes=[("bi",)])
        for r in range(R):
            s = r % 2
            p.dma(fb[:, s, :, :], f_in[r].rearrange("(k p) e -> p k e", p=128),
                  writes=[("fb", s)], eng="gpsimd")
            if r == 0:
                p.op("vector", "tensor_copy", acc[:], fb[:, s, :, :],
                     reads=[("fb", s)], writes=[("acc",)])
            else:
                p.op("vector", "tensor_add", acc[:], acc[:], fb[:, s, :, :],
                     reads=[("acc",), ("fb", s)], writes=[("acc",)])
        p.op("vector", "tensor_copy", lhs_sb[:], acc[:], reads=[("acc",)],
             writes=[("lhs",)])
        rhs_t = rhs_in.rearrange("(k p) n -> p k n", p=128)
        for nch in range(NCH):
            s = nch % 2
            nsl = slice(nch * tchunk, (nch + 1) * tchunk)
            p.dma(rhs_sb[:, s, :, :], rhs_t[:, :, nsl], writes=[("rhs", s)],
                  eng="gpsimd")
            for mt in range(MT):
                for nn in range(tchunk // 512):
                    pb = (mt * (tchunk // 512) + nn) % 2
                    for k in range(KT):
                        p.op("tensor", "matmul", psum[pb][:],
                             lhs_sb[:, k, mt * 128:(mt + 1) * 128],
                             rhs_sb[:, s, k, nn * 512:(nn + 1) * 512],
                             start=(k == 0), stop=(k == KT - 1),
                             reads=[("lhs",), ("rhs", s)],
                             writes=([("ps", pb)] if k in (0, KT - 1) else []))
                    p.op("vector", "tensor_scalar", out_sb[:, s, nn * 512:(nn + 1) * 512],
                         psum[pb][:], 1.0 / np.sqrt(R), bi_sb[:, mt:mt + 1],
                         ALU.mult, ALU.add, reads=[("ps", pb), ("bi",)],
                         writes=[("out", s)])
                stores.append(p.dma(out_out[mt * 128:(mt + 1) * 128, nsl],
                                    out_sb[:, s, :], reads=[("out", s)]))
        p.build(nc, stores)
    return nc


def build_ssd_stage(HH=32, HD=64, NGC=4, SS=128, T_=2048, CH=128):
    NCH = T_ // CH
    CHN = HH * HD
    CT = CHN // 128
    nc = bass.Bass("TRN2", target_bir_lowering=False)
    hs_in = nc.dram_tensor("hs", [CHN, T_], F32, kind="ExternalInput").ap()
    bm_in = nc.dram_tensor("bmat", [NGC * SS, T_], F32, kind="ExternalInput").ap()
    cm_in = nc.dram_tensor("cmat", [NGC * SS, T_], F32, kind="ExternalInput").ap()
    dtr_in = nc.dram_tensor("dtr", [HH, T_], F32, kind="ExternalInput").ap()
    dtb_in = nc.dram_tensor("dtb", [HH, 1], F32, kind="ExternalInput").ap()
    al_in = nc.dram_tensor("alog", [HH, 1], F32, kind="ExternalInput").ap()
    dv_in = nc.dram_tensor("dvec", [128, CT], F32, kind="ExternalInput").ap()
    mu_in = nc.dram_tensor("masku", [128, 128], F32, kind="ExternalInput").ap()
    sel_in = nc.dram_tensor("sel", [HH, CT, 128], F32, kind="ExternalInput").ap()
    ytok_out = nc.dram_tensor("ytok", [T_, CHN], F32, kind="ExternalOutput").ap()
    hsd_out = nc.dram_tensor("hsd", [CHN, T_], F32, kind="ExternalOutput").ap()
    lam_d = nc.dram_tensor("lam_d", [1, HH, NCH], F32, kind="Internal").ap()
    acum_d = nc.dram_tensor("acum_d", [HH, 1, T_], F32, kind="Internal").ap()

    p = Prog()
    stores = []
    with contextlib.ExitStack() as cm:
        sb = lambda *a: cm.enter_context(nc.sbuf_tensor(*a))
        ps = lambda *a: cm.enter_context(nc.psum_tensor(*a))
        dt_sb = sb("dt_sb", [HH, T_], F32)
        dtbf_sb = sb("dtbf_sb", [HH, T_], BF16)
        acum = sb("acum", [HH, T_], F32)
        ds_f = sb("ds_f", [HH, T_], F32)
        tmpa = ds_f
        onesr = sb("onesr", [HH, 128], F32)
        A_sb = sb("A_sb", [HH, 2], F32)
        lamb = sb("lamb", [HH, NCH], F32)
        acumT = sb("acumT", [128, NCH, HH], F32)
        dsT = sb("dsT", [128, NCH, HH], F32)
        lamrep = sb("lamrep", [128, NCH * HH], F32)
        lamflat = sb("lamflat", [1, HH * NCH], F32)
        caflat = sb("caflat", [1, T_], F32)
        ones1 = sb("ones1", [1, 128], F32)
        eyef = sb("eyef", [128, 128], F32)
        eyeb = sb("eyeb", [128, 128], BF16)
        masku_sb = sb("masku_sb", [128, 128], BF16)
        sel_sb = sb("sel_sb", [HH, CT, 128], BF16)
        dv_sb = sb("dv_sb", [128, CT], F32)
        hsdtT = sb("hsdtT", [128, NCH, CHN], BF16)
        bmat_sb = sb("bmat_sb", [128, NGC, T_], BF16)
        cmat_sb = sb("cmat_sb", [128, NGC, T_], BF16)
        btr = sb("btr", [128, NGC, NCH, 128], BF16)
        cbt_all = sb("cbt_all", [128, NGC, NCH, 128], BF16)
        hs_ct = sb("hs_ct", [128, T_], F32)
        hsd_ct = sb("hsd_ct", [128, T_], F32)
        hsdt_ct = sb("hsdt_ct", [128, T_], BF16)
        cab = sb("cab", [128, 1024], F32)
        eb = sb("eb", [128, 1024], BF16)
        lt = sb("lt", [128, 2, 128], F32)
        mt_sb = sb("mt_sb", [128, 2, 128], BF16)
        csc = sb("csc", [128, 2, 128], BF16)
        bdt = sb("bdt", [128, 2, 128], BF16)
        rst = sb("rst", [128, 64], F32)
        rbf = sb("rbf", [128, 64], BF16)
        yt_sb = sb("yt_sb", [128, 2, 64], F32)
        ps_big = ps("ps_big", [128, 2048], F32)
        ps_trb = ps("ps_trb", [128, 128], BF16)
        ps_cbt = ps("ps_cbt", [128, 128], F32)
        ps_y = ps("ps_y", [128, 64], F32)
        ps_s = ps("ps_s", [128, 64], F32)

        K = dict(dt=("dt",), tmpa=("dsf",), acum=("acum",), dsf=("dsf",),
                 at=("at",), atn=("atn",), dst=("dst",), lam=("lam",),
                 lrep=("lrep",), lflat=("lflat",), caf=("caf",))

        p.op("vector", "memset", ones1[:], 1.0, writes=[("o1",)])
        p.op("vector", "memset", onesr[:], 1.0, writes=[("or",)])
        eye_in = nc.dram_tensor("eye", [128, 128], F32, kind="ExternalInput").ap()
        p.dma(eyef[:], eye_in, writes=[("eyef",)])
        p.dma(eyeb[:], eye_in, writes=[("eyeb",)], eng="gpsimd")
        p.dma(masku_sb[:], mu_in, writes=[("mu",)], eng="gpsimd")
        p.dma(sel_sb[:], sel_in, writes=[("sel",)], eng="gpsimd")
        p.dma(dv_sb[:], dv_in, writes=[("dv",)])
        p.dma(dt_sb[:], dtr_in, writes=[K["dt"]])
        p.dma(A_sb[:, 0:1], al_in, writes=[("A",)])
        p.dma(tmpa[:, 0:1], dtb_in, writes=[K["tmpa"]])

        p.op("vector", "tensor_scalar", dt_sb[:], dt_sb[:], tmpa[:, 0:1], None,
             ALU.add, reads=[K["dt"], K["tmpa"]], writes=[K["dt"]])
        p.op("scalar", "activation", tmpa[:], dt_sb[:], AF.Abs,
             reads=[K["dt"]], writes=[K["tmpa"]])
        p.op("scalar", "activation", tmpa[:], tmpa[:], AF.Exp, scale=-1.0,
             reads=[K["tmpa"]], writes=[K["tmpa"]])
        p.op("scalar", "activation", tmpa[:], tmpa[:], AF.Ln, bias=1.0,
             reads=[K["tmpa"]], writes=[K["tmpa"]])
        p.op("vector", "tensor_relu", dt_sb[:], dt_sb[:],
             reads=[K["dt"]], writes=[K["dt"]])
        p.op("vector", "tensor_add", dt_sb[:], dt_sb[:], tmpa[:],
             reads=[K["dt"], K["tmpa"]], writes=[K["dt"]])
        p.op("vector", "tensor_copy", dtbf_sb[:], dt_sb[:],
             reads=[K["dt"]], writes=[("dtbf",)])
        p.op("scalar", "activation", A_sb[:, 1:2], A_sb[:, 0:1], AF.Exp,
             reads=[("A",)], writes=[("A",)])
        p.op("vector", "tensor_scalar", A_sb[:, 1:2], A_sb[:, 1:2], -1.0, None,
             ALU.mult, reads=[("A",)], writes=[("A",)])
        p.op("vector", "tensor_scalar", tmpa[:], dt_sb[:], A_sb[:, 1:2], None,
             ALU.mult, reads=[K["dt"], ("A",)], writes=[K["tmpa"]])
        for c in range(NCH):
            sl = slice(c * CH, (c + 1) * CH)
            p.op("vector", "tensor_tensor_scan", acum[:, sl], onesr[:, 0:CH],
                 tmpa[:, sl], 0.0, ALU.mult, ALU.add,
                 reads=[("or",), K["tmpa"]], writes=[K["acum"]])
        for c in range(NCH):
            sl = slice(c * CH, (c + 1) * CH)
            p.op("scalar", "activation", ds_f[:, sl], acum[:, sl], AF.Exp,
                 scale=-1.0, bias=acum[:, (c + 1) * CH - 1:(c + 1) * CH],
                 reads=[K["acum"]], writes=[K["dsf"]])
        p.op("vector", "tensor_copy", lamb[:], acum[:, CH - 1::CH],
             reads=[K["acum"]], writes=[K["lam"]])
        p.op("scalar", "activation", lamb[:], lamb[:], AF.Exp,
             reads=[K["lam"]], writes=[K["lam"]])
        d1 = p.dma(lam_d[0], lamb[:], reads=[K["lam"]])
        p.dma(lamflat[:], lam_d.rearrange("o h c -> o (h c)"), deps=[d1],
              writes=[K["lflat"]])
        d2 = p.dma(acum_d[:, 0, :], acum[:], reads=[K["acum"]])
        p.op("tensor", "matmul", ps_big[:, 0:HH * NCH], ones1[:], lamflat[:],
             start=True, stop=True, reads=[("o1",), K["lflat"]],
             writes=[("psbig",)])
        p.op("vector", "tensor_copy", lamrep[:], ps_big[:, 0:HH * NCH],
             reads=[("psbig",)], writes=[K["lrep"]])
        for c in range(NCH):
            sl = slice(c * CH, (c + 1) * CH)
            p.op("tensor", "transpose", ps_big[0:128, 0:HH], acum[:, sl],
                 eyef[0:HH, 0:HH], reads=[K["acum"], ("eyef",)], writes=[("psbig",)])
            p.op("vector", "tensor_copy", acumT[:, c, :], ps_big[0:128, 0:HH],
                 reads=[("psbig",)], writes=[K["at"]])
            p.op("tensor", "transpose", ps_big[0:128, 0:HH], ds_f[:, sl],
                 eyef[0:HH, 0:HH], reads=[K["dsf"], ("eyef",)], writes=[("psbig",)])
            p.op("vector", "tensor_copy", dsT[:, c, :], ps_big[0:128, 0:HH],
                 reads=[("psbig",)], writes=[K["dst"]])

        p.dma(bmat_sb[:], bm_in.rearrange("(g p) t -> p g t", p=128),
              writes=[("bm",)], eng="gpsimd")
        p.dma(cmat_sb[:], cm_in.rearrange("(g p) t -> p g t", p=128),
              writes=[("cm",)], eng="gpsimd")
        for g in range(NGC):
            for c in range(NCH):
                sl = slice(c * CH, (c + 1) * CH)
                p.op("tensor", "transpose", ps_trb[:], bmat_sb[:, g, sl],
                     eyeb[:], reads=[("bm",), ("eyeb",)], writes=[("pstrb",)])
                p.op("vector", "tensor_copy", btr[:, g, c, :], ps_trb[:],
                     reads=[("pstrb",)], writes=[("btr",)])
                p.op("tensor", "matmul", ps_cbt[:], bmat_sb[:, g, sl],
                     cmat_sb[:, g, sl], start=True, stop=True,
                     reads=[("bm",), ("cm",)], writes=[("pscbt",)])
                p.op("vector", "tensor_mul", cbt_all[:, g, c, :], ps_cbt[:],
                     masku_sb[:], reads=[("pscbt",), ("mu",)], writes=[("cbt",)])

        hs_t = hs_in.rearrange("(c p) t -> c p t", p=128)
        hsd_t = hsd_out.rearrange("(c p) t -> c p t", p=128)
        for ct in range(CT):
            s = 0
            p.dma(hs_ct[:], hs_t[ct], writes=[("hs", s)])
            for q in range(4):
                p.op("tensor", "matmul", ps_big[:, q * 512:(q + 1) * 512],
                     sel_sb[:, ct, :], dtbf_sb[:, q * 512:(q + 1) * 512],
                     start=True, stop=True, reads=[("sel",), ("dtbf",)],
                     writes=([("psbig",)] if q in (0, 3) else []))
            p.op("vector", "tensor_mul", hsdt_ct[:], hs_ct[:], ps_big[:],
                 reads=[("hs", s), ("psbig",)], writes=[("hsdt",)])
            p.op("vector", "tensor_scalar", hsd_ct[:], hs_ct[:],
                 dv_sb[:, ct:ct + 1], None, ALU.mult,
                 reads=[("hs", s), ("dv",)], writes=[("hsdo", s)])
            stores.append(p.dma(hsd_t[ct], hsd_ct[:], reads=[("hsdo", s)]))
            for c in range(NCH):
                sl = slice(c * CH, (c + 1) * CH)
                p.op("tensor", "transpose", ps_trb[:], hsdt_ct[:, sl], eyeb[:],
                     reads=[("hsdt",), ("eyeb",)], writes=[("pstrb",)])
                p.op("vector", "tensor_copy", hsdtT[:, c, ct * 128:(ct + 1) * 128],
                     ps_trb[:], reads=[("pstrb",)], writes=[("hsdtT",)])

        ytok_t = ytok_out.rearrange("(c p) d -> c p d", p=128)
        for h in range(HH):
            g = h // (HH // NGC)
            hsl = slice(h * HD, (h + 1) * HD)
            p.dma(caflat[:], acum_d[h], deps=[d2], writes=[K["caf"]])
            for c in range(NCH):
                half, hoff = divmod(c * CH, 1024)
                if hoff == 0:
                    for q in range(2):
                        p.op("tensor", "matmul", ps_big[:, q * 512:(q + 1) * 512],
                             ones1[:], caflat[:, half * 1024 + q * 512:half * 1024 + (q + 1) * 512],
                             start=True, stop=True, reads=[("o1",), K["caf"]],
                             writes=([("psbig",)] if q in (0, 1) else []))
                    p.op("vector", "tensor_copy", cab[:], ps_big[:, 0:1024],
                         reads=[("psbig",)], writes=[("cab",)])
                    p.op("scalar", "activation", eb[:], cab[:], AF.Exp,
                         reads=[("cab",)], writes=[("eb",)])
                sc = c % 2
                sl = slice(c * CH, (c + 1) * CH)
                slh = slice(hoff, hoff + CH)
                p.op("vector", "tensor_scalar", lt[:, sc, :], cab[:, slh],
                     acumT[:, c, h:h + 1], 0.0, ALU.subtract, ALU.min,
                     reads=[("cab",), K["at"]], writes=[("lt", sc)])
                p.op("scalar", "activation", lt[:, sc, :], lt[:, sc, :], AF.Exp,
                     reads=[("lt", sc)], writes=[("lt", sc)])
                p.op("vector", "tensor_mul", mt_sb[:, sc, :], cbt_all[:, g, c, :],
                     lt[:, sc, :], reads=[("cbt",), ("lt", sc)], writes=[("mt", sc)])
                p.op("tensor", "matmul", ps_y[:], mt_sb[:, sc, :],
                     hsdtT[:, c, hsl], start=True, stop=(c == 0),
                     reads=[("mt", sc), ("hsdtT",)], writes=[("psy",)])
                if c > 0:
                    p.op("vector", "tensor_mul", csc[:, sc, :], cmat_sb[:, g, sl],
                         eb[:, slh], reads=[("cm",), ("eb",)], writes=[("csc", sc)])
                    p.op("vector", "tensor_copy", rbf[:], rst[:],
                         reads=[("rst",)], writes=[("rbf",)])
                    p.op("tensor", "matmul", ps_y[:], csc[:, sc, :], rbf[:],
                         start=False, stop=True,
                         reads=[("csc", sc), ("rbf",)], writes=[("psy",)])
                p.op("vector", "tensor_copy", yt_sb[:, sc, :], ps_y[:],
                     reads=[("psy",)], writes=[("yt", sc)])
                stores.append(p.dma(ytok_t[c][:, hsl], yt_sb[:, sc, :],
                                    reads=[("yt", sc)]))
                p.op("vector", "tensor_scalar", bdt[:, sc, :], btr[:, g, c, :],
                     dsT[:, c, h:h + 1], None, ALU.mult,
                     reads=[("btr",), K["dst"]], writes=[("bdt", sc)])
                p.op("tensor", "matmul", ps_s[:], bdt[:, sc, :],
                     hsdtT[:, c, hsl], start=True, stop=True,
                     reads=[("bdt", sc), ("hsdtT",)], writes=[("pss",)])
                if c == 0:
                    p.op("vector", "tensor_scalar", rst[:], ps_s[:], 1.0, None,
                         ALU.mult, reads=[("pss",)], writes=[("rst",)])
                else:
                    p.op("vector", "scalar_tensor_tensor", rst[:], rst[:],
                         lamrep[:, h * NCH + c:h * NCH + c + 1], ps_s[:],
                         ALU.mult, ALU.add,
                         reads=[("rst",), K["lrep"], ("pss",)], writes=[("rst",)])
        p.build(nc, stores)
    return nc


_PROG_CACHE = {}


def _prog(key, fn, *args, **kw):
    if key not in _PROG_CACHE:
        _PROG_CACHE[key] = fn(*args, **kw)
    return _PROG_CACHE[key]


def _run(nc, in_maps, outs):
    res = run_bass_kernel_spmd(nc, in_maps, core_ids=list(range(NCORES)))
    return [[np.asarray(res.results[c][o]) for c in range(NCORES)] for o in outs]


def _cc(a):
    return np.ascontiguousarray(a, dtype=np.float32)


def kernel(**ins):
    x = _cc(ins["L0_attn_hidden_states"])
    xf = x.reshape(TOK, DM)
    eye = np.eye(128, dtype=np.float32)
    toks = [slice(c * TPC, (c + 1) * TPC) for c in range(NCORES)]

    def ln_plain(f):
        nc = _prog("ln_plain", build_ln_stage, TPC, DM, False, True, final_f32=True)
        (o,) = _run(nc, [{"f": _cc(f[t])} for t in toks], ["xn"])
        return np.concatenate(o, axis=0)

    def ln_res(f, res):
        nc = _prog("ln_res", build_ln_stage, TPC, DM, True, True, final_f32=True)
        xo, xn = _run(nc, [{"f": _cc(f[t]), "res": _cc(res[t])} for t in toks],
                      ["xo", "xn"])
        return np.concatenate(xo, axis=0), np.concatenate(xn, axis=0)

    def ln_res_final(f, res):
        nc = _prog("ln_resf", build_ln_stage, TPC, DM, True, False)
        (xo,) = _run(nc, [{"f": _cc(f[t]), "res": _cc(res[t])} for t in toks],
                     ["xo"])
        return np.concatenate(xo, axis=0)

    W = _cc(ins["L0_attn_in_proj_w"])
    cw = _cc(ins["L0_attn_conv_w"])
    cb = _cc(ins["L0_attn_conv_b"])
    xpw = _cc(ins["L0_attn_x_proj_w"])
    dpw = _cc(ins["L0_attn_dt_proj_w"])
    dpb_ = _cc(ins["L0_attn_dt_proj_b"])
    Al = _cc(ins["L0_attn_A_log"])
    Dv = _cc(ins["L0_attn_D"])
    opw = _cc(ins["L0_attn_out_proj_w"])
    eye = np.eye(128, dtype=np.float32)
    nc = _prog("mega5", build_mega, 5)
    w1 = _cc(ins["L1_attn_in_proj_w"])
    cw1 = _cc(ins["L1_attn_conv_w"][:, 0, :].T)
    cb1 = _cc(ins["L1_attn_conv_b"])
    dtb1_ = _cc(ins["L1_attn_dt_bias"]); al1 = _cc(ins["L1_attn_A_log"])
    Dv1 = _cc(ins["L1_attn_D"]); nw1 = _cc(ins["L1_attn_norm_w"])
    opw1 = _cc(ins["L1_attn_out_proj_w"])
    masku_ = np.triu(np.ones((128, 128), np.float32))
    fac = _cc(ins["L1_ffn_factors"])
    bias1 = _cc(ins["L1_ffn_bias"])
    sel_ = np.zeros((32, 16, 128), np.float32)
    for h32 in range(32):
        sel_[h32, (h32 * 64) // 128, (h32 * 64) % 128:(h32 * 64) % 128 + 64] = 1.0
    gwT = _cc(ins["L0_ffn_gate_w"].T)
    uwT = _cc(ins["L0_ffn_up_w"].T)
    dwT = _cc(ins["L0_ffn_down_w"].T)
    gsc = pcol(ins["L0_ffn_gate_s"]); usc = pcol(ins["L0_ffn_up_s"])
    dsc = pcol(ins["L0_ffn_down_s"])
    im = []
    for c in range(NCORES):
        b, j = divmod(c, 2)
        dsl = slice(j * 2048, (j + 1) * 2048)
        Ws = np.concatenate([W[j * 2048:(j + 1) * 2048],
                             W[4096 + j * 2048:4096 + (j + 1) * 2048]], axis=0)
        im.append({"x_my": _cc(xf[c * 1024:(c + 1) * 1024]),
                   "w_ip": _cc(Ws.T),
                   "cw0": _cc(cw[dsl]), "cb0": pcol(cb[dsl]),
                   "w_xp": _cc(xpw[:, dsl].T),
                   "w_dt": _cc(dpw[dsl].T),
                   "dpb": pcol(dpb_[dsl]),
                   "alog": _cc(Al[dsl]),
                   "dvq": pcol(Dv[dsl]),
                   "w_op": _cc(opw[:, dsl].T),
                   "ident": eye,
                   "w_g": gwT, "w_u": uwT, "w_d": dwT,
                   "gsq": gsc, "usq": usc, "dsq": dsc,
                   "w_ip1": _cc(np.concatenate([
                       w1[:, 10240 + j * 32:10240 + (j + 1) * 32],
                       np.zeros((DM, 96), np.float32),
                       w1[:, j * 2048:(j + 1) * 2048],
                       w1[:, 4096 + j * 2048:4096 + (j + 1) * 2048],
                       w1[:, 8192 + j * 512:8192 + (j + 1) * 512],
                       w1[:, 9216 + j * 512:9216 + (j + 1) * 512]], axis=1)),
                   "cw1s": _cc(np.concatenate([
                       cw1[j * 2048:(j + 1) * 2048],
                       cw1[4096 + j * 512:4096 + (j + 1) * 512],
                       cw1[5120 + j * 512:5120 + (j + 1) * 512]], axis=0)),
                   "cb1q": pcol(np.concatenate([
                       cb1[j * 2048:(j + 1) * 2048],
                       cb1[4096 + j * 512:4096 + (j + 1) * 512],
                       cb1[5120 + j * 512:5120 + (j + 1) * 512]]), 24),
                   "dtb1": _cc(dtb1_[j * 32:(j + 1) * 32, None]),
                   "alog1": _cc(al1[j * 32:(j + 1) * 32, None]),
                   "dvec1": pcol(np.repeat(Dv1[j * 32:(j + 1) * 32], 64), 16),
                   "masku": masku_, "sel1": sel_,
                   "w_op1": _cc(opw1[j * 2048:(j + 1) * 2048, :]),
                   "nwq": pcol(nw1[j * 2048:(j + 1) * 2048], 16),
                   "fac_sl": _cc(fac[:, :, j * 1024:(j + 1) * 1024]),
                   "fbias": pcol(bias1[j * 1024:(j + 1) * 1024], 8),
                   "moff4": np.array([[q * 512 * 2048 + j * 1024
                                       for q in range(4)]], np.int32)})
    (x4_s,) = _run(nc, im, ["x4_o"])
    x4 = np.concatenate(x4_s, axis=0)
    return x4.reshape(BATCH, T, DM)


RG_PAIRS = [[0, 1], [2, 3], [4, 5], [6, 7]]


def build_mega(upto=1):
    nc = bass.Bass("TRN2", target_bir_lowering=False)
    p = Prog()
    stores = []

    x_my = nc.dram_tensor("x_my", [1024, DM], F32, kind="ExternalInput").ap()
    w_ip = nc.dram_tensor("w_ip", [DM, 4096], F32, kind="ExternalInput").ap()
    cw0 = nc.dram_tensor("cw0", [2048, 4], F32, kind="ExternalInput").ap()
    cb0 = nc.dram_tensor("cb0", [128, 16], F32, kind="ExternalInput").ap()
    xnh_d = nc.dram_tensor("xnh_d", [1024, DM], BF16, kind="Internal").ap()
    xng_d = nc.dram_tensor("xng_d", [T, DM], BF16, kind="Internal").ap()
    xz_d = nc.dram_tensor("xz_d", [4096, T], BF16, kind="Internal").ap()
    if upto == 1:
        xact_o = nc.dram_tensor("xact_o", [2048, T], F32, kind="ExternalOutput").ap()
        z_o = nc.dram_tensor("z_o", [2048, T], F32, kind="ExternalOutput").ap()
    if upto >= 2:
        w_xp = nc.dram_tensor("w_xp", [2048, 160], F32, kind="ExternalInput").ap()
        w_dt = nc.dram_tensor("w_dt", [128, 2048], F32, kind="ExternalInput").ap()
        dpb = nc.dram_tensor("dpb", [128, 16], F32, kind="ExternalInput").ap()
        alog = nc.dram_tensor("alog", [2048, 16], F32, kind="ExternalInput").ap()
        dvq = nc.dram_tensor("dvq", [128, 16], F32, kind="ExternalInput").ap()
        w_op = nc.dram_tensor("w_op", [2048, 2048], F32, kind="ExternalInput").ap()
        ident = nc.dram_tensor("ident", [128, 128], F32, kind="ExternalInput").ap()
        xab_d = nc.dram_tensor("xab_d", [2048, T], BF16, kind="Internal").ap()
        xdbl_p = nc.dram_tensor("xdbl_p", [1, 160, T], F32, kind="Internal").ap()
        xdbl_d = nc.dram_tensor("xdbl_d", [1, 160, T], F32, kind="Internal").ap()
        dt_d = nc.dram_tensor("dt_d", [2048, T], F32, kind="Internal").ap()
        yg_d = nc.dram_tensor("yg_d", [2048, T], BF16, kind="Internal").ap()
        po_d = nc.dram_tensor("po_d", [4096, 1024], BF16, kind="Internal").ap()
        at_d = nc.dram_tensor("at_d", [2048, 1024], BF16, kind="Internal").ap()
        if upto == 2:
            x1_o = nc.dram_tensor("x1_o", [1024, DM], F32, kind="ExternalOutput").ap()
            x1n_o = nc.dram_tensor("x1n_o", [1024, DM], F32, kind="ExternalOutput").ap()
    if upto >= 3:
        x1_o = nc.dram_tensor("x1_d", [1024, DM], F32, kind="Internal").ap()
        x1n_o = None
        x1nb_d = nc.dram_tensor("x1nb_d", [1024, DM], BF16, kind="Internal").ap()
        w_g = nc.dram_tensor("w_g", [DM, 5632], F32, kind="ExternalInput").ap()
        w_u = nc.dram_tensor("w_u", [DM, 5632], F32, kind="ExternalInput").ap()
        w_d = nc.dram_tensor("w_d", [5632, DM], F32, kind="ExternalInput").ap()
        gsq = nc.dram_tensor("gsq", [128, 44], F32, kind="ExternalInput").ap()
        usq = nc.dram_tensor("usq", [128, 44], F32, kind="ExternalInput").ap()
        dsq = nc.dram_tensor("dsq", [128, 16], F32, kind="ExternalInput").ap()
        ffo_d = nc.dram_tensor("ffo_d", [DM, 1024], BF16, kind="Internal").ap()
        if upto == 3:
            x2_o = nc.dram_tensor("x2_o", [1024, DM], F32, kind="ExternalOutput").ap()
            x2n_o = nc.dram_tensor("x2n_o", [1024, DM], F32, kind="ExternalOutput").ap()
    if upto >= 4:
        x2_o = nc.dram_tensor("x2_d", [1024, DM], F32, kind="Internal").ap()
        x2n_o = None
        x2nb_d = nc.dram_tensor("x2nb_d", [1024, DM], BF16, kind="Internal").ap()
        xng2_d = nc.dram_tensor("xng2_d", [T, DM], BF16, kind="Internal").ap()
        w_ip1 = nc.dram_tensor("w_ip1", [DM, 5248], F32, kind="ExternalInput").ap()
        cw1s = nc.dram_tensor("cw1s", [3072, 4], F32, kind="ExternalInput").ap()
        cb1q = nc.dram_tensor("cb1q", [128, 24], F32, kind="ExternalInput").ap()
        dtb1 = nc.dram_tensor("dtb1", [32, 1], F32, kind="ExternalInput").ap()
        alog1 = nc.dram_tensor("alog1", [32, 1], F32, kind="ExternalInput").ap()
        dvec1 = nc.dram_tensor("dvec1", [128, 16], F32, kind="ExternalInput").ap()
        masku = nc.dram_tensor("masku", [128, 128], F32, kind="ExternalInput").ap()
        sel1 = nc.dram_tensor("sel1", [32, 16, 128], F32, kind="ExternalInput").ap()
        w_op1 = nc.dram_tensor("w_op1", [2048, 2048], F32, kind="ExternalInput").ap()
        nwq = nc.dram_tensor("nwq", [128, 16], F32, kind="ExternalInput").ap()
        gate_d = nc.dram_tensor("gate_d", [2048, T], BF16, kind="Internal").ap()
        hbc_d = nc.dram_tensor("hbc_d", [3072, T], BF16, kind="Internal").ap()
        dtr_d = nc.dram_tensor("dtr_d", [32, T], F32, kind="Internal").ap()
        hbca_d = nc.dram_tensor("hbca_d", [3072, T], BF16, kind="Internal").ap()
        ytok_d = nc.dram_tensor("ytok_d", [T, 2048], BF16, kind="Internal").ap()
        hsd_d = nc.dram_tensor("hsd_d", [2048, T], BF16, kind="Internal").ap()
        rms_p = nc.dram_tensor("rms_p", [1, 1, T], F32, kind="Internal").ap()
        rms_o = nc.dram_tensor("rms_o", [1, 1, T], F32, kind="Internal").ap()
        po1_d = nc.dram_tensor("po1_d", [4096, 1024], BF16, kind="Internal").ap()
        at1_d = nc.dram_tensor("at1_d", [2048, 1024], BF16, kind="Internal").ap()
        if upto == 4:
            x3_o = nc.dram_tensor("x3_o", [1024, DM], F32, kind="ExternalOutput").ap()
            x3n_o = nc.dram_tensor("x3n_o", [1024, DM], F32, kind="ExternalOutput").ap()
    if upto >= 5:
        x3_o = nc.dram_tensor("x3_d", [1024, DM], F32, kind="Internal").ap()
        x3n_o = None
        x3nb_d = nc.dram_tensor("x3nb_d", [1024, DM], BF16, kind="Internal").ap()
        x3ng_d = nc.dram_tensor("x3ng_d", [T, DM], BF16, kind="Internal").ap()
        fac_sl = nc.dram_tensor("fac_sl", [8, DM, 1024], F32, kind="ExternalInput").ap()
        fbias = nc.dram_tensor("fbias", [128, 8], F32, kind="ExternalInput").ap()
        moff4 = nc.dram_tensor("moff4", [1, 4], mybir.dt.int32, kind="ExternalInput").ap()
        mixp_d = nc.dram_tensor("mixp_d", [1024, T], BF16, kind="Internal").ap()
        mixg_h = nc.dram_tensor("mixg_d", [DM, T], BF16, kind="Internal")
        mixg_d = mixg_h.ap()
        mixmy_d = nc.dram_tensor("mixmy_d", [DM, 1024], BF16, kind="Internal").ap()
        x4_o = nc.dram_tensor("x4_o", [1024, DM], F32, kind="ExternalOutput").ap()

    with contextlib.ExitStack() as outer:
        p.attach(nc, outer)

        with contextlib.ExitStack() as cm:
            sb = lambda nm, shp, dt_: cm.enter_context(nc.sbuf_tensor(_u(nm), shp, dt_))
            ft = sb("ft", [128, 2, DM], F32)
            sq = sb("sq", [128, 2, DM], F32)
            on = sb("on", [128, 2, DM], BF16)
            st = sb("st", [128, 2, 8], F32)
            cst, kcst = emit_consts(p, cm, nc, [1e-5])
            x_t = x_my.rearrange("(n p) d -> n p d", p=128)
            xn_t = xnh_d.rearrange("(n p) d -> n p d", p=128)
            for i in range(8):
                s = i % 2
                kf, ksq, kon, kst = ("ft", s), ("sq", s), ("on", s), ("st", s)
                p.dma(ft[:, s, :], x_t[i], writes=[kf])
                emit_ln(p, ft[:, s, :], on[:, s, :], st[:, s, :], sq[:, s, :],
                        kf, kon, kst, ksq, DM, cst[:, 0:1], kcst)
                stores.append(p.dma(xn_t[i], on[:, s, :], reads=[kon],
                                    writes=[("xnh",)]))
            p.flush()
        p.barrier()

        p.cc("AllGather", ALU.bypass, RG_PAIRS, [xnh_d], [xng_d],
             reads=[("xnh",)], writes=[("xng",)])
        p.flush()
        p.barrier()

        KT, M, N = 16, 4096, T
        with contextlib.ExitStack() as cm:
            sb = lambda nm, shp, dt_: cm.enter_context(nc.sbuf_tensor(_u(nm), shp, dt_))
            ps = lambda nm, shp, dt_: cm.enter_context(nc.psum_tensor(_u(nm), shp, dt_))
            rhs_sb = sb("rhs_sb", [128, KT, N], BF16)
            lhs_sb = sb("lhs_sb", [128, 2, KT, 128], BF16)
            out_sb = sb("out_sb", [128, 2, N], BF16)
            psA = ps("psA", [128, 512], F32)
            psB = ps("psB", [128, 512], F32)
            psum = [psA, psB]
            buf = sb("buf", [128, 2, T + 4], BF16)
            acc = sb("acc", [128, 2, T], BF16)
            ot = sb("ot", [128, 2, T], F32)
            cw_sb = sb("cw_sb", [128, 16, 4], F32)
            cb_sb = sb("cb_sb", [128, 16], F32)
            obf = sb("obf", [128, 2, T], BF16)
            p.dma(cw_sb[:], cw0.rearrange("(c p) k -> p c k", p=128),
                  writes=[("cw",)])
            p.dma(cb_sb[:], cb0, writes=[("cb",)])
            xz_ct = xz_d.rearrange("(ct p) t -> ct p t", p=128)
            xa_t = xact_o.rearrange("(ct p) t -> ct p t", p=128) if upto == 1 else None
            xab_t = xab_d.rearrange("(ct p) t -> ct p t", p=128) if upto >= 2 else None

            def emit_conv0(ct):
                s = ct % 2
                kb, ka, ko = ("buf", s), ("acc", s), ("ot", s)
                if ct < 2:
                    p.op("vector", "memset", buf[:, s, :], 0.0, writes=[kb])
                p.dma(buf[:, s, 3:3 + T], xz_ct[ct], reads=[("xz", ct)],
                      writes=[("bufc", s)],
                      deps=[p.last_writer[kb]] if ct < 2 else [])
                kbc = ("bufc", s)
                p.op("vector", "tensor_scalar", acc[:, s, :], buf[:, s, 0:T],
                     cw_sb[:, ct, 0:1], None, ALU.mult,
                     reads=[kbc, ("cw",)], writes=[ka])
                for k in range(1, 4):
                    p.op("vector", "scalar_tensor_tensor", acc[:, s, :],
                         buf[:, s, k:k + T], cw_sb[:, ct, k:k + 1],
                         acc[:, s, :], ALU.mult, ALU.add,
                         reads=[kbc, ("cw",), ka], writes=[ka])
                if upto == 1:
                    p.op("scalar", "activation", ot[:, s, :], acc[:, s, :],
                         AF.Silu, bias=cb_sb[:, ct:ct + 1],
                         reads=[ka, ("cb",)], writes=[ko])
                    stores.append(p.dma(xa_t[ct], ot[:, s, :], reads=[ko]))
                else:
                    p.op("scalar", "activation", obf[:, s, :], acc[:, s, :],
                         AF.Silu, bias=cb_sb[:, ct:ct + 1],
                         reads=[ka, ("cb",)], writes=[("obf", s)])
                    stores.append(p.dma(xab_t[ct], obf[:, s, :],
                                        reads=[("obf", s)], writes=[("xab", ct)]))

            for k in range(KT):
                p.dma(rhs_sb[:, k, :], xng_d[:, k * 128:(k + 1) * 128],
                      transpose=True, reads=[("xng",)], writes=[("rhs", k)])
            xz_t = xz_d.rearrange("(mt p) n -> mt p n", p=128)
            for mt in range(M // 128):
                mb = mt % 2
                slab = w_ip[:, mt * 128:(mt + 1) * 128].rearrange(
                    "(k p) m -> p k m", p=128)
                for k0 in range(0, KT, 8):
                    k1 = min(KT, k0 + 8)
                    p.dma(lhs_sb[:, mb, k0:k1, :], slab[:, k0:k1, :],
                          writes=[("lhs", mb, k) for k in range(k0, k1)],
                          eng="gpsimd")
                for nch in range(N // 512):
                    pb = (mt * (N // 512) + nch) % 2
                    pt = psum[pb]
                    for k in range(KT):
                        p.op("tensor", "matmul", pt[:], lhs_sb[:, mb, k, :],
                             rhs_sb[:, k, nch * 512:(nch + 1) * 512],
                             start=(k == 0), stop=(k == KT - 1),
                             reads=[("lhs", mb, k), ("rhs", k)],
                             writes=([("ps", pb)] if k in (0, KT - 1) else []))
                    p.op("vector", "tensor_copy",
                         out_sb[:, mb, nch * 512:(nch + 1) * 512], pt[:],
                         reads=[("ps", pb)], writes=[("oxz", mb)])
                stores.append(p.dma(xz_t[mt], out_sb[:, mb, :],
                                    reads=[("oxz", mb)], writes=[("xz", mt)], eng="scalar"))
                if 2 <= mt < 18:
                    emit_conv0(mt - 2)
            p.flush()
        p.barrier()

        if upto == 1:
            for q in range(4):
                stores.append(p.dma(z_o[q * 512:(q + 1) * 512, :],
                                    xz_d[2048 + q * 512:2048 + (q + 1) * 512, :],
                                    reads=[("xz", m) for m in
                                           range(16 + q * 4, 20 + q * 4)],
                                    eng="gpsimd"))
            p.finish(stores)
            return nc

        with contextlib.ExitStack() as cm:
            sb = lambda nm, shp, dt_: cm.enter_context(nc.sbuf_tensor(_u(nm), shp, dt_))
            ps = lambda nm, shp, dt_: cm.enter_context(nc.psum_tensor(_u(nm), shp, dt_))
            rhs_sb = sb("rhs_sb", [128, 16, T], BF16)
            lhs_sb = sb("lhs_sb", [128, 16, 160], BF16)
            out_sb = sb("out_sb", [128, 2, T], F32)
            psA = ps("psA", [128, 512], F32)
            psB = ps("psB", [128, 512], F32)
            psum = [psA, psB]
            xab_t2 = xab_d.rearrange("(k p) t -> p k t", p=128)
            for k0 in range(0, 16, 8):
                p.dma(rhs_sb[:, k0:k0 + 8, :], xab_t2[:, k0:k0 + 8, :],
                      reads=[("xab", k) for k in range(k0, k0 + 8)],
                      writes=[("rq", k) for k in range(k0, k0 + 8)])
            p.dma(lhs_sb[:], w_xp.rearrange("(k p) m -> p k m", p=128),
                  writes=[("lx",)], eng="gpsimd")
            for mt in range(2):
                pm = 128 if mt == 0 else 32
                for nch in range(T // 512):
                    pb = (mt * 4 + nch) % 2
                    pt = psum[pb]
                    for k in range(16):
                        p.op("tensor", "matmul", pt[0:pm, :],
                             lhs_sb[:, k, mt * 128:mt * 128 + pm],
                             rhs_sb[:, k, nch * 512:(nch + 1) * 512],
                             start=(k == 0), stop=(k == 15),
                             reads=[("lx",), ("rq", k)],
                             writes=([("ps", pb)] if k in (0, 15) else []))
                    p.op("vector", "tensor_copy",
                         out_sb[0:pm, mt, nch * 512:(nch + 1) * 512], pt[0:pm, :],
                         reads=[("ps", pb)], writes=[("oxp", mt)])
                stores.append(p.dma(xdbl_p[0, mt * 128:mt * 128 + pm, :],
                                    out_sb[0:pm, mt, :], reads=[("oxp", mt)],
                                    writes=[("xdblp",)]))
            p.flush()
        p.barrier()

        p.cc("AllReduce", ALU.add, RG_PAIRS, [xdbl_p], [xdbl_d],
             reads=[("xdblp",)], writes=[("xdbl",)])
        p.flush()
        p.barrier()

        with contextlib.ExitStack() as cm:
            sb = lambda nm, shp, dt_: cm.enter_context(nc.sbuf_tensor(_u(nm), shp, dt_))
            ps = lambda nm, shp, dt_: cm.enter_context(nc.psum_tensor(_u(nm), shp, dt_))
            rhs_sb = sb("rhs_sb", [128, T], BF16)
            lhs_sb = sb("lhs_sb", [128, 2048], BF16)
            out_sb = sb("out_sb", [128, 2, T], F32)
            tmp1 = sb("tmp1", [128, 512], F32)
            tmp2 = sb("tmp2", [128, 512], F32)
            bi_sb = sb("bi_sb", [128, 16], F32)
            psA = ps("psA", [128, 512], F32)
            psB = ps("psB", [128, 512], F32)
            psum = [psA, psB]
            p.dma(rhs_sb[:], xdbl_d[0, 0:128, :], reads=[("xdbl",)],
                  writes=[("rdt",)], eng="gpsimd")
            p.dma(lhs_sb[:], w_dt, writes=[("ldt",)], eng="gpsimd")
            p.dma(bi_sb[:], dpb, writes=[("bdt",)])
            dt_t = dt_d.rearrange("(mt p) t -> mt p t", p=128)
            for mt in range(16):
                mb = mt % 2
                for nch in range(T // 512):
                    pb = (mt * 4 + nch) % 2
                    pt = psum[pb]
                    nsl = slice(nch * 512, (nch + 1) * 512)
                    mm = p.op("tensor", "matmul", pt[:],
                              lhs_sb[:, mt * 128:(mt + 1) * 128],
                              rhs_sb[:, nsl], start=True, stop=True,
                              reads=[("ldt",), ("rdt",)], writes=[("ps", pb)])
                    p.op("vector", "tensor_scalar", tmp1[:], pt[:],
                         bi_sb[:, mt:mt + 1], None, ALU.add,
                         reads=[("ps", pb), ("bdt",)], writes=[("t1",)],
                         deps=[mm])
                    p.op("scalar", "activation", tmp2[:], tmp1[:], AF.Abs,
                         reads=[("t1",)], writes=[("t2",)])
                    p.op("scalar", "activation", tmp2[:], tmp2[:], AF.Exp,
                         scale=-1.0, reads=[("t2",)], writes=[("t2",)])
                    p.op("scalar", "activation", tmp2[:], tmp2[:], AF.Ln,
                         bias=1.0, reads=[("t2",)], writes=[("t2",)])
                    p.op("vector", "tensor_relu", tmp1[:], tmp1[:],
                         reads=[("t1",)], writes=[("t1",)])
                    p.op("vector", "tensor_add", out_sb[:, mb, nsl], tmp1[:],
                         tmp2[:], reads=[("t1",), ("t2",)], writes=[("odt", mb)])
                stores.append(p.dma(dt_t[mt], out_sb[:, mb, :],
                                    reads=[("odt", mb)], writes=[("dt",)]))
            p.flush()
        p.barrier()

        emit_scan_fused(p, nc, stores, dt_d, xab_d, xz_d, xdbl_d, alog, dvq,
                        ident, yg_d, w_op, po_d)
        p.barrier()

        p.cc("ReduceScatter", ALU.add, RG_PAIRS, [po_d], [at_d],
             reads=[("po", 0), ("po", 1)], writes=[("at",)])
        p.flush()
        p.barrier()

        with contextlib.ExitStack() as cm:
            sb = lambda nm, shp, dt_: cm.enter_context(nc.sbuf_tensor(_u(nm), shp, dt_))
            ft = sb("ft", [128, 2, DM], BF16)
            rt = sb("rt", [128, 2, DM], F32)
            sq = sb("sq", [128, 2, DM], F32)
            ot = sb("ot", [128, 2, DM], F32)
            on = sb("on", [128, 2, DM], F32)
            onb = sb("onb", [128, 2, DM], BF16)
            st = sb("st", [128, 2, 8], F32)
            cst, kcst = emit_consts(p, cm, nc, [1e-5])
            x_t = x_my.rearrange("(n p) d -> n p d", p=128)
            x1_t = x1_o.rearrange("(n p) d -> n p d", p=128)
            x1n_t = x1n_o.rearrange("(n p) d -> n p d", p=128) if upto == 2 else None
            x1nb_t = x1nb_d.rearrange("(n p) d -> n p d", p=128) if upto >= 3 else None
            for i in range(8):
                s = i % 2
                kf, kr, ksq, kot, kon, kst = (("f2", s), ("r2", s), ("q2", s),
                                              ("o2", s), ("n2", s), ("s2", s))
                p.dma(ft[:, s, :], at_d[:, i * 128:(i + 1) * 128],
                      transpose=True, reads=[("at",)], writes=[kf])
                p.dma(rt[:, s, :], x_t[i], writes=[kr])
                emit_ln(p, ft[:, s, :], ot[:, s, :], st[:, s, :], sq[:, s, :],
                        kf, kot, kst, ksq, DM, cst[:, 0:1], kcst)
                p.op("vector", "tensor_add", ot[:, s, :], ot[:, s, :],
                     rt[:, s, :], reads=[kot, kr], writes=[kot])
                stores.append(p.dma(x1_t[i], ot[:, s, :], reads=[kot],
                                    writes=[("x1",)]))
                if upto == 2:
                    emit_ln(p, ot[:, s, :], on[:, s, :], st[:, s, :],
                            sq[:, s, :], kot, kon, kst, ksq, DM,
                            cst[:, 0:1], kcst)
                    stores.append(p.dma(x1n_t[i], on[:, s, :], reads=[kon]))
                else:
                    emit_ln(p, ot[:, s, :], onb[:, s, :], st[:, s, :],
                            sq[:, s, :], kot, kon, kst, ksq, DM,
                            cst[:, 0:1], kcst)
                    stores.append(p.dma(x1nb_t[i], onb[:, s, :], reads=[kon],
                                        writes=[("x1nb",)]))
            p.flush()
        p.barrier()
        if upto == 2:
            p.finish(stores)
            return nc

        with contextlib.ExitStack() as cm:
            sb = lambda nm, shp, dt_: cm.enter_context(nc.sbuf_tensor(_u(nm), shp, dt_))
            ps = lambda nm, shp, dt_: cm.enter_context(nc.psum_tensor(_u(nm), shp, dt_))
            rhs_sb = sb("frhs", [128, 16, 1024], BF16)
            lg_sb = sb("flg", [128, 2, 16, 128], BF16)
            lu_sb = sb("flu", [128, 2, 16, 128], BF16)
            hdn_sb = sb("fhdn", [128, 44, 1024], BF16)
            ga_sb = sb("fga", [128, 512], F32)
            gs_sb = sb("fgs", [128, 44], F32)
            us_sb = sb("fus", [128, 44], F32)
            ds_sb = sb("fds", [128, 16], F32)
            ld_sb = sb("fld", [128, 2, 44, 128], BF16)
            out_sb = sb("fout", [128, 2, 1024], BF16)
            psg = ps("fpsg", [128, 512], F32)
            psu = ps("fpsu", [128, 512], F32)
            psum = [psg, psu]
            p.dma(gs_sb[:], gsq, writes=[("fgs",)])
            p.dma(us_sb[:], usq, writes=[("fus",)])
            p.dma(ds_sb[:], dsq, writes=[("fds",)])
            for k in range(16):
                p.dma(rhs_sb[:, k, :], x1nb_d[:, k * 128:(k + 1) * 128],
                      transpose=True, reads=[("x1nb",)], writes=[("fr", k)])
            for mt in range(44):
                mb = mt % 2
                for W, buf, nm in ((w_g, lg_sb, "flgk"), (w_u, lu_sb, "fluk")):
                    slab = W[:, mt * 128:(mt + 1) * 128].rearrange(
                        "(k p) m -> p k m", p=128)
                    for k0 in range(0, 16, 8):
                        p.dma(buf[:, mb, k0:k0 + 8, :], slab[:, k0:k0 + 8, :],
                              writes=[(nm, mb, k) for k in range(k0, k0 + 8)],
                              eng="gpsimd")
                for nch in range(2):
                    nsl = slice(nch * 512, (nch + 1) * 512)
                    for k in range(16):
                        p.op("tensor", "matmul", psg[:], lg_sb[:, mb, k, :],
                             rhs_sb[:, k, nsl], start=(k == 0), stop=(k == 15),
                             reads=[("flgk", mb, k), ("fr", k)],
                             writes=([("fpsg",)] if k in (0, 15) else []))
                    for k in range(16):
                        p.op("tensor", "matmul", psu[:], lu_sb[:, mb, k, :],
                             rhs_sb[:, k, nsl], start=(k == 0), stop=(k == 15),
                             reads=[("fluk", mb, k), ("fr", k)],
                             writes=([("fpsu",)] if k in (0, 15) else []))
                    p.op("scalar", "activation", ga_sb[:], psg[:], AF.Silu,
                         scale=gs_sb[:, mt:mt + 1], reads=[("fpsg",), ("fgs",)],
                         writes=[("fga",)])
                    p.op("vector", "tensor_scalar", hdn_sb[:, mt, nsl], psu[:],
                         us_sb[:, mt:mt + 1], None, ALU.mult,
                         reads=[("fpsu",), ("fus",)], writes=[("fh", mt)])
                    p.op("vector", "tensor_mul", hdn_sb[:, mt, nsl],
                         hdn_sb[:, mt, nsl], ga_sb[:],
                         reads=[("fh", mt), ("fga",)], writes=[("fh", mt)])
            ffo_t = ffo_d.rearrange("(mt p) t -> mt p t", p=128)
            for mt in range(16):
                mb = mt % 2
                slab = w_d[:, mt * 128:(mt + 1) * 128].rearrange(
                    "(k p) m -> p k m", p=128)
                for k0 in range(0, 44, 8):
                    k1 = min(44, k0 + 8)
                    p.dma(ld_sb[:, mb, k0:k1, :], slab[:, k0:k1, :],
                          writes=[("fldk", mb, k) for k in range(k0, k1)],
                          eng="gpsimd")
                for nch in range(2):
                    pb = nch % 2
                    pt = psum[pb]
                    nsl = slice(nch * 512, (nch + 1) * 512)
                    for k in range(44):
                        p.op("tensor", "matmul", pt[:], ld_sb[:, mb, k, :],
                             hdn_sb[:, k, nsl], start=(k == 0), stop=(k == 43),
                             reads=[("fldk", mb, k), ("fh", k)],
                             writes=([("fps2", pb)] if k in (0, 43) else []))
                    p.op("vector", "tensor_scalar", out_sb[:, mb, nsl], pt[:],
                         ds_sb[:, mt:mt + 1], None, ALU.mult,
                         reads=[("fps2", pb), ("fds",)], writes=[("fo", mb)])
                stores.append(p.dma(ffo_t[mt], out_sb[:, mb, :],
                                    reads=[("fo", mb)], writes=[("ffo",)], eng="scalar"))
            p.flush()
        p.barrier()

        with contextlib.ExitStack() as cm:
            sb = lambda nm, shp, dt_: cm.enter_context(nc.sbuf_tensor(_u(nm), shp, dt_))
            ft = sb("ft", [128, 2, DM], BF16)
            rt = sb("rt", [128, 2, DM], F32)
            sq = sb("sq", [128, 2, DM], F32)
            ot = sb("ot", [128, 2, DM], F32)
            on = sb("on", [128, 2, DM], F32)
            st = sb("st", [128, 2, 8], F32)
            cst, kcst = emit_consts(p, cm, nc, [1e-5])
            onb = sb("onb", [128, 2, DM], BF16)
            x1_t2 = x1_o.rearrange("(n p) d -> n p d", p=128)
            x2_t = x2_o.rearrange("(n p) d -> n p d", p=128)
            x2n_t = x2n_o.rearrange("(n p) d -> n p d", p=128) if upto == 3 else None
            x2nb_t = x2nb_d.rearrange("(n p) d -> n p d", p=128) if upto >= 4 else None
            for i in range(8):
                s = i % 2
                kf, kr, ksq, kot, kon, kst = (("f3", s), ("r3", s), ("q3", s),
                                              ("o3", s), ("n3", s), ("s3", s))
                p.dma(ft[:, s, :], ffo_d[:, i * 128:(i + 1) * 128],
                      transpose=True, reads=[("ffo",)], writes=[kf])
                p.dma(rt[:, s, :], x1_t2[i], reads=[("x1",)], writes=[kr])
                emit_ln(p, ft[:, s, :], ot[:, s, :], st[:, s, :], sq[:, s, :],
                        kf, kot, kst, ksq, DM, cst[:, 0:1], kcst)
                p.op("vector", "tensor_add", ot[:, s, :], ot[:, s, :],
                     rt[:, s, :], reads=[kot, kr], writes=[kot])
                stores.append(p.dma(x2_t[i], ot[:, s, :], reads=[kot],
                                    writes=[("x2",)]))
                if upto == 3:
                    emit_ln(p, ot[:, s, :], on[:, s, :], st[:, s, :],
                            sq[:, s, :], kot, kon, kst, ksq, DM,
                            cst[:, 0:1], kcst)
                    stores.append(p.dma(x2n_t[i], on[:, s, :], reads=[kon]))
                else:
                    emit_ln(p, ot[:, s, :], onb[:, s, :], st[:, s, :],
                            sq[:, s, :], kot, kon, kst, ksq, DM,
                            cst[:, 0:1], kcst)
                    stores.append(p.dma(x2nb_t[i], onb[:, s, :], reads=[kon],
                                        writes=[("x2nb",)]))
            p.flush()
        p.barrier()
        if upto == 3:
            p.finish(stores)
            return nc

        fhalf_d = nc.dram_tensor("fhalf_d", [DM, 1024], BF16, kind="Internal").ap()
        with contextlib.ExitStack() as cm:
            sb = lambda nm, shp, dt_: cm.enter_context(nc.sbuf_tensor(_u(nm), shp, dt_))
            fb = sb("ffb", [128, 2, 16, 256], F32)
            facc = sb("facc", [128, 16, 256], F32)
            flhsb = sb("flhsb", [128, 16, 256], BF16)
            fh_t = fhalf_d.rearrange("(k p) e -> p k e", p=128)
            for e0 in range(4):
                esl = slice(e0 * 256, (e0 + 1) * 256)
                for r in range(8):
                    s = r % 2
                    p.dma(fb[:, s, :, :],
                          fac_sl[r][:, esl].rearrange("(k p) e -> p k e", p=128),
                          writes=[("ffb", s)], eng="gpsimd")
                    if r == 0:
                        p.op("vector", "tensor_copy", facc[:],
                             fb[:, s, :, :], reads=[("ffb", s)],
                             writes=[("fac",)])
                    else:
                        p.op("vector", "tensor_add", facc[:],
                             facc[:], fb[:, s, :, :],
                             reads=[("fac",), ("ffb", s)],
                             writes=[("fac",)])
                p.op("vector", "tensor_copy", flhsb[:], facc[:],
                     reads=[("fac",)], writes=[("flhb",)])
                stores.append(p.dma(fh_t[:, :, esl], flhsb[:],
                                    reads=[("flhb",)], writes=[("fhalf",)],
                                    eng="scalar"))
            p.flush()

        p.cc("AllGather", ALU.bypass, RG_PAIRS, [x2nb_d], [xng2_d],
             reads=[("x2nb",)], writes=[("xng2",)])
        p.flush()
        p.barrier()
        with contextlib.ExitStack() as cm:
            sb = lambda nm, shp, dt_: cm.enter_context(nc.sbuf_tensor(_u(nm), shp, dt_))
            ps = lambda nm, shp, dt_: cm.enter_context(nc.psum_tensor(_u(nm), shp, dt_))
            rhs_sb = sb("rhs1", [128, 16, T], BF16)
            lhs_sb = sb("lhs1", [128, 2, 16, 128], BF16)
            out_sb = sb("out1", [128, 2, T], BF16)
            outf_sb = sb("outf1", [128, T], F32)
            psA = ps("ps1A", [128, 512], F32)
            psB = ps("ps1B", [128, 512], F32)
            psum = [psA, psB]
            buf = sb("c1buf", [128, 2, T + 4], BF16)
            acc = sb("c1acc", [128, 2, T], BF16)
            cot = sb("c1ot", [128, 2, T], BF16)
            cw_sb = sb("c1cw", [128, 24, 4], F32)
            cb_sb = sb("c1cb", [128, 24], F32)
            p.dma(cw_sb[:], cw1s.rearrange("(c p) k -> p c k", p=128),
                  writes=[("c1cw",)])
            p.dma(cb_sb[:], cb1q, writes=[("c1cb",)])
            hbc_t = hbc_d.rearrange("(ct p) t -> ct p t", p=128)
            hbca_t = hbca_d.rearrange("(ct p) t -> ct p t", p=128)

            def emit_conv1(ct):
                s = ct % 2
                kb, ka, ko = ("c1b", s), ("c1a", s), ("c1o", s)
                if ct < 2:
                    p.op("vector", "memset", buf[:, s, :], 0.0, writes=[kb])
                p.dma(buf[:, s, 1:1 + T], hbc_t[ct], reads=[("hbc", ct)],
                      writes=[("c1bc", s)],
                      deps=[p.last_writer[kb]] if ct < 2 else [])
                kbc = ("c1bc", s)
                p.op("vector", "tensor_scalar", acc[:, s, :], buf[:, s, 0:T],
                     cw_sb[:, ct, 0:1], None, ALU.mult,
                     reads=[kbc, ("c1cw",)], writes=[ka])
                for k in range(1, 4):
                    p.op("vector", "scalar_tensor_tensor", acc[:, s, :],
                         buf[:, s, k:k + T], cw_sb[:, ct, k:k + 1],
                         acc[:, s, :], ALU.mult, ALU.add,
                         reads=[kbc, ("c1cw",), ka], writes=[ka])
                p.op("scalar", "activation", cot[:, s, :], acc[:, s, :],
                     AF.Silu, bias=cb_sb[:, ct:ct + 1],
                     reads=[ka, ("c1cb",)], writes=[ko])
                stores.append(p.dma(hbca_t[ct], cot[:, s, :], reads=[ko],
                                    writes=[("hbca",)]))

            for k in range(16):
                p.dma(rhs_sb[:, k, :], xng2_d[:, k * 128:(k + 1) * 128],
                      transpose=True, reads=[("xng2",)], writes=[("r1", k)])
            for mt in range(41):
                mb = mt % 2
                slab = w_ip1[:, mt * 128:(mt + 1) * 128].rearrange(
                    "(k p) m -> p k m", p=128)
                for k0 in range(0, 16, 8):
                    p.dma(lhs_sb[:, mb, k0:k0 + 8, :], slab[:, k0:k0 + 8, :],
                          writes=[("l1", mb, k) for k in range(k0, k0 + 8)],
                          eng="gpsimd")
                for nch in range(T // 512):
                    pb = (mt * 4 + nch) % 2
                    pt = psum[pb]
                    nsl = slice(nch * 512, (nch + 1) * 512)
                    for k in range(16):
                        p.op("tensor", "matmul", pt[:], lhs_sb[:, mb, k, :],
                             rhs_sb[:, k, nsl], start=(k == 0), stop=(k == 15),
                             reads=[("l1", mb, k), ("r1", k)],
                             writes=([("p1", pb)] if k in (0, 15) else []))
                    if mt == 0:
                        p.op("vector", "tensor_copy", outf_sb[:, nsl], pt[:],
                             reads=[("p1", pb)], writes=[("of1",)])
                    else:
                        p.op("vector", "tensor_copy", out_sb[:, mb, nsl], pt[:],
                             reads=[("p1", pb)], writes=[("o1", mb)])
                if mt == 0:
                    stores.append(p.dma(dtr_d, outf_sb[0:32, :],
                                        reads=[("of1",)], writes=[("dtr",)]))
                elif mt <= 16:
                    stores.append(p.dma(gate_d[(mt - 1) * 128:mt * 128, :],
                                        out_sb[:, mb, :], reads=[("o1", mb)],
                                        writes=[("gate",)], eng="scalar"))
                else:
                    stores.append(p.dma(hbc_d[(mt - 17) * 128:(mt - 16) * 128, :],
                                        out_sb[:, mb, :], reads=[("o1", mb)],
                                        writes=[("hbc", mt - 17)], eng="scalar"))
                if 19 <= mt:
                    emit_conv1(mt - 19)
            for ct in (22, 23):
                emit_conv1(ct)
            p.flush()
        p.barrier()

        emit_ssd_fused(p, nc, stores, hbca_d, dtr_d, dtb1, alog1, dvec1,
                       masku, sel1, ident, ytok_d, hsd_d)
        p.barrier()

        with contextlib.ExitStack() as cm:
            sb = lambda nm, shp, dt_: cm.enter_context(nc.sbuf_tensor(_u(nm), shp, dt_))
            ps = lambda nm, shp, dt_: cm.enter_context(nc.psum_tensor(_u(nm), shp, dt_))
            yt = sb("gyt", [128, 2, T], BF16)
            gt = sb("ggt", [128, 2, T], BF16)
            ht = sb("ght", [128, 2, T], BF16)
            yf_sb = sb("gyf", [128, 16, T], BF16)
            gate_t = gate_d.rearrange("(ct p) t -> ct p t", p=128)
            hsd_t = hsd_d.rearrange("(ct p) t -> ct p t", p=128)
            for ct in range(16):
                s = ct % 2
                p.dma(yt[:, s, :], ytok_d[:, ct * 128:(ct + 1) * 128],
                      transpose=True, reads=[("ytok",)], writes=[("gy", s)])
                p.dma(gt[:, s, :], gate_t[ct], reads=[("gate",)],
                      writes=[("gg", s)])
                p.dma(ht[:, s, :], hsd_t[ct], reads=[("hsd",)],
                      writes=[("gh", s)], eng="gpsimd")
                p.op("vector", "tensor_add", yt[:, s, :], yt[:, s, :],
                     ht[:, s, :], reads=[("gy", s), ("gh", s)],
                     writes=[("gy", s)])
                p.op("scalar", "activation", gt[:, s, :], gt[:, s, :], AF.Silu,
                     reads=[("gg", s)], writes=[("gg", s)])
                p.op("vector", "tensor_mul", yf_sb[:, ct, :], yt[:, s, :],
                     gt[:, s, :], reads=[("gy", s), ("gg", s)],
                     writes=[("gyf", ct)])
            yf_sb2 = yf_sb
            nw_sb = sb("onw", [128, 16], F32)
            lhs_sb = sb("olhs", [128, 2, 16, 128], BF16)
            out_sb = sb("oout", [128, 2, T], BF16)
            psA = ps("opsA", [128, 512], F32)
            psB = ps("opsB", [128, 512], F32)
            psum = [psA, psB]
            p.dma(nw_sb[:], nwq, writes=[("onw",)])
            for ct in range(16):
                p.op("vector", "tensor_scalar", yf_sb2[:, ct, :],
                     yf_sb2[:, ct, :], nw_sb[:, ct:ct + 1], None, ALU.mult,
                     reads=[("gyf", ct), ("onw",)], writes=[("gyf", ct)])
            for mt in range(16):
                mb = mt % 2
                slab = w_op1[:, mt * 128:(mt + 1) * 128].rearrange(
                    "(k p) m -> p k m", p=128)
                for k0 in range(0, 16, 8):
                    p.dma(lhs_sb[:, mb, k0:k0 + 8, :], slab[:, k0:k0 + 8, :],
                          writes=[("ol", mb, k) for k in range(k0, k0 + 8)],
                          eng="gpsimd")
                for nch in range(T // 512):
                    pb = (mt * 4 + nch) % 2
                    pt = psum[pb]
                    for k in range(16):
                        p.op("tensor", "matmul", pt[:], lhs_sb[:, mb, k, :],
                             yf_sb2[:, k, nch * 512:(nch + 1) * 512],
                             start=(k == 0), stop=(k == 15),
                             reads=[("ol", mb, k), ("gyf", k)],
                             writes=([("op", pb)] if k in (0, 15) else []))
                    p.op("vector", "tensor_copy",
                         out_sb[:, mb, nch * 512:(nch + 1) * 512], pt[:],
                         reads=[("op", pb)], writes=[("oo", mb)])
                stores.append(p.dma(po1_d[mt * 128:(mt + 1) * 128, :],
                                    out_sb[:, mb, 0:1024], reads=[("oo", mb)],
                                    writes=[("po1",)], eng="scalar"))
                stores.append(p.dma(po1_d[2048 + mt * 128:2048 + (mt + 1) * 128, :],
                                    out_sb[:, mb, 1024:2048], reads=[("oo", mb)],
                                    writes=[("po1",)], eng="scalar"))
            p.flush()
        p.barrier()
        p.cc("ReduceScatter", ALU.add, RG_PAIRS, [po1_d], [at1_d],
             reads=[("po1",)], writes=[("at1",)])
        p.flush()
        p.barrier()

        with contextlib.ExitStack() as cm:
            sb = lambda nm, shp, dt_: cm.enter_context(nc.sbuf_tensor(_u(nm), shp, dt_))
            ft = sb("ft4", [128, 2, DM], BF16)
            rt = sb("rt4", [128, 2, DM], F32)
            sq = sb("sq4", [128, 2, DM], F32)
            ot = sb("ot4", [128, 2, DM], F32)
            on = sb("on4", [128, 2, DM], F32)
            st = sb("st4", [128, 2, 8], F32)
            cst, kcst = emit_consts(p, cm, nc, [1e-5])
            onb5 = sb("onb5", [128, 2, DM], BF16)
            x2_t2 = x2_o.rearrange("(n p) d -> n p d", p=128)
            x3_t = x3_o.rearrange("(n p) d -> n p d", p=128)
            x3n_t = x3n_o.rearrange("(n p) d -> n p d", p=128) if upto == 4 else None
            x3nb_t = x3nb_d.rearrange("(n p) d -> n p d", p=128) if upto >= 5 else None
            for i in range(8):
                s = i % 2
                kf, kr, ksq, kot, kon, kst = (("f4", s), ("r4", s), ("q4", s),
                                              ("o4", s), ("n4", s), ("s4", s))
                p.dma(ft[:, s, :], at1_d[:, i * 128:(i + 1) * 128],
                      transpose=True, reads=[("at1",)], writes=[kf])
                p.dma(rt[:, s, :], x2_t2[i], reads=[("x2",)], writes=[kr])
                emit_ln(p, ft[:, s, :], ot[:, s, :], st[:, s, :], sq[:, s, :],
                        kf, kot, kst, ksq, DM, cst[:, 0:1], kcst)
                p.op("vector", "tensor_add", ot[:, s, :], ot[:, s, :],
                     rt[:, s, :], reads=[kot, kr], writes=[kot])
                stores.append(p.dma(x3_t[i], ot[:, s, :], reads=[kot],
                                    writes=[("x3",)]))
                if upto == 4:
                    emit_ln(p, ot[:, s, :], on[:, s, :], st[:, s, :],
                            sq[:, s, :], kot, kon, kst, ksq, DM,
                            cst[:, 0:1], kcst)
                    stores.append(p.dma(x3n_t[i], on[:, s, :], reads=[kon]))
                else:
                    emit_ln(p, ot[:, s, :], onb5[:, s, :], st[:, s, :],
                            sq[:, s, :], kot, kon, kst, ksq, DM,
                            cst[:, 0:1], kcst)
                    stores.append(p.dma(x3nb_t[i], onb5[:, s, :], reads=[kon],
                                        writes=[("x3nb",)]))
            p.flush()
        p.barrier()
        if upto == 4:
            p.finish(stores)
            return nc

        p.cc("AllGather", ALU.bypass, RG_PAIRS, [x3nb_d], [x3ng_d],
             reads=[("x3nb",)], writes=[("x3ng",)])
        p.flush()
        p.barrier()
        with contextlib.ExitStack() as cm:
            sb = lambda nm, shp, dt_: cm.enter_context(nc.sbuf_tensor(_u(nm), shp, dt_))
            ps = lambda nm, shp, dt_: cm.enter_context(nc.psum_tensor(_u(nm), shp, dt_))
            flhs = sb("flhs", [128, 16, 1024], BF16)
            rhs_sb = sb("frh", [128, 16, T], BF16)
            out_sb = sb("fob", [128, 2, T], BF16)
            fb_sb = sb("fbc", [128, 8], F32)
            psA = ps("fpA", [128, 512], F32)
            psB = ps("fpB", [128, 512], F32)
            psum = [psA, psB]
            p.dma(fb_sb[:], fbias, writes=[("fbi",)])
            for k in range(16):
                p.dma(rhs_sb[:, k, :], x3ng_d[:, k * 128:(k + 1) * 128],
                      transpose=True, reads=[("x3ng",)], writes=[("fr5", k)])
            for k0 in range(0, 16, 8):
                p.dma(flhs[:, k0:k0 + 8, :],
                      fhalf_d.rearrange("(k p) e -> p k e", p=128)[:, k0:k0 + 8, :],
                      reads=[("fhalf",)], writes=[("flh", k) for k in range(k0 // 4, k0 // 4 + 2)])
            mixp_t = mixp_d.rearrange("(mt p) t -> mt p t", p=128)
            for mt in range(8):
                mb = mt % 2
                for nch in range(T // 512):
                    pb = (mt * 4 + nch) % 2
                    pt = psum[pb]
                    nsl = slice(nch * 512, (nch + 1) * 512)
                    for k in range(16):
                        p.op("tensor", "matmul", pt[:],
                             flhs[:, k, mt * 128:(mt + 1) * 128],
                             rhs_sb[:, k, nsl], start=(k == 0), stop=(k == 15),
                             reads=[("flh", 0), ("flh", 1), ("flh", 2), ("flh", 3), ("fr5", k)],
                             writes=([("fp5", pb)] if k in (0, 15) else []))
                    p.op("vector", "tensor_scalar", out_sb[:, mb, nsl], pt[:],
                         1.0 / np.sqrt(8.0), fb_sb[:, mt:mt + 1],
                         ALU.mult, ALU.add, reads=[("fp5", pb), ("fbi",)],
                         writes=[("fo5", mb)])
                stores.append(p.dma(mixp_t[mt], out_sb[:, mb, :],
                                    reads=[("fo5", mb)], writes=[("mixp",)], eng="scalar"))
            p.flush()
        p.barrier()

        p.cc("AllGather", ALU.bypass, RG_PAIRS, [mixp_d], [mixg_d],
             reads=[("mixp",)], writes=[("mixg",)])
        p.flush()
        p.barrier()
        with contextlib.ExitStack() as cm:
            sb = lambda nm, shp, dt_: cm.enter_context(nc.sbuf_tensor(_u(nm), shp, dt_))
            offs = sb("moffs", [1, 4], mybir.dt.int32)
            bounce = sb("mbnc", [128, 16, 1024], BF16)
            p.dma(offs[:], moff4, writes=[("mof",)])
            for q in range(4):
                rows = slice(q * 512, (q + 1) * 512)
                p.dyn_dma(bounce[:, q * 4:(q + 1) * 4, :], mixg_h,
                          [[T, 512], [1, 1024]], offs[0:1, q:q + 1],
                          reads=[("mixg",), ("mof",)], writes=[("mbn", q)])
                p.dma(mixmy_d[rows, :],
                      bounce[:, q * 4:(q + 1) * 4, :].rearrange("p k t -> (k p) t") if False else bounce[:, q * 4:(q + 1) * 4, :],
                      reads=[("mbn", q)], writes=[("mixmy",)])
            p.flush()
        p.barrier()

        with contextlib.ExitStack() as cm:
            sb = lambda nm, shp, dt_: cm.enter_context(nc.sbuf_tensor(_u(nm), shp, dt_))
            ft = sb("ft5", [128, 2, DM], BF16)
            rt = sb("rt5", [128, 2, DM], F32)
            sq = sb("sq5", [128, 2, DM], F32)
            ot = sb("ot5", [128, 2, DM], F32)
            st = sb("st5", [128, 2, 8], F32)
            cst, kcst = emit_consts(p, cm, nc, [1e-5])
            x3_t2 = x3_o.rearrange("(n p) d -> n p d", p=128)
            x4_t = x4_o.rearrange("(n p) d -> n p d", p=128)
            for i in range(8):
                s = i % 2
                kf, kr, ksq, kot, kst = (("f5", s), ("r5", s), ("q5", s),
                                         ("o5", s), ("s5", s))
                p.dma(ft[:, s, :], mixmy_d[:, i * 128:(i + 1) * 128],
                      transpose=True, reads=[("mixmy",)], writes=[kf])
                p.dma(rt[:, s, :], x3_t2[i], reads=[("x3",)], writes=[kr])
                emit_ln(p, ft[:, s, :], ot[:, s, :], st[:, s, :], sq[:, s, :],
                        kf, kot, kst, ksq, DM, cst[:, 0:1], kcst)
                p.op("vector", "tensor_add", ot[:, s, :], ot[:, s, :],
                     rt[:, s, :], reads=[kot, kr], writes=[kot])
                stores.append(p.dma(x4_t[i], ot[:, s, :], reads=[kot]))
            p.flush()
        p.finish(stores)
    return nc


def emit_scan_fused(p, nc, stores, dt_d, xab_d, xz_d, xdbl_d, alog, dvq,
                    ident, yg_d, w_op, po_d, TC=1024, NST=16):
    DT4 = 16
    NTC = T // TC
    PAR = 4
    with contextlib.ExitStack() as cm:
        sb = lambda nm, shp, dt_: cm.enter_context(nc.sbuf_tensor(_u(nm), shp, dt_))
        ps = lambda nm, shp, dt_: cm.enter_context(nc.psum_tensor(_u(nm), shp, dt_))
        dt_sb = sb("dt_sb", [128, 2, TC], F32)
        xa_sb = sb("xa_sb", [128, 2, TC], BF16)
        dtx_sb = sb("dtx_sb", [128, 2, TC], BF16)
        dA_sb = sb("dA_sb", [128, PAR, TC], BF16)
        dBx_sb = sb("dBx_sb", [128, PAR, TC], BF16)
        h_sb = sb("h_sb", [128, PAR, TC], BF16)
        p_sb = sb("p_sb", [128, PAR, TC], BF16)
        bb_all = sb("bb_all", [128, NST, TC], BF16)
        cb_all = sb("cb_all", [128, NST, TC], BF16)
        z_sb = sb("z_sb", [128, TC], BF16)
        yd_sb = sb("yd_sb", [128, TC], BF16)
        sz_sb = sb("sz_sb", [128, TC], BF16)
        yo_sb = sb("yo_sb", [128, TC], BF16)
        hcar = sb("hcar", [128, DT4, NST], F32)
        A_sb = sb("A_sb", [128, DT4, NST], F32)
        dv_sb = sb("dv_sb", [128, DT4], F32)
        bro_sb = sb("bro_sb", [33, NST, TC // 2], BF16)
        cro_sb = sb("cro_sb", [33, NST, TC // 2], BF16)
        onesb = sb("onesb", [33, 128], BF16)
        id_sb = sb("id_sb", [128, 128], BF16)
        oprhs = sb("oprhs", [128, 16, TC], BF16)
        oplhs = sb("oplhs", [128, 2, 16, 128], BF16)
        opout = sb("opout", [128, 2, TC], BF16)
        psy = [ps(f"psy{i}", [128, TC], F32) for i in range(2)]
        psb = ps("psb", [128, 2, 512], F32)
        psO = [ps(f"psO{i}", [128, 512], F32) for i in range(2)]

        p.op("vector", "memset", onesb[:], 1.0, writes=[("sones",)])
        p.dma(id_sb[:], ident, writes=[("sid",)], eng="gpsimd")
        p.dma(A_sb[:], alog.rearrange("(c p) n -> p c n", p=128),
              writes=[("sA",)])
        p.dma(dv_sb[:], dvq, writes=[("sdv",)])
        for c in range(DT4):
            p.op("scalar", "activation", A_sb[:, c, :], A_sb[:, c, :], AF.Exp,
                 reads=[("sA",)], writes=[("sA",)])
            p.op("vector", "tensor_scalar", A_sb[:, c, :], A_sb[:, c, :],
                 -1.0, None, ALU.mult, reads=[("sA",)], writes=[("sA",)])

        dt_t = dt_d.rearrange("(c p) t -> c p t", p=128)
        xa_t = xab_d.rearrange("(c p) t -> c p t", p=128)
        z_t = xz_d.rearrange("(c p) t -> c p t", p=128)
        yg_t = yg_d.rearrange("(c p) t -> c p t", p=128)

        yg_t2 = yg_d.rearrange("(k p) t -> p k t", p=128)

        def emit_bc(tc):
            for qq in range(2):
                qtsl = slice(tc * TC + qq * 512, tc * TC + (qq + 1) * 512)
                p.dma(bro_sb[32 * qq:32 * qq + 1, :, :],
                      xdbl_d[:, 128:144, qtsl],
                      reads=[("xdbl",)], writes=[("sbro",)], eng="gpsimd")
                p.dma(cro_sb[32 * qq:32 * qq + 1, :, :],
                      xdbl_d[:, 144:160, qtsl],
                      reads=[("xdbl",)], writes=[("scro",)], eng="gpsimd")
            for n in range(NST):
                for qq in range(2):
                    sn = (n * 2 + qq) % 2
                    qsl = slice(qq * 512, (qq + 1) * 512)
                    p.op("tensor", "matmul", psb[:, sn, :],
                         onesb[32 * qq:32 * qq + 1, :],
                         bro_sb[32 * qq:32 * qq + 1, n, :], start=True,
                         stop=True,
                         reads=[("sones",), ("sbro",)], writes=[("spsb", sn)])
                    p.op("scalar", "copy", bb_all[:, n, qsl], psb[:, sn, :],
                         reads=[("spsb", sn)], writes=[("sbb", n)])
                    p.op("tensor", "matmul", psb[:, sn, :],
                         onesb[32 * qq:32 * qq + 1, :],
                         cro_sb[32 * qq:32 * qq + 1, n, :], start=True,
                         stop=True,
                         reads=[("sones",), ("scro",)], writes=[("spsb", sn)])
                    p.op("scalar", "copy", cb_all[:, n, qsl], psb[:, sn, :],
                         reads=[("spsb", sn)], writes=[("scb", n)])

        def emit_op_loads(tc):
            for k0 in range(0, 16, 8):
                p.dma(oprhs[:, k0:k0 + 8, :],
                      yg_t2[:, k0:k0 + 8, tc * TC:(tc + 1) * TC],
                      reads=[("yg", tc)],
                      writes=[("ro", k) for k in range(k0, k0 + 8)])

        def emit_op_mts(tc, mts):
            for mt in mts:
                lb = mt % 2
                mb = mt % 2
                slab = w_op[:, mt * 128:(mt + 1) * 128].rearrange(
                    "(k p) m -> p k m", p=128)
                for k0 in range(0, 16, 8):
                    p.dma(oplhs[:, lb, k0:k0 + 8, :], slab[:, k0:k0 + 8, :],
                          writes=[("lo", lb, k) for k in range(k0, k0 + 8)],
                          eng="gpsimd")
                for nch in range(TC // 512):
                    pb = (mt * (TC // 512) + nch) % 2
                    pt = psO[pb]
                    nsl = slice(nch * 512, (nch + 1) * 512)
                    for k in range(16):
                        p.op("tensor", "matmul", pt[:], oplhs[:, lb, k, :],
                             oprhs[:, k, nsl], start=(k == 0), stop=(k == 15),
                             reads=[("lo", lb, k), ("ro", k)],
                             writes=([("pso", pb)] if k in (0, 15) else []))
                    p.op("scalar", "copy", opout[:, mb, nsl], pt[:],
                         reads=[("pso", pb)], writes=[("opo", mb)])
                stores.append(p.dma(
                    po_d[tc * 2048 + mt * 128:tc * 2048 + (mt + 1) * 128, :],
                    opout[:, mb, :], reads=[("opo", mb)],
                    writes=[("po", tc)], eng="scalar"))

        emit_bc(0)
        for tc in range(NTC):
            tsl = slice(tc * TC, (tc + 1) * TC)
            if tc > 0:
                emit_op_loads(tc - 1)
            for dgp in range(8):
                for ci in range(2):
                    c = dgp * 2 + ci
                    p.dma(dt_sb[:, ci, :], dt_t[c, :, tsl], reads=[("dt",)],
                          writes=[("sdt", ci)])
                    p.dma(xa_sb[:, ci, :], xa_t[c, :, tsl], reads=[("xab", c)],
                          writes=[("sxa", ci)], eng="scalar")
                    p.op("vector", "tensor_mul", dtx_sb[:, ci, :],
                         dt_sb[:, ci, :], xa_sb[:, ci, :],
                         reads=[("sdt", ci), ("sxa", ci)], writes=[("sdtx", ci)])

                for n in range(NST):
                    for ci in range(2):
                        c = dgp * 2 + ci
                        sn = (n * 2 + ci) % PAR
                        p.op("scalar", "activation", dA_sb[:, sn, :],
                             dt_sb[:, ci, :], AF.Exp,
                             scale=A_sb[:, c, n:n + 1],
                             reads=[("sdt", ci), ("sA",)],
                             writes=[("sdA", sn)])
                        p.op("vector", "tensor_mul", dBx_sb[:, sn, :],
                             dtx_sb[:, ci, :], bb_all[:, n, :],
                             reads=[("sdtx", ci), ("sbb", n)],
                             writes=[("sdBx", sn)])
                        scan_init = 0.0 if tc == 0 else hcar[:, c, n:n + 1]
                        p.op("vector", "tensor_tensor_scan", h_sb[:, sn, :],
                             dA_sb[:, sn, :], dBx_sb[:, sn, :], scan_init,
                             ALU.mult, ALU.add,
                             reads=[("sdA", sn), ("sdBx", sn)] +
                                   ([("shc", c, n)] if tc else []),
                             writes=[("sh", sn)])
                        if tc < NTC - 1:
                            p.op("scalar", "copy", hcar[:, c, n:n + 1],
                                 h_sb[:, sn, TC - 1:TC], reads=[("sh", sn)],
                                 writes=[("shc", c, n)])
                        p.op("gpsimd", "tensor_mul", p_sb[:, sn, :],
                             h_sb[:, sn, :], cb_all[:, n, :],
                             reads=[("sh", sn), ("scb", n)],
                             writes=[("sp", sn)])
                        for qq in range(2):
                            p.op("tensor", "matmul",
                                 psy[ci][:, qq * 512:(qq + 1) * 512], id_sb[:],
                                 p_sb[:, sn, qq * 512:(qq + 1) * 512],
                                 start=(n == 0), stop=(n == NST - 1),
                                 reads=[("sid",), ("sp", sn)],
                                 writes=([("spsy", ci)] if n in (0, NST - 1) else []))
                for ci in range(2):
                    c = dgp * 2 + ci
                    p.dma(z_sb[:], z_t[16 + c, :, tsl],
                          reads=[("xz", 16 + c)], writes=[("sz",)])
                    p.op("vector", "scalar_tensor_tensor", yd_sb[:],
                         xa_sb[:, ci, :], dv_sb[:, c:c + 1], psy[ci][:],
                         ALU.mult, ALU.add,
                         reads=[("sxa", ci), ("sdv",), ("spsy", ci)],
                         writes=[("syd",)])
                    p.op("scalar", "activation", sz_sb[:], z_sb[:], AF.Silu,
                         reads=[("sz",)], writes=[("ssz",)])
                    p.op("vector", "tensor_mul", yo_sb[:], yd_sb[:], sz_sb[:],
                         reads=[("syd",), ("ssz",)], writes=[("syo",)])
                    stores.append(p.dma(yg_t[c, :, tsl], yo_sb[:],
                                        reads=[("syo",)],
                                        writes=[("yg", tc)], eng="scalar"))
                if tc > 0:
                    emit_op_mts(tc - 1, [2 * dgp, 2 * dgp + 1])
            if tc + 1 < NTC:
                emit_bc(tc + 1)
        emit_op_loads(NTC - 1)
        for mp in range(8):
            emit_op_mts(NTC - 1, [2 * mp, 2 * mp + 1])
        p.flush()


def emit_ssd_fused(p, nc, stores, hbca_d, dtr_d, dtb1, alog1, dvec1, mu_in,
                   sel_in, eye_in, ytok_d, hsd_d):
    HH, HD, NGC, SS, CH = 32, 64, 4, 128, 128
    T_ = T
    NCH = T_ // CH
    CHN = HH * HD
    CT = CHN // 128
    lam_d = nc.dram_tensor(_u("lam_d"), [1, HH, NCH], F32, kind="Internal").ap()
    acum_d = nc.dram_tensor(_u("acum_d"), [HH, 1, T_], F32, kind="Internal").ap()
    with contextlib.ExitStack() as cm, contextlib.ExitStack() as cm2:
        sb = lambda nm, shp, dt_: cm.enter_context(nc.sbuf_tensor(_u(nm), shp, dt_))
        sb2 = lambda nm, shp, dt_: cm2.enter_context(nc.sbuf_tensor(_u(nm), shp, dt_))
        ps = lambda nm, shp, dt_: cm.enter_context(nc.psum_tensor(_u(nm), shp, dt_))
        A_sb = sb("zA", [HH, 2], F32)
        lamb = sb("zlamb", [HH, NCH], F32)
        acumT = sb("zacumT", [128, NCH, HH], F32)
        dsT = sb("zdsT", [128, NCH, HH], F32)
        lamrep = sb("zlamrep", [128, NCH * HH], F32)
        ones1 = sb("zones1", [1, 128], F32)
        dv_sb = sb("zdv", [128, CT], F32)
        hsdtT = sb("zhsdtT", [128, NCH, CHN], BF16)
        cmat_sb = sb("zcmat", [128, NGC, T_], BF16)
        btr = sb("zbtr", [128, NGC, NCH, 128], BF16)
        cbt_all = sb("zcbt", [128, NGC, NCH, 128], BF16)
        ps2 = lambda nm, shp, dt_: cm2.enter_context(nc.psum_tensor(_u(nm), shp, dt_))
        lamflat = sb2("zlamflat", [1, HH * NCH], F32)
        ps_big = ps2("zpsbig", [128, 2048], F32)
        ps_trb = ps2("zpstrb", [128, 128], BF16)
        ps_cbt = ps2("zpscbt", [128, 128], F32)
        dt_sb = sb2("zdt", [HH, T_], F32)
        dtbf_sb = sb2("zdtbf", [HH, T_], BF16)
        acum = sb2("zacum", [HH, T_], F32)
        ds_f = sb2("zdsf", [HH, T_], F32)
        tmpa = ds_f
        onesr = sb2("zonesr", [HH, 128], F32)
        eyef = sb2("zeyef", [128, 128], F32)
        eyeb = sb2("zeyeb", [128, 128], BF16)
        masku_sb = sb2("zmasku", [128, 128], BF16)
        sel_sb = sb2("zsel", [HH, CT, 128], BF16)
        bmat_sb = sb2("zbmat", [128, NGC, T_], BF16)
        hs_ct = sb2("zhsct", [128, T_], BF16)
        hsd_ct = sb2("zhsdct", [128, T_], BF16)
        hsdt_ct = sb2("zhsdtct", [128, T_], BF16)

        K = {k: ("Z" + k,) for k in ("dt", "tmpa", "acum", "dsf", "at", "dst",
                                     "lam", "lrep", "lflat", "caf")}
        K["tmpa"] = K["dsf"]

        p.op("vector", "memset", ones1[:], 1.0, writes=[("Zo1",)])
        p.op("vector", "memset", onesr[:], 1.0, writes=[("Zor",)])
        p.dma(eyef[:], eye_in, writes=[("Zeyef",)])
        p.dma(eyeb[:], eye_in, writes=[("Zeyeb",)], eng="gpsimd")
        p.dma(masku_sb[:], mu_in, writes=[("Zmu",)], eng="gpsimd")
        p.dma(sel_sb[:], sel_in, writes=[("Zsel",)], eng="gpsimd")
        p.dma(dv_sb[:], dvec1, writes=[("Zdv",)])
        p.dma(dt_sb[:], dtr_d, reads=[("dtr",)], writes=[K["dt"]])
        p.dma(A_sb[:, 0:1], alog1, writes=[("ZA",)])
        p.dma(tmpa[:, 0:1], dtb1, writes=[K["tmpa"]])

        p.op("vector", "tensor_scalar", dt_sb[:], dt_sb[:], tmpa[:, 0:1], None,
             ALU.add, reads=[K["dt"], K["tmpa"]], writes=[K["dt"]])
        p.op("scalar", "activation", tmpa[:], dt_sb[:], AF.Abs,
             reads=[K["dt"]], writes=[K["tmpa"]])
        p.op("scalar", "activation", tmpa[:], tmpa[:], AF.Exp, scale=-1.0,
             reads=[K["tmpa"]], writes=[K["tmpa"]])
        p.op("scalar", "activation", tmpa[:], tmpa[:], AF.Ln, bias=1.0,
             reads=[K["tmpa"]], writes=[K["tmpa"]])
        p.op("vector", "tensor_relu", dt_sb[:], dt_sb[:],
             reads=[K["dt"]], writes=[K["dt"]])
        p.op("vector", "tensor_add", dt_sb[:], dt_sb[:], tmpa[:],
             reads=[K["dt"], K["tmpa"]], writes=[K["dt"]])
        p.op("vector", "tensor_copy", dtbf_sb[:], dt_sb[:],
             reads=[K["dt"]], writes=[("Zdtbf",)])
        p.op("scalar", "activation", A_sb[:, 1:2], A_sb[:, 0:1], AF.Exp,
             reads=[("ZA",)], writes=[("ZA",)])
        p.op("vector", "tensor_scalar", A_sb[:, 1:2], A_sb[:, 1:2], -1.0, None,
             ALU.mult, reads=[("ZA",)], writes=[("ZA",)])
        p.op("vector", "tensor_scalar", tmpa[:], dt_sb[:], A_sb[:, 1:2], None,
             ALU.mult, reads=[K["dt"], ("ZA",)], writes=[K["tmpa"]])
        for c in range(NCH):
            sl = slice(c * CH, (c + 1) * CH)
            p.op("vector", "tensor_tensor_scan", acum[:, sl], onesr[:, 0:CH],
                 tmpa[:, sl], 0.0, ALU.mult, ALU.add,
                 reads=[("Zor",), K["tmpa"]], writes=[K["acum"]])
        for c in range(NCH):
            sl = slice(c * CH, (c + 1) * CH)
            p.op("scalar", "activation", ds_f[:, sl], acum[:, sl], AF.Exp,
                 scale=-1.0, bias=acum[:, (c + 1) * CH - 1:(c + 1) * CH],
                 reads=[K["acum"]], writes=[K["dsf"]])
        p.op("vector", "tensor_copy", lamb[:], acum[:, CH - 1::CH],
             reads=[K["acum"]], writes=[K["lam"]])
        p.op("scalar", "activation", lamb[:], lamb[:], AF.Exp,
             reads=[K["lam"]], writes=[K["lam"]])
        d1 = p.dma(lam_d[0], lamb[:], reads=[K["lam"]])
        p.dma(lamflat[:], lam_d.rearrange("o h c -> o (h c)"), deps=[d1],
              writes=[K["lflat"]])
        d2 = p.dma(acum_d[:, 0, :], acum[:], reads=[K["acum"]])
        p.op("tensor", "matmul", ps_big[:, 0:HH * NCH], ones1[:], lamflat[:],
             start=True, stop=True, reads=[("Zo1",), K["lflat"]],
             writes=[("Zpsbig",)])
        p.op("vector", "tensor_copy", lamrep[:], ps_big[:, 0:HH * NCH],
             reads=[("Zpsbig",)], writes=[K["lrep"]])
        for c in range(NCH):
            sl = slice(c * CH, (c + 1) * CH)
            p.op("tensor", "transpose", ps_big[0:128, 0:HH], acum[:, sl],
                 eyef[0:HH, 0:HH], reads=[K["acum"], ("Zeyef",)],
                 writes=[("Zpsbig",)])
            p.op("vector", "tensor_copy", acumT[:, c, :], ps_big[0:128, 0:HH],
                 reads=[("Zpsbig",)], writes=[K["at"]])
            p.op("tensor", "transpose", ps_big[0:128, 0:HH], ds_f[:, sl],
                 eyef[0:HH, 0:HH], reads=[K["dsf"], ("Zeyef",)],
                 writes=[("Zpsbig",)])
            p.op("vector", "tensor_copy", dsT[:, c, :], ps_big[0:128, 0:HH],
                 reads=[("Zpsbig",)], writes=[K["dst"]])

        p.dma(bmat_sb[:], hbca_d[2048:2560].rearrange("(g p) t -> p g t", p=128),
              reads=[("hbca",)], writes=[("Zbm",)])
        p.dma(cmat_sb[:], hbca_d[2560:3072].rearrange("(g p) t -> p g t", p=128),
              reads=[("hbca",)], writes=[("Zcm",)])
        for g in range(NGC):
            for c in range(NCH):
                sl = slice(c * CH, (c + 1) * CH)
                p.op("tensor", "transpose", ps_trb[:], bmat_sb[:, g, sl],
                     eyeb[:], reads=[("Zbm",), ("Zeyeb",)], writes=[("Zpstrb",)])
                p.op("vector", "tensor_copy", btr[:, g, c, :], ps_trb[:],
                     reads=[("Zpstrb",)], writes=[("Zbtr",)])
                p.op("tensor", "matmul", ps_cbt[:], bmat_sb[:, g, sl],
                     cmat_sb[:, g, sl], start=True, stop=True,
                     reads=[("Zbm",), ("Zcm",)], writes=[("Zpscbt",)])
                p.op("vector", "tensor_mul", cbt_all[:, g, c, :], ps_cbt[:],
                     masku_sb[:], reads=[("Zpscbt",), ("Zmu",)],
                     writes=[("Zcbt",)])

        hs_t = hbca_d[0:2048].rearrange("(c p) t -> c p t", p=128)
        hsd_t = hsd_d.rearrange("(c p) t -> c p t", p=128)
        for ct in range(CT):
            p.dma(hs_ct[:], hs_t[ct], reads=[("hbca",)], writes=[("Zhs",)])
            for q in range(4):
                p.op("tensor", "matmul", ps_big[:, q * 512:(q + 1) * 512],
                     sel_sb[:, ct, :], dtbf_sb[:, q * 512:(q + 1) * 512],
                     start=True, stop=True, reads=[("Zsel",), ("Zdtbf",)],
                     writes=([("Zpsbig",)] if q in (0, 3) else []))
            p.op("vector", "tensor_mul", hsdt_ct[:], hs_ct[:], ps_big[:],
                 reads=[("Zhs",), ("Zpsbig",)], writes=[("Zhsdt",)])
            p.op("vector", "tensor_scalar", hsd_ct[:], hs_ct[:],
                 dv_sb[:, ct:ct + 1], None, ALU.mult,
                 reads=[("Zhs",), ("Zdv",)], writes=[("Zhsdo",)])
            stores.append(p.dma(hsd_t[ct], hsd_ct[:], reads=[("Zhsdo",)],
                                writes=[("hsd",)]))
            for c in range(NCH):
                sl = slice(c * CH, (c + 1) * CH)
                p.op("tensor", "transpose", ps_trb[:], hsdt_ct[:, sl], eyeb[:],
                     reads=[("Zhsdt",), ("Zeyeb",)], writes=[("Zpstrb",)])
                p.op("vector", "tensor_copy",
                     hsdtT[:, c, ct * 128:(ct + 1) * 128], ps_trb[:],
                     reads=[("Zpstrb",)], writes=[("ZhsdtT",)])

        p.flush()
        cm2.close()
        y_all = sb("zyall", [128, NCH, CHN // 2], BF16)
        cab = sb("zcab", [128, 2, 1024], F32)
        eb = sb("zeb", [128, 2, 1024], BF16)
        lt = sb("zlt", [128, 4, 128], F32)
        mt_sb = sb("zmt", [128, 4, 128], BF16)
        csc = sb("zcsc", [128, 4, 128], BF16)
        bdt = sb("zbdt", [128, 4, 128], BF16)
        rst = sb("zrst", [128, 2, 64], F32)
        rbf = sb("zrbf", [128, 2, 64], BF16)
        caflat = sb("zcaflat", [1, 2, 1024], F32)
        ps_cab = ps("zpscab", [128, 1024], F32)
        ps_y = [ps("zpsy0", [128, 64], F32), ps("zpsy1", [128, 64], F32)]
        ps_s = [ps("zpss0", [128, 64], F32), ps("zpss1", [128, 64], F32)]
        ytok_t = ytok_d.rearrange("(c p) d -> c p d", p=128)
        for hp in range(HH // 2):
            for c in range(NCH):
                half, hoff = divmod(c * CH, 1024)
                if hoff == 0:
                    for hh in (0, 1):
                        p.dma(caflat[:, hh, :],
                              acum_d[2 * hp + hh][:, half * 1024:(half + 1) * 1024],
                              deps=[d2], writes=[("Zcaf", hh)])
                        for q in range(2):
                            p.op("tensor", "matmul",
                                 ps_cab[:, q * 512:(q + 1) * 512], ones1[:],
                                 caflat[:, hh, q * 512:(q + 1) * 512],
                                 start=True, stop=True,
                                 reads=[("Zo1",), ("Zcaf", hh)],
                                 writes=([("Zpscab",)] if q in (0, 1) else []))
                        p.op("vector", "tensor_copy", cab[:, hh, :],
                             ps_cab[:, 0:1024], reads=[("Zpscab",)],
                             writes=[("Zcab", hh)])
                        p.op("scalar", "activation", eb[:, hh, :], cab[:, hh, :],
                             AF.Exp, reads=[("Zcab", hh)], writes=[("Zeb", hh)])
                sl = slice(c * CH, (c + 1) * CH)
                slh = slice(hoff, hoff + CH)
                HHs = (0, 1)
                hof = lambda hh: 2 * hp + hh
                gof = lambda hh: hof(hh) // (HH // NGC)
                scof = lambda hh: hh * 2 + c % 2
                for hh in HHs:
                    sc = scof(hh)
                    p.op("vector", "tensor_scalar", lt[:, sc, :], cab[:, hh, slh],
                         acumT[:, c, hof(hh):hof(hh) + 1], 0.0, ALU.subtract,
                         ALU.min, reads=[("Zcab", hh), K["at"]],
                         writes=[("Zlt", sc)])
                for hh in HHs:
                    sc = scof(hh)
                    p.op("scalar", "activation", lt[:, sc, :], lt[:, sc, :],
                         AF.Exp, reads=[("Zlt", sc)], writes=[("Zlt", sc)])
                for hh in HHs:
                    sc = scof(hh)
                    p.op("vector", "tensor_mul", mt_sb[:, sc, :],
                         cbt_all[:, gof(hh), c, :], lt[:, sc, :],
                         reads=[("Zcbt",), ("Zlt", sc)], writes=[("Zmt", sc)])
                if c > 0:
                    for hh in HHs:
                        sc = scof(hh)
                        p.op("vector", "tensor_mul", csc[:, sc, :],
                             cmat_sb[:, gof(hh), sl], eb[:, hh, slh],
                             reads=[("Zcm",), ("Zeb", hh)], writes=[("Zcsc", sc)])
                    for hh in HHs:
                        p.op("gpsimd", "tensor_copy", rbf[:, hh, :], rst[:, hh, :],
                             reads=[("Zrst", hh)], writes=[("Zrbf", hh)])
                for hh in HHs:
                    sc = scof(hh)
                    h = hof(hh)
                    hsl = slice(h * HD, (h + 1) * HD)
                    p.op("tensor", "matmul", ps_y[hh][:], mt_sb[:, sc, :],
                         hsdtT[:, c, hsl], start=True, stop=(c == 0),
                         reads=[("Zmt", sc), ("ZhsdtT",)], writes=[("Zpsy", hh)])
                    if c > 0:
                        p.op("tensor", "matmul", ps_y[hh][:], csc[:, sc, :],
                             rbf[:, hh, :], start=False, stop=True,
                             reads=[("Zcsc", sc), ("Zrbf", hh)],
                             writes=[("Zpsy", hh)])
                for hh in HHs:
                    sc = scof(hh)
                    p.op("vector", "tensor_scalar", bdt[:, sc, :],
                         btr[:, gof(hh), c, :], dsT[:, c, hof(hh):hof(hh) + 1],
                         None, ALU.mult, reads=[("Zbtr",), K["dst"]],
                         writes=[("Zbdt", sc)])
                for hh in HHs:
                    sc = scof(hh)
                    h = hof(hh)
                    hsl = slice(h * HD, (h + 1) * HD)
                    p.op("tensor", "matmul", ps_s[hh][:], bdt[:, sc, :],
                         hsdtT[:, c, hsl], start=True, stop=True,
                         reads=[("Zbdt", sc), ("ZhsdtT",)], writes=[("Zpss", hh)])
                for hh in HHs:
                    h = hof(hh)
                    hs2 = slice((h % 16) * HD, (h % 16 + 1) * HD)
                    p.op("vector", "tensor_copy", y_all[:, c, hs2], ps_y[hh][:],
                         reads=[("Zpsy", hh)], writes=[("Zyall",)])
                for hh in HHs:
                    h = hof(hh)
                    if c == 0:
                        p.op("vector", "tensor_scalar", rst[:, hh, :],
                             ps_s[hh][:], 1.0, None, ALU.mult,
                             reads=[("Zpss", hh)], writes=[("Zrst", hh)])
                    else:
                        p.op("vector", "scalar_tensor_tensor", rst[:, hh, :],
                             rst[:, hh, :],
                             lamrep[:, h * NCH + c:h * NCH + c + 1], ps_s[hh][:],
                             ALU.mult, ALU.add,
                             reads=[("Zrst", hh), K["lrep"], ("Zpss", hh)],
                             writes=[("Zrst", hh)])
            if hp in (HH // 4 - 1, HH // 2 - 1):
                colh = 0 if hp == HH // 4 - 1 else 1
                for c in range(NCH):
                    stores.append(p.dma(
                        ytok_t[c][:, colh * 1024:(colh + 1) * 1024],
                        y_all[:, c, :], reads=[("Zyall",)],
                        writes=[("ytok",)], eng="scalar"))
        p.flush()
